# version 10
# baseline (speedup 1.0000x reference)
import numpy as np
import ml_dtypes
import concourse.bass as bass
import concourse.mybir as mybir
from concourse.bass_utils import run_bass_kernel_spmd
from contextlib import ExitStack

F32 = mybir.dt.float32
BF16 = mybir.dt.bfloat16
AF = mybir.ActivationFunctionType
ALU = mybir.AluOpType
AX = mybir.AxisListType

B, N, E, H, Dh = 4, 4096, 2048, 16, 128
NT = N
NO = 2048
NTIL = NT // 128
NOTIL = NO // 128
NS = 4
QS = 512
W_SLOT = [4096, 3072, 2048, 1024]
FF = 8192
EPS = 1e-5
SLOTS_A = [7, 4, 3, 0]
SLOTS_B = [6, 5, 2, 1]

CN = {"eps": 0, "neghalf": 1, "negone": 2, "inv128": 3, "inv2048": 4,
      "c0447": 5, "one": 6, "sig_gelu": 7, "isq128": 8, "half": 9}
CN_VALS = {"eps": EPS, "neghalf": -0.5, "negone": -1.0, "inv128": 1.0 / 128,
           "inv2048": 1.0 / 2048, "c0447": 0.044715, "one": 1.0,
           "sig_gelu": 1.5957691216057308, "isq128": 1.0 / np.sqrt(128.0),
           "half": 0.5}


class Sem:
    def __init__(self, h):
        self.h = h
        self.n = 0


class Plan:
    ENGINES = ("sync", "tensor", "scalar", "vector", "gpsimd")

    def __init__(self):
        self.ev = {e: [] for e in self.ENGINES}
        self.esem = {}

    def add(self, engine, fn, waits=(), incs=(), dma=False, nochain=False):
        waits = list(waits)
        incs = list(incs)
        es = self.esem.get(engine)
        if es is not None and not dma and fn is not None:
            if engine != "tensor" and not nochain and es.n > 0:
                waits.append((es, es.n))
            incs.append((es, 1))
        for s, k in incs:
            s.n += k
        self.ev[engine].append((fn, [(s, v) for s, v in waits if v > 0],
                                [(s, k) for s, k in incs]))

    def emit(self, eng_name, eng):
        for fn, waits, incs in self.ev[eng_name]:
            for s, v in waits:
                eng.wait_ge(s.h, v)
            inst = fn(eng) if fn is not None else None
            if incs:
                if inst is None:
                    inst = eng.nop()
                for s, k in incs:
                    inst.then_inc(s.h, k)


class PsumTracker:

    def __init__(self, nbanks=8):
        self.rel = [[] for _ in range(nbanks)]

    def note_read(self, bank, sem):
        self.rel[bank].append((sem, sem.n))

    def write_waits(self, bank):
        w = self.rel[bank]
        self.rel[bank] = []
        return w


class Arena:
    def __init__(self, t, cols):
        self.t = t
        self.cols = cols
        self.off = 0
        self.marks = []

    def alloc(self, n_bf16):
        n = (n_bf16 + 15) & ~15
        assert self.off + n <= self.cols, f"arena overflow {self.off}+{n}>{self.cols}"
        o = self.off
        self.off += n
        return o

    def bf(self, n):
        o = self.alloc(n)
        return self.t[:, o:o + n]

    def f32(self, n):
        o = self.alloc(2 * n)
        return self.t[:, o:o + 2 * n].bitcast(F32)

    def push(self):
        self.marks.append(self.off)

    def pop(self):
        self.off = self.marks.pop()


def _mask_patterns():
    r = np.arange(1024)[:, None]
    q = np.arange(512)[None, :]
    m0 = (r <= q).astype(ml_dtypes.bfloat16)
    m512 = ((r - 512) <= q).astype(ml_dtypes.bfloat16)
    return np.ascontiguousarray(np.stack([m0, m512]).reshape(2, 8, 128, 512))


def _rope_tables():
    inv = 1.0 / 10000.0 ** (np.arange(0, Dh, 2, dtype=np.float32) / Dh)
    t = np.arange(NT, dtype=np.float32)[:, None] * inv[None, :]
    def shp(a):
        return np.ascontiguousarray(
            a.reshape(NTIL, 128, 64).transpose(1, 0, 2).reshape(128, NTIL * 64))
    return shp(np.cos(t).astype(np.float32)), shp(np.sin(t).astype(np.float32))


def _col128(v):
    return np.ascontiguousarray(np.asarray(v, np.float32).reshape(16, 128).T)


def host_prep(inputs):
    x = np.asarray(inputs["L0_attn_x"], np.float32)
    wrt = np.asarray(inputs["L0_ffn_w_router"], np.float32)
    wdn = np.asarray(inputs["L0_ffn_w_down"], np.float32)
    wup = np.asarray(inputs["L0_ffn_w_up"], np.float32)
    sbias = np.asarray(inputs["L0_ffn_simple_bias"], np.float32)

    wrt128 = np.zeros((E, 128), np.float32); wrt128[:, 0:3] = wrt.T
    wrt128 = np.ascontiguousarray(
        wrt128.reshape(16, 128, 128).transpose(1, 0, 2).reshape(128, 2048))
    wdn128 = np.zeros((E, 128), np.float32); wdn128[:, 0:32] = wdn.T
    wdn128 = np.ascontiguousarray(
        wdn128.reshape(16, 128, 128).transpose(1, 0, 2).reshape(128, 2048))
    wup128 = np.zeros((128, E), np.float32); wup128[0:32, :] = wup.T

    cos_t, sin_t = _rope_tables()
    pat = _mask_patterns()
    consts = np.zeros((128, 16), np.float32)
    for k, j in CN.items():
        consts[:, j] = CN_VALS[k]

    common = {
        "wqkv": np.ascontiguousarray(inputs["L0_attn_w_qkv"], dtype=np.float32),
        "wout": np.ascontiguousarray(inputs["L0_attn_w_out"], dtype=np.float32),
        "ln_s": np.asarray(inputs["L0_attn_ln_scale"], np.float32).reshape(128, 1),
        "ln_b": np.asarray(inputs["L0_attn_ln_bias"], np.float32).reshape(128, 1),
        "wq_t": np.ascontiguousarray(np.asarray(inputs["L0_ffn_wq"], np.float32).T),
        "wk_t": np.ascontiguousarray(np.asarray(inputs["L0_ffn_wk"], np.float32).T),
        "wv_t": np.ascontiguousarray(np.asarray(inputs["L0_ffn_wv"], np.float32).T),
        "wo_t": np.ascontiguousarray(np.asarray(inputs["L0_ffn_wo"], np.float32).T),
        "wff1_t": np.ascontiguousarray(np.asarray(inputs["L0_ffn_w_ff1"], np.float32).T),
        "wff2_t": np.ascontiguousarray(np.asarray(inputs["L0_ffn_w_ff2"], np.float32).T),
        "wrt128": wrt128, "wdn128": wdn128, "wup128": wup128,
        "g1": _col128(inputs["L0_ffn_ln1_scale"]), "b1": _col128(inputs["L0_ffn_ln1_bias"]),
        "g2": _col128(inputs["L0_ffn_ln2_scale"]), "b2": _col128(inputs["L0_ffn_ln2_bias"]),
        "cos_t": cos_t, "sin_t": sin_t,
        "id32": np.eye(128, dtype=np.float32),
        "idbf": np.eye(128, dtype=ml_dtypes.bfloat16),
        "ones_col": np.ones((128, 1), np.float32),
        "ones_row": np.ones((1, 128), np.float32),
        "bias_bc": np.ascontiguousarray(np.tile(sbias[None, :], (128, 1))),
        "consts": consts,
    }

    in_maps = []
    for core in range(8):
        b = core // 2
        slots = SLOTS_A if core % 2 == 0 else SLOTS_B
        masks = np.empty((NS, 8, 128, 512), ml_dtypes.bfloat16)
        for s, ch in enumerate(slots):
            shift = 512 * (ch + 1) + 512 - W_SLOT[s]
            masks[s] = pat[0 if shift == 0 else 1]
        m = dict(common)
        m["x_full"] = np.ascontiguousarray(x[b])
        inv = 1.0 / 10000.0 ** (np.arange(0, Dh, 2, dtype=np.float32) / Dh)
        pos = np.concatenate([512 * c + np.arange(512) for c in slots]).astype(np.float32)
        tt = pos[:, None] * inv[None, :]
        m["cos_o"] = np.ascontiguousarray(
            np.cos(tt).astype(np.float32).reshape(NOTIL, 128, 64).transpose(1, 0, 2).reshape(128, NOTIL * 64))
        m["sin_o"] = np.ascontiguousarray(
            np.sin(tt).astype(np.float32).reshape(NOTIL, 128, 64).transpose(1, 0, 2).reshape(128, NOTIL * 64))
        m["x_own"] = np.ascontiguousarray(
            np.concatenate([x[b, 512 * c:512 * c + 512] for c in slots], 0))
        m["masks"] = np.ascontiguousarray(masks)
        in_maps.append(m)
    return in_maps


def build_nc(upto=99, dumps=(), p0n=(NTIL, NOTIL), p0parts=3):
    nc = bass.Bass("TRN2", target_bir_lowering=False, debug=False)

    D = {}
    def din(name, shape, dt):
        D[name] = nc.dram_tensor(name, list(shape), dt, kind="ExternalInput").ap()
    din("x_full", (NT, E), F32); din("x_own", (NO, E), F32)
    din("wqkv", (E, 3 * E), F32); din("wout", (E, E), F32)
    din("ln_s", (128, 1), F32); din("ln_b", (128, 1), F32)
    din("wq_t", (E, E), F32); din("wk_t", (E, E), F32)
    din("wv_t", (E, E), F32); din("wo_t", (E, E), F32)
    din("wff1_t", (E, FF), F32); din("wff2_t", (FF, E), F32)
    din("wrt128", (128, 2048), F32)
    din("wdn128", (128, 2048), F32); din("wup128", (128, E), F32)
    din("g1", (128, 16), F32); din("b1", (128, 16), F32)
    din("g2", (128, 16), F32); din("b2", (128, 16), F32)
    din("cos_t", (128, NTIL * 64), F32); din("sin_t", (128, NTIL * 64), F32)
    din("cos_o", (128, NOTIL * 64), F32); din("sin_o", (128, NOTIL * 64), F32)
    din("id32", (128, 128), F32); din("idbf", (128, 128), BF16)
    din("ones_col", (128, 1), F32); din("ones_row", (1, 128), F32)
    din("bias_bc", (128, E), F32); din("consts", (128, 16), F32)
    din("masks", (NS, 8, 128, 512), BF16)

    out_d = nc.dram_tensor("out", [NOTIL, 128, E], F32, kind="ExternalOutput").ap()

    S = {}
    def dsc(name, shape, dt):
        S[name] = nc.dram_tensor(name, list(shape), dt, kind="Internal").ap()
    dsc("xnT_sc", (NTIL, 128, E), BF16)
    dsc("xn1T_sc", (NTIL, 128, E), BF16)
    dsc("xnTo_sc", (NOTIL, 128, E), BF16)
    dsc("xn1To_sc", (NOTIL, 128, E), BF16)
    dsc("xno_sc", (NOTIL, 128, E), F32)
    dsc("ctxp_sc", (H, 128, NO), BF16)
    dsc("ctxs_sc", (H, 128, NO), BF16)
    dsc("s_sc", (NOTIL, 128, E), F32)
    dsc("xattn_sc", (NOTIL, 128, E), F32)
    dsc("g_sc", (FF // 128, 128, NO), BF16)
    dsc("wff2bf_sc", (FF // 128, 128, E), BF16)

    dump_aps = {nm: nc.dram_tensor("dump_" + nm, list(S[nm].shape), S[nm].dtype,
                                   kind="ExternalOutput").ap() for nm in dumps
                if nm != "lr1"}

    with ExitStack() as es:
        ar_t = es.enter_context(nc.sbuf_tensor("arena", [128, 100096], BF16))
        ar = Arena(ar_t, 100096)
        pw = [es.enter_context(nc.psum_tensor(f"pw{i}", [128, 1024], F32))
              for i in range(2)]
        pb = [pw[0][:, 0:512], pw[0][:, 512:1024],
              pw[1][:, 0:512], pw[1][:, 512:1024]] + \
             [es.enter_context(nc.psum_tensor(f"pb{i}", [128, 512], F32))[:]
              for i in range(4, 8)]
        sD = Sem(es.enter_context(nc.semaphore("sD")))
        sW = Sem(es.enter_context(nc.semaphore("sW")))
        sP = Sem(es.enter_context(nc.semaphore("sP")))
        sS = Sem(es.enter_context(nc.semaphore("sS")))
        sV = Sem(es.enter_context(nc.semaphore("sV")))
        sG = Sem(es.enter_context(nc.semaphore("sG")))
        sL = [Sem(es.enter_context(nc.semaphore(f"sL{i}"))) for i in range(56)]
        ALLSEMS = tuple([sD, sW, sP, sS, sV, sG] + sL)
        P = Plan()
        P.esem = {"tensor": sP, "scalar": sS, "vector": sV, "gpsimd": sG}
        PT = PsumTracker()

        def snap():
            return [(s, s.n) for s in ALLSEMS]

        def gate(snp, engines=Plan.ENGINES):
            for en in engines:
                P.add(en, None, waits=snp)

        def barrier():
            sB_t = [(s, s.n) for s in ALLSEMS]
            gate(sB_t)

        def ld(dst_ap, src_ap, waits=(), sem=None):
            sem = sem if sem is not None else sD
            P.add("sync", lambda e, d=dst_ap, s=src_ap: e.dma_start(out=d, in_=s),
                  waits=[(sem, sem.n)] + list(waits), incs=[(sem, 16)], dma=True)
            return sem.n

        def st(dst_ap, src_ap, waits=(), sem=None):
            sem = sem if sem is not None else sW
            P.add("scalar", lambda e, d=dst_ap, s=src_ap: e.dma_start(out=d, in_=s),
                  waits=[(sem, sem.n), (sS, sS.n)] + list(waits),
                  incs=[(sem, 16)], dma=True)
            return sem.n

        v_id32 = ar.f32(128)
        v_idbf = ar.bf(128)
        v_consts = ar.f32(16)
        v_lns = ar.f32(1); v_lnb = ar.f32(1)
        v_g1 = ar.f32(16); v_b1 = ar.f32(16)
        v_g2 = ar.f32(16); v_b2 = ar.f32(16)
        v_ones_col = ar.f32(1)
        v_ones_row = ar.f32(128)
        v_lr1 = ar.bf(2048)
        v_m2c = ar.f32(16); v_m1c = ar.f32(16); v_m0c = ar.f32(16); v_m01c = ar.f32(16)

        def cn(name):
            return v_consts[:, CN[name]:CN[name] + 1]

        for dst, srcn in [(v_id32, "id32"), (v_idbf, "idbf"), (v_consts, "consts"),
                          (v_lns, "ln_s"), (v_lnb, "ln_b"), (v_g1, "g1"),
                          (v_b1, "b1"), (v_g2, "g2"), (v_b2, "b2"),
                          (v_ones_col, "ones_col")]:
            ld(dst, D[srcn][:])
        ld(v_ones_row[0:1, 0:128], D["ones_row"][:])
        w_persist = snap()
        gate(w_persist)

        ar.push()
        v_wrt = ar.f32(2048)
        v_wdn = ar.bf(2048)
        ld(v_wrt, D["wrt128"][:])
        ar.push()
        stg0 = ar.f32(2048)
        t1 = ld(stg0, D["wdn128"][:])
        P.add("vector", lambda e: e.tensor_copy(out=v_wdn, in_=stg0),
              waits=[(sD, t1)])
        ar.pop()
        gate(snap())
        xr = [ar.f32(2048) for _ in range(6)]
        junk = ar.f32(2048)
        xnr = [ar.f32(2048) for _ in range(3)]
        xn1r = [ar.f32(2048) for _ in range(3)]
        bstg = [ar.bf(2048) for _ in range(3)]
        b1stg = [ar.bf(2048) for _ in range(3)]
        x32stg = ar.f32(2048)
        rtmp = ar.f32(128)
        rl3 = ar.f32(16)
        sm = [ar.f32(32) for _ in range(2)]
        SC = {"ssum": 0, "ssq": 4, "m": 8, "e2": 12, "v": 16, "r": 20,
              "rr": 24, "s1": 28}

        xr_rel = [0] * 6
        bst_rel = [0, 0, 0]
        b1st_rel = [0, 0, 0]
        xno_rel = [0, 0, 0]
        xnr_tr = [0, 0, 0]
        xn1_tr = [0, 0, 0]

        def p0_stats_batch(b4, own, n_in_batch=4):
            Z = sm[b4 % 2]
            def c(nm, i=0, w=4):
                return Z[:, SC[nm] + i:SC[nm] + i + w]
            dvs = []
            for i in range(n_in_batch):
                nt = 4 * b4 + i
                par = nt % 6
                src = (D["x_own"] if own else D["x_full"]).rearrange(
                    "(n p) e -> n p e", p=128)[nt]
                xsem = sL[40 + par]
                dv = ld(xr[par], src, sem=xsem,
                        waits=[(sV, xr_rel[par])] if xr_rel[par] else [])
                dvs.append((xsem, dv))
                P.add("scalar", lambda e, par=par, i=i: e.activation(
                    out=junk, in_=xr[par], func=AF.Square,
                    accum_out=c("ssq", i, 1)), waits=[(xsem, dv)])
                P.add("vector", lambda e, par=par, i=i: e.reduce_sum(
                    out=c("ssum", i, 1), in_=xr[par], axis=AX.X),
                    waits=[(xsem, dv)])
            s_sq = sS.n
            P.add("vector", lambda e: e.tensor_scalar(out=c("m"), in0=c("ssum"),
                  scalar1=cn("inv2048"), scalar2=None, op0=ALU.mult))
            P.add("vector", lambda e: e.tensor_scalar(out=c("e2"), in0=c("ssq"),
                  scalar1=cn("inv2048"), scalar2=None, op0=ALU.mult),
                  waits=[(sS, s_sq)])
            P.add("vector", lambda e: e.tensor_mul(out=c("v"), in0=c("m"), in1=c("m")))
            P.add("vector", lambda e: e.tensor_sub(out=c("v"), in0=c("e2"), in1=c("v")))
            v_v = sV.n
            P.add("scalar", lambda e: e.activation(out=c("rr"), in_=c("v"), func=AF.Ln,
                                                   bias=cn("eps")),
                  waits=[(sV, v_v)])
            P.add("scalar", lambda e: e.activation(out=c("r"), in_=c("rr"), func=AF.Exp,
                                                   scale=cn("neghalf")))
            s_r = sS.n
            P.add("vector", lambda e: e.tensor_mul(out=c("rr"), in0=c("r"), in1=c("r")),
                  waits=[(sS, s_r)])
            P.add("vector", lambda e: e.tensor_mul(out=c("s1"), in0=c("rr"), in1=c("v")))
            v_s1 = sV.n
            P.add("scalar", lambda e: e.activation(out=c("rr"), in_=c("s1"), func=AF.Ln,
                                                   bias=cn("eps")),
                  waits=[(sV, v_s1)])
            P.add("scalar", lambda e: e.activation(out=c("s1"), in_=c("rr"),
                                                   func=AF.Exp, scale=cn("neghalf")))
            return s_r, sS.n

        def p0_tile(nt, own, s_r, s_rstd1):
            par = nt % 3
            xt = xr[nt % 6]
            Z = sm[(nt // 4) % 2]
            i = nt % 4
            def c(nm):
                return Z[:, SC[nm] + i:SC[nm] + i + 1]
            P.add("vector", lambda e: e.tensor_scalar(out=xnr[par], in0=xt,
                  scalar1=c("m"), scalar2=c("r"), op0=ALU.subtract, op1=ALU.mult),
                  waits=[(sS, s_r), (sP, xnr_tr[par])] +
                  ([(sL[49 + par], xno_rel[par])] if xno_rel[par] else []))
            v_xn = sV.n
            xr_rel[nt % 6] = v_xn
            P.add("vector", lambda e: e.tensor_scalar(out=xn1r[par], in0=xnr[par],
                  scalar1=c("s1"), scalar2=None, op0=ALU.mult),
                  waits=[(sS, s_rstd1), (sP, xn1_tr[par])])
            v_xn1 = sV.n
            bXi = par
            bYi = 3 + par
            ev_last = {}
            for half, (bi, srcbuf, v_src) in enumerate(
                    [(bXi, xnr[par], v_xn), (bYi, xn1r[par], v_xn1)]):
                bank = pb[bi]
                for grp in range(4):
                    for k4 in range(4):
                        ci = grp * 4 + k4
                        wt = []
                        if k4 == 0:
                            wt = [(sV, v_src)] + PT.write_waits(bi)
                        P.add("tensor", lambda e, ci=ci, k4=k4, bk=bank, sb=srcbuf:
                              e.transpose(out=bk[:, k4 * 128:(k4 + 1) * 128],
                                          in_=sb[:, ci * 128:(ci + 1) * 128],
                                          identity=v_id32),
                              waits=wt)
                    p_tr = sP.n
                    if half == 0:
                        xnr_tr[par] = sP.n
                        P.add("vector", lambda e, bk=bank, g=grp:
                              e.tensor_copy(out=bstg[par][:, g * 512:(g + 1) * 512],
                                            in_=bk[:]),
                              waits=[(sP, p_tr)] +
                              ([(sL[43 + par], bst_rel[par])]
                               if (grp == 0 and bst_rel[par]) else []))
                        PT.note_read(bi, sV)
                        if own and not (p0parts & 4):
                            P.add("vector", lambda e, bk=bank, g=grp:
                                  e.tensor_copy(out=x32stg[:, g * 512:(g + 1) * 512],
                                                in_=bk[:]),
                                  waits=[(sP, p_tr)])
                            PT.note_read(bi, sV)
                    else:
                        xn1_tr[par] = sP.n
                        for k4 in range(4):
                            ci = grp * 4 + k4
                            P.add("scalar", lambda e, bk=bank, ci=ci, k4=k4:
                                  e.activation(out=b1stg[par][:, ci * 128:(ci + 1) * 128],
                                               in_=bk[:, k4 * 128:(k4 + 1) * 128],
                                               func=AF.Identity,
                                               scale=v_g1[:, ci:ci + 1],
                                               bias=v_b1[:, ci:ci + 1]),
                                  waits=([(sP, p_tr)] +
                                         ([(sL[46 + par], b1st_rel[par])]
                                          if (grp == 0 and b1st_rel[par]) else []))
                                  if k4 == 0 else [])
                        PT.note_read(bi, sS)
                ev_last[0] = (sV, sV.n) if half == 0 else ev_last.get(0, (sV, sV.n))
                ev_last[1] = (sS, sS.n)
            if own:
                bst_rel[par] = st(S["xnTo_sc"][nt], bstg[par], waits=[ev_last[0]],
                                  sem=sL[43 + par])
                b1st_rel[par] = st(S["xn1To_sc"][nt], b1stg[par], waits=[ev_last[1]],
                                   sem=sL[46 + par])
                if not (p0parts & 8):
                    xno_rel[par] = st(S["xno_sc"][nt], xnr[par], waits=[(sV, v_xn)],
                                      sem=sL[49 + par])
            else:
                bst_rel[par] = st(S["xnT_sc"][nt], bstg[par], waits=[ev_last[0]],
                                  sem=sL[43 + par])
                b1st_rel[par] = st(S["xn1T_sc"][nt], b1stg[par], waits=[ev_last[1]],
                                   sem=sL[46 + par])
            return v_xn

        def p0_router_lr(nt, par):
            v_now = sV.n
            if not (p0parts & 1):
                if p0parts & 2:
                    _lowrank_only(nt, par)
                return
            for ci in range(16):
                P.add("tensor", lambda e, ci=ci: e.matmul(
                    out=pb[6][:, 0:128], lhsT=v_wrt[:, ci * 128:(ci + 1) * 128],
                    rhs=x32stg[:, ci * 128:(ci + 1) * 128],
                    start=(ci == 0), stop=(ci == 15)),
                    waits=([(sV, v_now)] + PT.write_waits(6)) if ci == 0 else [])
            p_rl = sP.n
            P.add("vector", lambda e: e.tensor_copy(out=rtmp, in_=pb[6][:, 0:128]),
                  waits=[(sP, p_rl)])
            PT.note_read(6, sV)
            v_rt = sV.n
            P.add("tensor", lambda e: e.transpose(out=pb[7][:, 0:128], in_=rtmp,
                                                  identity=v_id32),
                  waits=[(sV, v_rt)] + PT.write_waits(7))
            p_rt = sP.n
            P.add("vector", lambda e: e.tensor_copy(out=rl3[:, 0:3],
                                                    in_=pb[7][:, 0:3]),
                  waits=[(sP, p_rt)])
            PT.note_read(7, sV)
            def l(j):
                return rl3[:, j:j + 1]
            P.add("vector", lambda e: e.tensor_tensor(out=l(4), in0=l(2), in1=l(0), op=ALU.is_gt))
            P.add("vector", lambda e: e.tensor_tensor(out=l(5), in0=l(2), in1=l(1), op=ALU.is_gt))
            P.add("vector", lambda e, nt=nt: e.tensor_mul(out=v_m2c[:, nt:nt + 1], in0=l(4), in1=l(5)))
            P.add("vector", lambda e: e.tensor_tensor(out=l(6), in0=l(1), in1=l(0), op=ALU.is_gt))
            P.add("vector", lambda e: e.tensor_tensor(out=l(7), in0=l(1), in1=l(2), op=ALU.is_ge))
            P.add("vector", lambda e: e.tensor_mul(out=l(8), in0=l(6), in1=l(7)))
            P.add("vector", lambda e, nt=nt: e.tensor_scalar(
                out=v_m01c[:, nt:nt + 1], in0=v_m2c[:, nt:nt + 1],
                scalar1=cn("negone"), scalar2=cn("one"), op0=ALU.mult, op1=ALU.add))
            P.add("vector", lambda e, nt=nt: e.tensor_mul(
                out=v_m1c[:, nt:nt + 1], in0=l(8), in1=v_m01c[:, nt:nt + 1]))
            P.add("vector", lambda e, nt=nt: e.tensor_sub(
                out=v_m0c[:, nt:nt + 1], in0=v_m01c[:, nt:nt + 1],
                in1=v_m1c[:, nt:nt + 1]))
            if p0parts & 2:
                _lowrank_only(nt, par)

        def _lowrank_only(nt, par):
            s_bst = sV.n
            for ci in range(16):
                P.add("tensor", lambda e, ci=ci, par=par: e.matmul(
                    out=pb[6][:, 0:128], lhsT=v_wdn[:, ci * 128:(ci + 1) * 128],
                    rhs=bstg[par][:, ci * 128:(ci + 1) * 128],
                    start=(ci == 0), stop=(ci == 15)),
                    waits=([(sV, s_bst)] + PT.write_waits(6)) if ci == 0 else [])
            p_lr = sP.n
            P.add("vector", lambda e, nt=nt: e.tensor_copy(
                out=v_lr1[:, nt * 128:(nt + 1) * 128], in_=pb[6][:, 0:128]),
                waits=[(sP, p_lr)])
            PT.note_read(6, sV)

        snaps = {}
        for b4 in range(p0n[0] // 4):
            if b4 >= 2:
                gate(snaps[b4 - 2])
            s_r, s_rstd1 = p0_stats_batch(b4, own=False)
            for i in range(4):
                p0_tile(4 * b4 + i, False, s_r, s_rstd1)
            snaps[b4] = snap()
        barrier()
        snaps = {}
        for b4 in range(p0n[1] // 4):
            if b4 >= 2:
                gate(snaps[b4 - 2])
            s_r, s_rstd1 = p0_stats_batch(b4, own=True)
            for i in range(4):
                nt = 4 * b4 + i
                p0_tile(nt, True, s_r, s_rstd1)
                p0_router_lr(nt, nt % 3)
            snaps[b4] = snap()
        barrier()
        ar.pop()

        if upto >= 1:
            ar.push()
            a_mask = ar.bf(8 * 512)
            a_sq = [ar.bf(1024) for _ in range(2)]
            a_at = [ar.bf(1024) for _ in range(3)]
            a_acc = ar.f32(1024)
            a_rbc = ar.f32(512)
            a_ctxe = [ar.bf(512) for _ in range(2)]
            a_dr = ar.f32(1024)
            a_sm2 = ar.f32(32)
            SC2 = {"ssq": 0, "sum": 4, "mean": 8, "e2": 12, "var": 16, "rstd": 20}

            def c2(nm, h=0, w=2):
                return a_sm2[:, SC2[nm] + h:SC2[nm] + h + w]

            p_khat = ar.bf(2 * NT)
            p_vtok = ar.bf(NTIL * 256)
            p_qhat = ar.bf(2 * NO)
            s_khat = ar.bf(2 * NT)
            s_vtok = ar.bf(NTIL * 256)
            s_qhat = ar.bf(2 * NO)
            a_wst = ar.f32(1024)
            w_kv = ar.bf(16 * 512)
            w_q = ar.bf(16 * 256)
            a_xnb = ar.bf(8 * 2048)
            a_xnt = [a_xnb[:, i * 2048:(i + 1) * 2048] for i in range(8)]
            a_cos = ar.f32(NTIL * 64); a_sin = ar.f32(NTIL * 64)
            a_rope = ar.f32(256)
            a_kstg = [ar.bf(256) for _ in range(2)]
            a_ksq = a_sq[0].bitcast(F32)
            a_tmpA = a_sq[1].bitcast(F32)[:, 0:128]
            a_tmpB = a_sq[1].bitcast(F32)[:, 128:256]
            kstg_rel = [0, 0]
            xnt_rel = [0] * 8
            acc_rel = [0]
            ctxe_rel = [0, 0]

            def load_w2(dst_bf, wname, col0, width, sub0, prev_users, sem):
                wsrc = D[wname].rearrange("(c p) m -> p c m", p=128)
                blk = dst_bf.shape[-1] // 16
                for cc in range(0, 16, 2):
                    lw = ld(a_wst[:, 0:2 * width]
                            .rearrange("p (c m) -> p c m", c=2),
                            wsrc[:, cc:cc + 2, col0:col0 + width],
                            waits=[(sV, sV.n)] + (prev_users if cc == 0 else []),
                            sem=sem)
                    P.add("vector", lambda e, cc=cc, dst_bf=dst_bf, width=width,
                          sub0=sub0: e.tensor_copy(
                              out=dst_bf.rearrange("p (c m) -> p c m", c=16)
                              [:, cc:cc + 2, sub0:sub0 + width],
                              in_=a_wst[:, 0:2 * width]
                              .rearrange("p (c m) -> p c m", c=2)),
                        waits=[(sem, lw)])
                return sV.n

            def rope_ln2(bank_src, col0, bank_tr, cosv, sinv, dst, dst_col,
                         p_src, par, evac_extra=()):
                pbk = pb[bank_src][:, col0:col0 + 256]
                x1 = pbk.rearrange("p (h z f) -> p h z f", h=2, z=2)[:, :, 0, :]
                x2 = pbk.rearrange("p (h z f) -> p h z f", h=2, z=2)[:, :, 1, :]
                cosb = cosv.rearrange("p (o f) -> p o f", o=1).broadcast_to([128, 2, 64])
                sinb = sinv.rearrange("p (o f) -> p o f", o=1).broadcast_to([128, 2, 64])
                rp1 = a_rope.rearrange("p (h z f) -> p h z f", h=2, z=2)[:, :, 0, :]
                rp2 = a_rope.rearrange("p (h z f) -> p h z f", h=2, z=2)[:, :, 1, :]
                tA = a_tmpA.rearrange("p (h f) -> p h f", h=2)
                tB = a_tmpB.rearrange("p (h f) -> p h f", h=2)
                P.add("vector", lambda e: e.tensor_tensor(out=tA, in0=x1, in1=cosb,
                                                          op=ALU.mult),
                      waits=[(sP, p_src)])
                P.add("vector", lambda e: e.tensor_tensor(out=tB, in0=x2, in1=sinb,
                                                          op=ALU.mult))
                P.add("vector", lambda e: e.tensor_tensor(out=rp1, in0=tA, in1=tB,
                                                          op=ALU.subtract))
                P.add("vector", lambda e: e.tensor_tensor(out=tA, in0=x1, in1=sinb,
                                                          op=ALU.mult))
                P.add("vector", lambda e: e.tensor_tensor(out=tB, in0=x2, in1=cosb,
                                                          op=ALU.mult))
                v_rope_rd = sV.n
                P.add("vector", lambda e: e.tensor_tensor(out=rp2, in0=tA, in1=tB,
                                                          op=ALU.add))
                PT.note_read(bank_src, sV)
                for h in range(2):
                    P.add("scalar", lambda e, h=h: e.activation(
                        out=a_ksq[:, h * 128:(h + 1) * 128],
                        in_=pbk[:, h * 128:(h + 1) * 128], func=AF.Square,
                        accum_out=c2("ssq", h, 1)),
                        waits=[(sV, v_rope_rd)] if h == 0 else [])
                PT.note_read(bank_src, sS)
                P.add("vector", lambda e: e.reduce_sum(
                    out=c2("sum"), in_=a_rope.rearrange("p (h f) -> p h f", h=2),
                    axis=AX.X))
                P.add("vector", lambda e: e.tensor_scalar(
                    out=c2("mean"), in0=c2("sum"), scalar1=cn("inv128"),
                    scalar2=None, op0=ALU.mult))
                s_ssq = sS.n
                P.add("vector", lambda e: e.tensor_scalar(
                    out=c2("e2"), in0=c2("ssq"), scalar1=cn("inv128"),
                    scalar2=None, op0=ALU.mult), waits=[(sS, s_ssq)])
                P.add("vector", lambda e: e.tensor_mul(out=c2("var"), in0=c2("mean"),
                                                       in1=c2("mean")))
                P.add("vector", lambda e: e.tensor_sub(out=c2("var"), in0=c2("e2"),
                                                       in1=c2("var")))
                v_var = sV.n
                P.add("scalar", lambda e: e.activation(out=c2("var"), in_=c2("var"),
                                                       func=AF.Ln, bias=cn("eps")),
                      waits=[(sV, v_var)])
                P.add("scalar", lambda e: e.activation(out=c2("rstd"), in_=c2("var"),
                                                       func=AF.Exp,
                                                       scale=cn("neghalf")))
                s_rstd = sS.n
                ks = a_kstg[par]
                for h in range(2):
                    P.add("vector", lambda e, h=h, ks=ks: e.tensor_scalar(
                        out=ks[:, h * 128:(h + 1) * 128],
                        in0=a_rope[:, h * 128:(h + 1) * 128],
                        scalar1=c2("mean", h, 1), scalar2=c2("rstd", h, 1),
                        op0=ALU.subtract, op1=ALU.mult),
                        waits=([(sS, s_rstd)] +
                               ([(sP, kstg_rel[par])] if kstg_rel[par] else []))
                        if h == 0 else [(sS, s_rstd)])
                v_kstg = sV.n

                def do_transposes():
                    trv = pb[bank_tr][:].bitcast(BF16)
                    for h in range(2):
                        P.add("tensor", lambda e, h=h, ks=ks: e.transpose(
                            out=trv[:, h * 128:(h + 1) * 128],
                            in_=ks[:, h * 128:(h + 1) * 128], identity=v_idbf),
                            waits=([(sV, v_kstg)] + PT.write_waits(bank_tr))
                            if h == 0 else [])
                    kstg_rel[par] = sP.n
                    p_tr = sP.n
                    for h in range(2):
                        P.add("scalar", lambda e, h=h, dc=dst_col, dst=dst:
                              e.activation(
                                  out=dst[:, dc[h]:dc[h] + 128],
                                  in_=trv[:, h * 128:(h + 1) * 128],
                                  func=AF.Identity, scale=v_lns, bias=v_lnb),
                              waits=([(sP, p_tr)] + list(evac_extra))
                              if h == 0 else [])
                    PT.note_read(bank_tr, sS)
                return do_transposes

            def gen_proj_poly(e8, att_done):
                kcol = E + e8 * 256
                vcol = 2 * E + e8 * 256
                qcol = e8 * 256
                lcs = ld(a_cos, D["cos_t"][:], waits=[(sV, sV.n)], sem=sL[10])
                lcs = ld(a_sin, D["sin_t"][:], sem=sL[10])
                v_wk = load_w2(w_kv, "wqkv", kcol, 256, 0, list(att_done), sL[12])
                yield
                v_wv = load_w2(w_kv, "wqkv", vcol, 256, 256, [], sL[12])
                yield
                pend_tr = None
                for nt in range(NTIL):
                    par = nt % 8
                    kbk = 0 if nt % 2 == 0 else 1
                    lx = ld(a_xnt[par], S["xnT_sc"][nt],
                            waits=[(sP, xnt_rel[par])] if xnt_rel[par] else [],
                            sem=sL[20 + par])
                    wk0 = PT.write_waits(kbk) + [(sL[20 + par], lx), (sV, v_wv)]
                    if nt == 0:
                        wk0.append((sL[10], sL[10].n))
                    for ci in range(16):
                        P.add("tensor", lambda e, ci=ci, par=par, kbk=kbk: e.matmul(
                            out=pb[kbk][:], lhsT=a_xnt[par][:, ci * 128:(ci + 1) * 128],
                            rhs=w_kv[:, ci * 512:(ci + 1) * 512],
                            start=(ci == 0), stop=(ci == 15)),
                            waits=wk0 if ci == 0 else [])
                    p_kv = sP.n
                    xnt_rel[par] = p_kv
                    dcols = [h * NT + nt * 128 for h in range(2)]
                    fin = rope_ln2(kbk, 0, 5, a_cos[:, nt * 64:(nt + 1) * 64],
                                   a_sin[:, nt * 64:(nt + 1) * 64], p_khat,
                                   dcols, p_kv, nt % 2,
                                   evac_extra=att_done if nt == 0 else ())
                    P.add("scalar", lambda e, kbk=kbk, nt=nt: e.activation(
                        out=p_vtok[:, nt * 256:(nt + 1) * 256],
                        in_=pb[kbk][:, 256:512], func=AF.Copy),
                        waits=[(sP, p_kv)] + (list(att_done) if nt == 0 else []))
                    PT.note_read(kbk, sS)
                    if pend_tr is not None:
                        pend_tr()
                    pend_tr = fin
                    yield
                pend_tr()
                lco = ld(a_cos[:, 0:NOTIL * 64], D["cos_o"][:],
                         waits=[(sV, sV.n)], sem=sL[10])
                lco = ld(a_sin[:, 0:NOTIL * 64], D["sin_o"][:], sem=sL[10])
                v_wq = load_w2(w_q, "wqkv", qcol, 256, 0, [(sP, sP.n)], sL[13])
                yield
                pend_tr = None
                for nt in range(NOTIL):
                    par = nt % 8
                    kbk = 0 if nt % 2 == 0 else 1
                    lx = ld(a_xnt[par], S["xnTo_sc"][nt],
                            waits=[(sP, xnt_rel[par])] if xnt_rel[par] else [],
                            sem=sL[20 + par])
                    wk0 = PT.write_waits(kbk) + [(sL[20 + par], lx), (sV, v_wq)]
                    if nt == 0:
                        wk0.append((sL[10], sL[10].n))
                    for ci in range(16):
                        P.add("tensor", lambda e, ci=ci, par=par, kbk=kbk: e.matmul(
                            out=pb[kbk][:, 0:256],
                            lhsT=a_xnt[par][:, ci * 128:(ci + 1) * 128],
                            rhs=w_q[:, ci * 256:(ci + 1) * 256],
                            start=(ci == 0), stop=(ci == 15)),
                            waits=wk0 if ci == 0 else [])
                    p_q = sP.n
                    xnt_rel[par] = p_q
                    dcols = [h * NO + nt * 128 for h in range(2)]
                    fin = rope_ln2(kbk, 0, 5, a_cos[:, nt * 64:(nt + 1) * 64],
                                   a_sin[:, nt * 64:(nt + 1) * 64], p_qhat,
                                   dcols, p_q, nt % 2,
                                   evac_extra=att_done if nt == 0 else ())
                    if pend_tr is not None:
                        pend_tr()
                    pend_tr = fin
                    yield
                pend_tr()

            def gen_proj_sm(e8, att_done):
                qcol = e8 * 256
                v_wk = load_w2(w_kv, "wk_t", qcol, 256, 0, list(att_done), sL[12])
                yield
                v_wv = load_w2(w_kv, "wv_t", qcol, 256, 256, [], sL[12])
                yield
                for g in range(NTIL // 4):
                    gpar = g % 2
                    lx = []
                    for i in range(4):
                        sl = gpar * 4 + i
                        nt = g * 4 + i
                        ld(a_xnt[sl], S["xn1T_sc"][nt],
                           waits=[(sP, xnt_rel[sl])] if xnt_rel[sl] else [],
                           sem=sL[20 + sl])
                        lx.append((sL[20 + sl], sL[20 + sl].n))

                    for m in range(2):
                        kbk = m % 2
                        wk0 = PT.write_waits(kbk) + lx + [(sV, v_wk)]
                        for ci in range(16):
                            P.add("tensor", lambda e, ci=ci, m=m, kbk=kbk,
                                  gpar=gpar: e.matmul(
                                      out=pb[kbk][:],
                                      lhsT=w_kv[:, ci * 512 + m * 128:ci * 512 + m * 128 + 128],
                                      rhs=a_xnb[:, gpar * 4 * 2048:(gpar * 4 + 4) * 2048]
                                      .rearrange("p (b f) -> p b f", b=4)
                                      [:, :, ci * 128:(ci + 1) * 128],
                                      start=(ci == 0), stop=(ci == 15)),
                                  waits=wk0 if ci == 0 else [])
                        p_k = sP.n
                        P.add("scalar", lambda e, m=m, g=g, kbk=kbk: e.activation(
                            out=s_khat[:, m * NT + g * 512:m * NT + (g + 1) * 512],
                            in_=pb[kbk][:], func=AF.Copy),
                            waits=[(sP, p_k)] +
                            (list(att_done) if (g == 0 and m == 0) else []))
                        PT.note_read(kbk, sS)
                        yield
                    for i in range(4):
                        sl = gpar * 4 + i
                        nt = g * 4 + i
                        vbk = i % 2
                        wk0 = PT.write_waits(vbk) + [lx[i], (sV, v_wv)]
                        for ci in range(16):
                            P.add("tensor", lambda e, ci=ci, sl=sl, vbk=vbk:
                                  e.matmul(
                                      out=pb[vbk][:, 0:256],
                                      lhsT=a_xnt[sl][:, ci * 128:(ci + 1) * 128],
                                      rhs=w_kv[:, ci * 512 + 256:(ci + 1) * 512],
                                      start=(ci == 0), stop=(ci == 15)),
                                  waits=wk0 if ci == 0 else [])
                        p_v = sP.n
                        P.add("scalar", lambda e, nt=nt, vbk=vbk: e.activation(
                            out=s_vtok[:, nt * 256:(nt + 1) * 256],
                            in_=pb[vbk][:, 0:256], func=AF.Copy),
                            waits=[(sP, p_v)] + (list(att_done) if nt == 0 else []))
                        PT.note_read(vbk, sS)
                        yield
                    rel = sP.n
                    for i in range(4):
                        xnt_rel[gpar * 4 + i] = rel
                v_wq = load_w2(w_q, "wq_t", qcol, 256, 0, [(sP, sP.n)], sL[13])
                yield
                for g in range(NOTIL // 4):
                    gpar = g % 2
                    lx = []
                    for i in range(4):
                        sl = gpar * 4 + i
                        nt = g * 4 + i
                        ld(a_xnt[sl], S["xn1To_sc"][nt],
                           waits=[(sP, xnt_rel[sl])] if xnt_rel[sl] else [],
                           sem=sL[20 + sl])
                        lx.append((sL[20 + sl], sL[20 + sl].n))
                    for m in range(2):
                        kbk = m % 2
                        wk0 = PT.write_waits(kbk) + lx + [(sV, v_wq)]
                        for ci in range(16):
                            P.add("tensor", lambda e, ci=ci, m=m, kbk=kbk,
                                  gpar=gpar: e.matmul(
                                      out=pb[kbk][:],
                                      lhsT=w_q[:, ci * 256 + m * 128:ci * 256 + m * 128 + 128],
                                      rhs=a_xnb[:, gpar * 4 * 2048:(gpar * 4 + 4) * 2048]
                                      .rearrange("p (b f) -> p b f", b=4)
                                      [:, :, ci * 128:(ci + 1) * 128],
                                      start=(ci == 0), stop=(ci == 15)),
                                  waits=wk0 if ci == 0 else [])
                        p_k = sP.n
                        P.add("scalar", lambda e, m=m, g=g, kbk=kbk: e.activation(
                            out=s_qhat[:, m * NO + g * 512:m * NO + (g + 1) * 512],
                            in_=pb[kbk][:], func=AF.Copy),
                            waits=[(sP, p_k)] +
                            (list(att_done) if (g == 0 and m == 0) else []))
                        PT.note_read(kbk, sS)
                        yield
                    rel = sP.n
                    for i in range(4):
                        xnt_rel[gpar * 4 + i] = rel

            def attention(e8, khat, vtok, qhat, kind, ctx_sc, ready_kq, filler):
                def pull():
                    if filler[0] is not None:
                        try:
                            next(filler[0])
                        except StopIteration:
                            filler[0] = None
                ctxp = 0
                for s in range(NS):
                    kt = W_SLOT[s] // 128
                    ktw = kt // 2
                    lm = ld(a_mask.rearrange("p (t f) -> p t f", t=8),
                            D["masks"][s].rearrange("t p f -> p t f"),
                            waits=[(sV, sV.n)], sem=sL[11])
                    for hl in range(2):
                        cp = ctxp % 2
                        cbk = 4
                        ctxp += 1
                        P.add("gpsimd", lambda e: e.memset(a_acc, 0.0),
                              waits=[(sP, acc_rel[0])] if acc_rel[0] else [])
                        g_ms = sG.n
                        avv = {}
                        accv = {}
                        accvv = {}
                        sq2v = {}
                        pend = None

                        def emit_av(jw, at_ready, last):
                            wav = [at_ready]
                            if jw == 0:
                                wav += PT.write_waits(cbk) + [(sS, ready_kq[1])]
                            for half in range(2):
                                j = 2 * jw + half
                                P.add("tensor", lambda e, j=j, hl=hl, ja=jw % 3,
                                      half=half, jw=jw, last=last:
                                      e.matmul(out=pb[cbk][:],
                                               lhsT=vtok[:, j * 256 + hl * 128:j * 256 + (hl + 1) * 128],
                                               rhs=a_at[ja][:, half * 512:(half + 1) * 512],
                                               start=(jw == 0 and half == 0),
                                               stop=(last and half == 1)),
                                      waits=wav if half == 0 else [])
                            avv[jw] = sP.n

                        for jw in range(ktw):
                            wqk = PT.write_waits(2) + PT.write_waits(3)
                            if jw == 0:
                                wqk += [(sS, ready_kq[0]), (sL[11], lm)]
                            for half in range(2):
                                j = 2 * jw + half
                                P.add("tensor", lambda e, j=j, hl=hl, s=s,
                                      half=half: e.matmul(
                                          out=pb[2 + half][:],
                                          lhsT=khat[:, hl * NT + j * 128:hl * NT + (j + 1) * 128],
                                          rhs=qhat[:, hl * NO + s * QS:hl * NO + (s + 1) * QS],
                                          start=True, stop=True),
                                      waits=wqk if half == 0 else [])
                            p_qk = sP.n
                            if kind == "poly":
                                war_sq = [(sV, sq2v[jw - 2])] if jw - 2 in sq2v else []
                                P.add("scalar", lambda e, jb=jw % 2:
                                      e.activation(out=a_sq[jb], in_=pw[1][:],
                                                   func=AF.Square),
                                      waits=[(sP, p_qk)] + war_sq, nochain=True)
                                PT.note_read(2, sS)
                                PT.note_read(3, sS)
                                s_sq1 = sS.n
                                war_at = []
                                if jw - 3 in avv:
                                    war_at += [(sP, avv[jw - 3]), (sG, accv[jw - 3])]
                                P.add("vector", lambda e, jb=jw % 2, ja=jw % 3:
                                      e.tensor_mul(out=a_at[ja], in0=a_sq[jb],
                                                   in1=a_sq[jb]),
                                      waits=[(sS, s_sq1)] + war_at, nochain=True)
                                sq2v[jw] = sV.n
                            else:
                                war_at = []
                                if jw - 3 in avv:
                                    war_at += [(sP, avv[jw - 3]), (sG, accv[jw - 3]),
                                               (sV, accvv[jw - 3])]
                                P.add("scalar", lambda e, ja=jw % 3:
                                      e.activation(out=a_at[ja], in_=pw[1][:],
                                                   func=AF.Exp,
                                                   scale=cn("isq128")),
                                      waits=[(sP, p_qk)] + war_at, nochain=True)
                                PT.note_read(2, sS)
                                PT.note_read(3, sS)
                            if jw >= ktw - 4:
                                mi = jw - (ktw - 4)
                                w_at = [(sV, sq2v[jw])] if kind == "poly" \
                                    else [(sS, sS.n)]
                                P.add("vector", lambda e, ja=jw % 3, mi=mi:
                                      e.tensor_mul(out=a_at[ja], in0=a_at[ja],
                                                   in1=a_mask[:, mi * 1024:(mi + 1) * 1024]),
                                      waits=w_at, nochain=True)
                            at_ready = (sV, sV.n) if (kind == "poly" or jw >= ktw - 4) \
                                else (sS, sS.n)
                            wacc = [] if kind == "poly" and jw < ktw - 4 else [at_ready]
                            P.add("vector", lambda e, ja=jw % 3: e.tensor_add(
                                out=a_acc[:, 0:512], in0=a_acc[:, 0:512],
                                in1=a_at[ja][:, 0:512]),
                                waits=wacc + ([(sG, g_ms)] if jw == 0 else []))
                            accvv[jw] = sV.n
                            P.add("gpsimd", lambda e, ja=jw % 3: e.tensor_add(
                                out=a_acc[:, 512:1024], in0=a_acc[:, 512:1024],
                                in1=a_at[ja][:, 512:1024]),
                                waits=[at_ready])
                            accv[jw] = sG.n
                            if pend is not None:
                                emit_av(pend[0], pend[1], False)
                            pend = (jw, at_ready)
                            pull()
                        emit_av(pend[0], pend[1], True)
                        P.add("tensor", lambda e: e.matmul(
                            out=pb[6][0:1, :], lhsT=v_ones_col,
                            rhs=a_acc[:, 0:512], start=True, stop=False),
                            waits=[(sG, accv[ktw - 1]), (sV, sV.n)] +
                            PT.write_waits(6))
                        P.add("tensor", lambda e: e.matmul(
                            out=pb[6][0:1, :], lhsT=v_ones_col,
                            rhs=a_acc[:, 512:1024], start=False, stop=True))
                        acc_rel[0] = sP.n
                        p_den = sP.n
                        P.add("scalar", lambda e: e.activation(
                            out=a_dr[0:1, 0:512], in_=pb[6][0:1, :], func=AF.Ln),
                            waits=[(sP, p_den)])
                        PT.note_read(6, sS)
                        P.add("scalar", lambda e: e.activation(
                            out=a_dr[0:1, 512:1024], in_=a_dr[0:1, 0:512],
                            func=AF.Exp, scale=cn("negone")[0:1, :]))
                        s_rec = sS.n
                        P.add("tensor", lambda e: e.matmul(
                            out=pb[7][:], lhsT=v_ones_row[0:1, 0:128],
                            rhs=a_dr[0:1, 512:1024], start=True, stop=True),
                            waits=[(sS, s_rec)] + PT.write_waits(7))
                        p_bc = sP.n
                        P.add("vector", lambda e: e.tensor_copy(out=a_rbc,
                                                                in_=pb[7][:]),
                              waits=[(sP, p_bc)])
                        PT.note_read(7, sV)
                        P.add("vector", lambda e, cp=cp:
                              e.tensor_mul(out=a_ctxe[cp], in0=pb[cbk][:],
                                           in1=a_rbc),
                              waits=[(sP, avv[ktw - 1])] +
                              ([(sL[14 + cp], ctxe_rel[cp])] if ctxe_rel[cp] else []))
                        PT.note_read(cbk, sV)
                        ctxe_rel[cp] = st(ctx_sc[e8 * 2 + hl][:, s * QS:(s + 1) * QS],
                                          a_ctxe[cp], waits=[(sV, sV.n)],
                                          sem=sL[14 + cp])
                        pull()
                while filler[0] is not None:
                    try:
                        next(filler[0])
                    except StopIteration:
                        filler[0] = None

            att_done_p = [(sP, 0)]
            att_done_s = [(sP, 0)]
            gp = gen_proj_poly(0, [])
            for _ in gp:
                pass
            ready_p = [sS.n, sS.n]
            for e8 in range(8):
                fil = [gen_proj_sm(e8, att_done_s)]
                attention(e8, p_khat, p_vtok, p_qhat, "poly", S["ctxp_sc"],
                          ready_p, fil)
                att_done_p = [(sP, sP.n)]
                ready_s = [sS.n, sS.n]
                fil2 = [gen_proj_poly(e8 + 1, att_done_p) if e8 < 7 else None]
                if fil2[0] is None:
                    fil2 = [iter(())]
                attention(e8, s_khat, s_vtok, s_qhat, "sm", S["ctxs_sc"],
                          ready_s, fil2)
                att_done_s = [(sP, sP.n)]
                ready_p = [sS.n, sS.n]
            barrier()
            ar.pop()

            if upto >= 2:
                ar.push()
                b_wo = ar.bf(16 * 2048)
                b_wst = ar.f32(2048)
                b_ctx = [ar.bf(2048) for _ in range(2)]
                b_s = [ar.f32(2048) for _ in range(2)]
                wosrc = D["wout"].rearrange("(c p) m -> p c m", p=128)
                for cc in range(0, 16, 1):
                    lw = ld(b_wst.rearrange("p (c m) -> p c m", c=1),
                            wosrc[:, cc:cc + 1, :], waits=[(sV, sV.n)], sem=sL[12])
                    P.add("vector", lambda e, cc=cc: e.tensor_copy(
                        out=b_wo[:, cc * 2048:(cc + 1) * 2048], in_=b_wst),
                        waits=[(sL[12], lw)])
                v_wo = sV.n
                ctx_rel = [0, 0]
                s_rel = [0, 0]
                for nt in range(NOTIL):
                    par = nt % 2
                    lx = ld(b_ctx[par].rearrange("p (h f) -> p h f", h=16),
                            S["ctxp_sc"][:, :, nt * 128:(nt + 1) * 128]
                            .rearrange("h p f -> p h f"),
                            waits=[(sP, ctx_rel[par])] if ctx_rel[par] else [],
                            sem=sL[16 + par])
                    for eo in range(4):
                        obk = (0 if par == 0 else 2) + (eo % 2)
                        wk0 = PT.write_waits(obk) + \
                            [(sL[16 + par], lx), (sV, v_wo)]
                        for ci in range(16):
                            P.add("tensor", lambda e, ci=ci, par=par, eo=eo, obk=obk:
                                  e.matmul(out=pb[obk][:],
                                           lhsT=b_ctx[par][:, ci * 128:(ci + 1) * 128],
                                           rhs=b_wo[:, ci * 2048 + eo * 512:ci * 2048 + (eo + 1) * 512],
                                           start=(ci == 0), stop=(ci == 15)),
                                  waits=wk0 if ci == 0 else [])
                        ctx_rel[par] = sP.n
                        p_o = sP.n
                        P.add("vector", lambda e, par=par, eo=eo, obk=obk:
                              e.tensor_copy(out=b_s[par][:, eo * 512:(eo + 1) * 512],
                                            in_=pb[obk][:]),
                              waits=[(sP, p_o)] +
                              ([(sL[18 + par], s_rel[par])] if (s_rel[par] and eo == 0) else []))
                        PT.note_read(obk, sV)
                    s_rel[par] = st(S["s_sc"][nt], b_s[par], waits=[(sV, sV.n)],
                                    sem=sL[18 + par])
                barrier()
                ar.pop()

            if upto >= 4:
                ar.push()
                x2T = ar.bf(16 * 2048)
                ar.push()
                d_wo = ar.bf(16 * 2048)
                d_wst = ar.f32(2048)
                d_ctx = [ar.bf(2048) for _ in range(2)]
                d_xno = [ar.f32(2048) for _ in range(2)]
                d_xatt = [ar.f32(2048) for _ in range(2)]
                d_xn2s = ar.f32(2048)
                d_sm = ar.f32(16)
                wosrc = D["wo_t"].rearrange("(c p) m -> p c m", p=128)
                for cc in range(16):
                    lw = ld(d_wst.rearrange("p (c m) -> p c m", c=1),
                            wosrc[:, cc:cc + 1, :], waits=[(sV, sV.n)], sem=sL[12])
                    P.add("vector", lambda e, cc=cc: e.tensor_copy(
                        out=d_wo[:, cc * 2048:(cc + 1) * 2048], in_=d_wst),
                        waits=[(sL[12], lw)])
                v_wo = sV.n
                ctx_rel = [0, 0]
                xno_rel = [0, 0]
                xat_rel = [0, 0]
                for nt in range(NOTIL):
                    par = nt % 2
                    lx = ld(d_ctx[par].rearrange("p (h f) -> p h f", h=16),
                            S["ctxs_sc"][:, :, nt * 128:(nt + 1) * 128]
                            .rearrange("h p f -> p h f"),
                            waits=[(sP, ctx_rel[par])] if ctx_rel[par] else [],
                            sem=sL[16 + par])
                    ln = ld(d_xno[par], S["xno_sc"][nt],
                            waits=[(sV, xno_rel[par])] if xno_rel[par] else [],
                            sem=sL[24 + par])
                    for eo in range(4):
                        obk = (0 if par == 0 else 2) + (eo % 2)
                        wk0 = PT.write_waits(obk) + \
                            [(sL[16 + par], lx), (sV, v_wo)]
                        for ci in range(16):
                            P.add("tensor", lambda e, ci=ci, par=par, eo=eo, obk=obk:
                                  e.matmul(out=pb[obk][:],
                                           lhsT=d_ctx[par][:, ci * 128:(ci + 1) * 128],
                                           rhs=d_wo[:, ci * 2048 + eo * 512:ci * 2048 + (eo + 1) * 512],
                                           start=(ci == 0), stop=(ci == 15)),
                                  waits=wk0 if ci == 0 else [])
                        ctx_rel[par] = sP.n
                        p_o = sP.n
                        P.add("vector", lambda e, par=par, eo=eo, obk=obk:
                              e.tensor_add(out=d_xatt[par][:, eo * 512:(eo + 1) * 512],
                                           in0=pb[obk][:],
                                           in1=d_xno[par][:, eo * 512:(eo + 1) * 512]),
                              waits=[(sP, p_o), (sL[24 + par], ln)] +
                              ([(sL[26 + par], xat_rel[par])] if (xat_rel[par] and eo == 0) else []))
                        PT.note_read(obk, sV)
                    xno_rel[par] = sV.n
                    v_xat = sV.n
                    xat_rel[par] = st(S["xattn_sc"][nt], d_xatt[par],
                                      waits=[(sV, v_xat)], sem=sL[26 + par])
                    def d2(j):
                        return d_sm[:, j:j + 1]
                    P.add("scalar", lambda e, par=par: e.activation(
                        out=d_xn2s, in_=d_xatt[par], func=AF.Square,
                        accum_out=d2(0)), waits=[(sV, v_xat)])
                    s_sq = sS.n
                    P.add("vector", lambda e, par=par: e.reduce_sum(
                        out=d2(1), in_=d_xatt[par], axis=AX.X))
                    P.add("vector", lambda e: e.tensor_scalar(
                        out=d2(2), in0=d2(1), scalar1=cn("inv2048"), scalar2=None,
                        op0=ALU.mult))
                    P.add("vector", lambda e: e.tensor_scalar(
                        out=d2(3), in0=d2(0), scalar1=cn("inv2048"), scalar2=None,
                        op0=ALU.mult), waits=[(sS, s_sq)])
                    P.add("vector", lambda e: e.tensor_mul(out=d2(4), in0=d2(2),
                                                           in1=d2(2)))
                    P.add("vector", lambda e: e.tensor_sub(out=d2(4), in0=d2(3),
                                                           in1=d2(4)))
                    v_var = sV.n
                    P.add("scalar", lambda e: e.activation(out=d2(5), in_=d2(4),
                                                           func=AF.Ln, bias=cn("eps")),
                          waits=[(sV, v_var)])
                    P.add("scalar", lambda e: e.activation(out=d2(6), in_=d2(5),
                                                           func=AF.Exp,
                                                           scale=cn("neghalf")))
                    s_r2 = sS.n
                    P.add("vector", lambda e, par=par: e.tensor_scalar(
                        out=d_xn2s, in0=d_xatt[par], scalar1=d2(2), scalar2=d2(6),
                        op0=ALU.subtract, op1=ALU.mult), waits=[(sS, s_r2)])
                    v_x2s = sV.n
                    for grp in range(4):
                        tbk = 4 + (grp % 2)
                        for k4 in range(4):
                            ci = grp * 4 + k4
                            P.add("tensor", lambda e, ci=ci, k4=k4, tbk=tbk:
                                  e.transpose(out=pb[tbk][:, k4 * 128:(k4 + 1) * 128],
                                              in_=d_xn2s[:, ci * 128:(ci + 1) * 128],
                                              identity=v_id32),
                                  waits=([(sV, v_x2s)] + PT.write_waits(tbk))
                                  if k4 == 0 else [])
                        p_tr = sP.n
                        for k4 in range(4):
                            ci = grp * 4 + k4
                            P.add("scalar", lambda e, ci=ci, k4=k4, tbk=tbk, nt=nt:
                                  e.activation(out=x2T[:, nt * 2048 + ci * 128:nt * 2048 + (ci + 1) * 128],
                                               in_=pb[tbk][:, k4 * 128:(k4 + 1) * 128],
                                               func=AF.Identity,
                                               scale=v_g2[:, ci:ci + 1],
                                               bias=v_b2[:, ci:ci + 1]),
                                  waits=[(sP, p_tr)] if k4 == 0 else [])
                        PT.note_read(tbk, sS)
                barrier()
                ar.pop()

                ar.push()
                e_wst = [ar.f32(2048) for _ in range(2)]
                e_wf1 = [ar.bf(2048) for _ in range(2)]
                e_sqg = ar.f32(512)
                e_z = ar.f32(512)
                e_u = ar.f32(512)
                e_sg = ar.f32(512)
                e_g = [ar.bf(2048) for _ in range(2)]
                e_cw = ar.f32(2048)
                e_cb = ar.bf(2048)
                f1src = D["wff1_t"].rearrange("(c p) m -> p c m", p=128)
                f2src = D["wff2_t"].rearrange("(m p) e -> p m e", p=128)
                wf1_rel = [0, 0]
                wf1_rel_p = [0, 0]
                g_rel = [0, 0]
                cw_rel = [0]
                for mm in range(64):
                    par = mm % 2
                    lw = ld(e_wst[par].rearrange("p (c m) -> p c m", c=16),
                            f1src[:, :, mm * 128:(mm + 1) * 128],
                            waits=[(sV, wf1_rel[par])] if wf1_rel[par] else [],
                            sem=sL[12 + par])
                    P.add("vector", lambda e, par=par: e.tensor_copy(
                        out=e_wf1[par], in_=e_wst[par]),
                        waits=[(sL[12 + par], lw)] +
                        ([(sP, wf1_rel_p[par])] if wf1_rel_p[par] else []))
                    wf1_rel[par] = sV.n
                    v_w = sV.n
                    for g in range(4):
                        hbk = g % 2
                        wk0 = PT.write_waits(hbk) + [(sV, v_w)]
                        for ci in range(16):
                            P.add("tensor", lambda e, ci=ci, g=g, hbk=hbk, par=par:
                                  e.matmul(out=pb[hbk][:],
                                           lhsT=e_wf1[par][:, ci * 128:(ci + 1) * 128],
                                           rhs=x2T[:, g * 4 * 2048:(g * 4 + 4) * 2048]
                                           .rearrange("p (b f) -> p b f", b=4)
                                           [:, :, ci * 128:(ci + 1) * 128],
                                           start=(ci == 0), stop=(ci == 15)),
                                  waits=wk0 if ci == 0 else [])
                        wf1_rel_p[par] = sP.n
                        p_h = sP.n
                        P.add("scalar", lambda e, hbk=hbk: e.activation(
                            out=e_sqg, in_=pb[hbk][:], func=AF.Square),
                            waits=[(sP, p_h), (sV, sV.n)])
                        s_hsq = sS.n
                        P.add("vector", lambda e: e.tensor_scalar(
                            out=e_z, in0=e_sqg, scalar1=cn("c0447"),
                            scalar2=cn("one"), op0=ALU.mult, op1=ALU.add),
                            waits=[(sS, s_hsq)])
                        P.add("vector", lambda e, hbk=hbk: e.tensor_mul(
                            out=e_u, in0=pb[hbk][:], in1=e_z),
                            waits=[(sS, sS.n)])
                        v_u = sV.n
                        P.add("scalar", lambda e: e.activation(
                            out=e_sg, in_=e_u, func=AF.Sigmoid,
                            scale=cn("sig_gelu")), waits=[(sV, v_u)])
                        s_sg = sS.n
                        P.add("vector", lambda e, hbk=hbk, g=g, par=par:
                              e.tensor_mul(out=e_g[par][:, g * 512:(g + 1) * 512],
                                           in0=pb[hbk][:], in1=e_sg),
                              waits=[(sS, s_sg)] +
                              ([(sL[16 + par], g_rel[par])] if (g_rel[par] and g == 0) else []))
                        PT.note_read(hbk, sV)
                    g_rel[par] = st(S["g_sc"][mm], e_g[par], waits=[(sV, sV.n)],
                                    sem=sL[16 + par])
                    lc = ld(e_cw, f2src[:, mm, :],
                            waits=[(sG, cw_rel[0])] if cw_rel[0] else [], sem=sL[28])
                    P.add("gpsimd", lambda e: e.tensor_copy(out=e_cb, in_=e_cw),
                          waits=[(sL[28], lc), (sL[29], sL[29].n)])
                    cw_rel[0] = sG.n
                    st(S["wff2bf_sc"][mm], e_cb, waits=[(sG, sG.n)], sem=sL[29])
                barrier()
                ar.pop()

                ar.push()
                f_g = ar.bf(64 * 512)
                f_w2 = [ar.bf(1024) for _ in range(2)]
                fr_xat = [ar.f32(512) for _ in range(2)]
                fr_xn = [ar.f32(512) for _ in range(2)]
                fr_s = [ar.f32(512) for _ in range(2)]
                fr_x = [ar.f32(512) for _ in range(2)]
                fr_t0 = [ar.f32(512) for _ in range(2)]
                fr_t1 = [ar.f32(512) for _ in range(2)]
                fr_t2 = [ar.f32(512) for _ in range(2)]
                fr_sg2 = [ar.f32(512) for _ in range(2)]
                fr_out = [ar.f32(512) for _ in range(2)]
                f_t0, f_t1, f_t2, f_sg2 = fr_t0[0], fr_t1[0], fr_t2[0], fr_sg2[0]
                f_wup = ar.bf(2048)
                f_bias = ar.f32(2048)
                lwu = ld(f_bias, D["bias_bc"][:], sem=sL[30])
                lu = ld(f_t0, D["wup128"][:, 0:512], sem=sL[31])
                lu2 = ld(f_t1, D["wup128"][:, 512:1024], sem=sL[31],
                         waits=[(sL[31], lu)])
                lu3 = ld(f_t2, D["wup128"][:, 1024:1536], sem=sL[31],
                         waits=[(sL[31], lu2)])
                lu4 = ld(f_sg2, D["wup128"][:, 1536:2048], sem=sL[31],
                         waits=[(sL[31], lu3)])
                P.add("vector", lambda e, b=f_t0: e.tensor_copy(
                    out=f_wup[:, 0:512], in_=b), waits=[(sL[31], lu)])
                P.add("vector", lambda e, b=f_t1: e.tensor_copy(
                    out=f_wup[:, 512:1024], in_=b), waits=[(sL[31], lu2)])
                P.add("vector", lambda e, b=f_t2: e.tensor_copy(
                    out=f_wup[:, 1024:1536], in_=b), waits=[(sL[31], lu3)])
                P.add("vector", lambda e, b=f_sg2: e.tensor_copy(
                    out=f_wup[:, 1536:2048], in_=b), waits=[(sL[31], lu4)])
                v_wup_done = sV.n
                if "lr1" in dumps:
                    wup2_d = nc.dram_tensor("dump_wup_early", [128, 2048], BF16,
                                            kind="ExternalOutput").ap()
                    P.add("scalar", lambda e: e.dma_start(out=wup2_d, in_=f_wup),
                          waits=[(sV, v_wup_done)], incs=[(sW, 16)], dma=True)
                g_blk_rel = [0]
                w2_rel = [0, 0]
                ep_ctr = [0]
                ep_rel = [(0, 0), (0, 0)]
                for g in range(4):
                    glw = []
                    for mq in range(0, 64, 16):
                        glw.append(ld(
                            f_g.rearrange("p (m f) -> p m f", m=64)[:, mq:mq + 16, :],
                            S["g_sc"][mq:mq + 16, :, g * 512:(g + 1) * 512]
                            .rearrange("m p f -> p m f"),
                            waits=[(sP, g_blk_rel[0])] if (g_blk_rel[0] and mq == 0) else [],
                            sem=sL[32]))
                    lg = (sL[32], sL[32].n)
                    for ep in range(2):
                        for mm in range(64):
                            par = mm % 2
                            lw2 = ld(f_w2[par], S["wff2bf_sc"][mm][:, ep * 1024:(ep + 1) * 1024],
                                     waits=[(sP, w2_rel[par])] if w2_rel[par] else [],
                                     sem=sL[34 + par])
                            first_w = [(sL[34 + par], lw2), lg]
                            for nt4 in range(4):
                                for eoi in range(2):
                                    bk = nt4 * 2 + eoi
                                    wk0 = list(first_w) if (nt4 == 0 and eoi == 0) else []
                                    if mm == 0:
                                        wk0 += PT.write_waits(bk)
                                    P.add("tensor", lambda e, mm=mm, nt4=nt4, eoi=eoi,
                                          ep=ep, bk=bk, par=par, g=g: e.matmul(
                                              out=pb[bk][:],
                                              lhsT=f_g[:, mm * 512 + nt4 * 128:mm * 512 + (nt4 + 1) * 128],
                                              rhs=f_w2[par][:, eoi * 512:(eoi + 1) * 512],
                                              start=(mm == 0), stop=(mm == 63)),
                                          waits=wk0)
                            w2_rel[par] = sP.n
                        p_ff2 = sP.n
                        g_blk_rel[0] = p_ff2
                        for nt4 in range(4):
                            nt = g * 4 + nt4
                            for eoi in range(2):
                                eo = ep * 2 + eoi
                                bk = nt4 * 2 + eoi
                                col = slice(eo * 512, (eo + 1) * 512)
                                ip = ep_ctr[0] % 2
                                ep_ctr[0] += 1
                                f_xat = fr_xat[ip]; f_xn = fr_xn[ip]
                                f_s = fr_s[ip]; f_x = fr_x[ip]
                                f_t0 = fr_t0[ip]; f_t1 = fr_t1[ip]
                                f_t2 = fr_t2[ip]; f_sg2 = fr_sg2[ip]
                                f_out = fr_out[ip]
                                rv, rg = ep_rel[ip]
                                lda = ld(f_xat, S["xattn_sc"][nt][:, col],
                                         waits=[(sV, rv)], sem=sL[36 + ip])
                                ldn = ld(f_xn, S["xno_sc"][nt][:, col],
                                         waits=[(sV, rv)], sem=sL[38 + ip])
                                lds = ld(f_s, S["s_sc"][nt][:, col],
                                         waits=[(sG, rg)], sem=sL[52 + ip])
                                ldx = ld(f_x, D["x_own"].rearrange(
                                    "(n p) e -> n p e", p=128)[nt][:, col],
                                    waits=[(sG, rg)], sem=sL[54 + ip])
                                m0c = v_m0c[:, nt:nt + 1]
                                m1c = v_m1c[:, nt:nt + 1]
                                m2c = v_m2c[:, nt:nt + 1]
                                m01c = v_m01c[:, nt:nt + 1]
                                P.add("vector", lambda e, m01c=m01c, f_t0=f_t0,
                                      f_xn=f_xn: e.tensor_scalar(
                                    out=f_t0, in0=f_xn, scalar1=m01c, scalar2=None,
                                    op0=ALU.mult),
                                    waits=[(sL[38 + ip], ldn), (sL[30], lwu),
                                           (sV, v_wup_done), (sG, rg)])
                                P.add("vector", lambda e, m2c=m2c, f_t1=f_t1,
                                      f_xat=f_xat, f_t0=f_t0:
                                      e.scalar_tensor_tensor(
                                          out=f_t1, in0=f_xat, scalar=m2c, in1=f_t0,
                                          op0=ALU.mult, op1=ALU.add),
                                      waits=[(sL[36 + ip], lda)])
                                P.add("vector", lambda e, m2c=m2c, bk=bk, f_t2=f_t2,
                                      f_t1=f_t1: e.scalar_tensor_tensor(
                                          out=f_t2, in0=pb[bk][:], scalar=m2c,
                                          in1=f_t1, op0=ALU.mult, op1=ALU.add),
                                      waits=[(sP, p_ff2)])
                                PT.note_read(bk, sV)
                                P.add("tensor", lambda e, nt=nt, eo=eo, bk=bk:
                                      e.matmul(out=pb[bk][:],
                                               lhsT=v_lr1[:, nt * 128:(nt + 1) * 128],
                                               rhs=f_wup[:, eo * 512:(eo + 1) * 512],
                                               start=True, stop=True),
                                      waits=[(sV, sV.n)] + PT.write_waits(bk))
                                p_lr = sP.n
                                P.add("vector", lambda e, m1c=m1c, bk=bk, f_t2=f_t2:
                                      e.scalar_tensor_tensor(
                                          out=f_t2, in0=pb[bk][:], scalar=m1c,
                                          in1=f_t2, op0=ALU.mult, op1=ALU.add),
                                      waits=[(sP, p_lr)])
                                PT.note_read(bk, sV)
                                P.add("vector", lambda e, m0c=m0c, col=col,
                                      f_t0=f_t0, f_t2=f_t2:
                                      e.scalar_tensor_tensor(
                                          out=f_t0, in0=f_bias[:, col], scalar=m0c,
                                          in1=f_t2, op0=ALU.mult, op1=ALU.add),
                                      waits=[(sG, rg)])
                                g_sel = sV.n
                                P.add("scalar", lambda e, f_sg2=f_sg2, f_t0=f_t0:
                                      e.activation(
                                          out=f_sg2, in_=f_t0, func=AF.Sigmoid),
                                      waits=[(sV, g_sel), (sG, rg)])
                                s_sig = sS.n
                                P.add("gpsimd", lambda e, f_t1=f_t1, f_t0=f_t0,
                                      f_sg2=f_sg2: e.tensor_mul(
                                          out=f_t1, in0=f_t0, in1=f_sg2),
                                      waits=[(sS, s_sig)])
                                P.add("gpsimd", lambda e, f_t1=f_t1, f_s=f_s:
                                      e.tensor_mul(
                                          out=f_t1, in0=f_t1, in1=f_s),
                                      waits=[(sL[52 + ip], lds)])
                                P.add("gpsimd", lambda e, f_out=f_out, f_t1=f_t1,
                                      f_x=f_x: e.tensor_add(
                                          out=f_out, in0=f_t1, in1=f_x),
                                      waits=[(sL[54 + ip], ldx),
                                             (sL[33 + ip * 18], sL[33 + ip * 18].n)])
                                v_o = sG.n
                                st(out_d[nt][:, col], f_out, waits=[(sG, v_o)],
                                   sem=sL[33 + ip * 18])
                                ep_rel[ip] = (sV.n, sG.n)
                barrier()
                ar.pop()
                ar.pop()

        if "lr1" in dumps:
            lr1_d = nc.dram_tensor("dump_lr1", [128, 2048], BF16,
                                   kind="ExternalOutput").ap()
            P.add("scalar", lambda e: e.dma_start(out=lr1_d, in_=v_lr1),
                  waits=[(s, s.n) for s in (sP, sS, sV, sG)], incs=[(sW, 16)],
                  dma=True)
            wup_d = nc.dram_tensor("dump_wup", [128, 2048], BF16,
                                   kind="ExternalOutput").ap()
            P.add("scalar", lambda e: e.dma_start(out=wup_d, in_=f_wup),
                  incs=[(sW, 16)], dma=True)
            msk_d = nc.dram_tensor("dump_masks", [128, 64], F32,
                                   kind="ExternalOutput").ap()
            P.add("scalar", lambda e: e.dma_start(out=msk_d[:, 0:16], in_=v_m0c),
                  incs=[(sW, 16)], dma=True)
            P.add("scalar", lambda e: e.dma_start(out=msk_d[:, 16:32], in_=v_m1c),
                  incs=[(sW, 16)], dma=True)
            P.add("scalar", lambda e: e.dma_start(out=msk_d[:, 32:48], in_=v_m2c),
                  incs=[(sW, 16)], dma=True)
            P.add("scalar", lambda e: e.dma_start(out=msk_d[:, 48:64], in_=v_m01c),
                  incs=[(sW, 16)], dma=True)
        for nm in dumps:
            if nm == "lr1":
                continue
            P.add("sync", lambda e, d=dump_aps[nm], s=S[nm]: e.dma_start(
                out=d[:], in_=s[:]), incs=[(sD, 16)], dma=True)

        P.add("sync", None, waits=[(s, s.n) for s in ALLSEMS])

        block = es.enter_context(nc.Block())

        @block.sync
        def _(sync):
            P.emit("sync", sync)

        @block.tensor
        def _(tensor):
            P.emit("tensor", tensor)

        @block.scalar
        def _(scalar):
            P.emit("scalar", scalar)

        @block.vector
        def _(vector):
            P.emit("vector", vector)

        @block.gpsimd
        def _(gpsimd):
            P.emit("gpsimd", gpsimd)

    return nc


def kernel(**inputs):
    in_maps = host_prep(inputs)
    nc = build_nc()
    res = run_bass_kernel_spmd(nc, in_maps, core_ids=list(range(8)))
    out = np.empty((B, N, E), np.float32)
    for core in range(8):
        b = core // 2
        slots = SLOTS_A if core % 2 == 0 else SLOTS_B
        o = np.asarray(res.results[core]["out"]).reshape(NO, E)
        for s, ch in enumerate(slots):
            out[b, 512 * ch:512 * ch + 512] = o[512 * s:512 * (s + 1)]
    return out


# version 11
# speedup vs baseline: 1.1014x; 1.1014x over previous
import numpy as np
import ml_dtypes
import concourse.bass as bass
import concourse.mybir as mybir
from concourse.bass_utils import run_bass_kernel_spmd
from contextlib import ExitStack

F32 = mybir.dt.float32
BF16 = mybir.dt.bfloat16
AF = mybir.ActivationFunctionType
ALU = mybir.AluOpType
AX = mybir.AxisListType

B, N, E, H, Dh = 4, 4096, 2048, 16, 128
NT = N
NO = 2048
NTIL = NT // 128
NOTIL = NO // 128
NS = 4
QS = 512
W_SLOT = [4096, 3072, 2048, 1024]
FF = 8192
EPS = 1e-5
SLOTS_A = [7, 4, 3, 0]
SLOTS_B = [6, 5, 2, 1]

CN = {"eps": 0, "neghalf": 1, "negone": 2, "inv128": 3, "inv2048": 4,
      "c0447": 5, "one": 6, "sig_gelu": 7, "isq128": 8, "half": 9}
CN_VALS = {"eps": EPS, "neghalf": -0.5, "negone": -1.0, "inv128": 1.0 / 128,
           "inv2048": 1.0 / 2048, "c0447": 0.044715, "one": 1.0,
           "sig_gelu": 1.5957691216057308, "isq128": 1.0 / np.sqrt(128.0),
           "half": 0.5}


class Sem:
    def __init__(self, h):
        self.h = h
        self.n = 0


class Plan:
    ENGINES = ("sync", "tensor", "scalar", "vector", "gpsimd")

    def __init__(self):
        self.ev = {e: [] for e in self.ENGINES}
        self.esem = {}

    def add(self, engine, fn, waits=(), incs=(), dma=False, nochain=False):
        waits = list(waits)
        incs = list(incs)
        es = self.esem.get(engine)
        if es is not None and not dma and fn is not None:
            if engine != "tensor" and not nochain and es.n > 0:
                waits.append((es, es.n))
            incs.append((es, 1))
        for s, k in incs:
            s.n += k
        self.ev[engine].append((fn, [(s, v) for s, v in waits if v > 0],
                                [(s, k) for s, k in incs]))

    def emit(self, eng_name, eng):
        for fn, waits, incs in self.ev[eng_name]:
            for s, v in waits:
                eng.wait_ge(s.h, v)
            inst = fn(eng) if fn is not None else None
            if incs:
                if inst is None:
                    inst = eng.nop()
                for s, k in incs:
                    inst.then_inc(s.h, k)


class PsumTracker:

    def __init__(self, nbanks=8):
        self.rel = [[] for _ in range(nbanks)]

    def note_read(self, bank, sem):
        self.rel[bank].append((sem, sem.n))

    def write_waits(self, bank):
        w = self.rel[bank]
        self.rel[bank] = []
        return w


class Arena:
    def __init__(self, t, cols):
        self.t = t
        self.cols = cols
        self.off = 0
        self.marks = []

    def alloc(self, n_bf16):
        n = (n_bf16 + 15) & ~15
        assert self.off + n <= self.cols, f"arena overflow {self.off}+{n}>{self.cols}"
        o = self.off
        self.off += n
        return o

    def bf(self, n):
        o = self.alloc(n)
        return self.t[:, o:o + n]

    def f32(self, n):
        o = self.alloc(2 * n)
        return self.t[:, o:o + 2 * n].bitcast(F32)

    def push(self):
        self.marks.append(self.off)

    def pop(self):
        self.off = self.marks.pop()


def _mask_patterns():
    r = np.arange(1024)[:, None]
    q = np.arange(512)[None, :]
    m0 = (r <= q).astype(ml_dtypes.bfloat16)
    m512 = ((r - 512) <= q).astype(ml_dtypes.bfloat16)
    return np.ascontiguousarray(np.stack([m0, m512]).reshape(2, 8, 128, 512))


def _rope_tables():
    inv = 1.0 / 10000.0 ** (np.arange(0, Dh, 2, dtype=np.float32) / Dh)
    t = np.arange(NT, dtype=np.float32)[:, None] * inv[None, :]
    def shp(a):
        return np.ascontiguousarray(
            a.reshape(NTIL, 128, 64).transpose(1, 0, 2).reshape(128, NTIL * 64))
    return shp(np.cos(t).astype(np.float32)), shp(np.sin(t).astype(np.float32))


def _col128(v):
    return np.ascontiguousarray(np.asarray(v, np.float32).reshape(16, 128).T)


def host_prep(inputs):
    x = np.asarray(inputs["L0_attn_x"], np.float32)
    wrt = np.asarray(inputs["L0_ffn_w_router"], np.float32)
    wdn = np.asarray(inputs["L0_ffn_w_down"], np.float32)
    wup = np.asarray(inputs["L0_ffn_w_up"], np.float32)
    sbias = np.asarray(inputs["L0_ffn_simple_bias"], np.float32)

    wrt128 = np.zeros((E, 128), np.float32); wrt128[:, 0:3] = wrt.T
    wrt128 = np.ascontiguousarray(
        wrt128.reshape(16, 128, 128).transpose(1, 0, 2).reshape(128, 2048))
    wdn128 = np.zeros((E, 128), np.float32); wdn128[:, 0:32] = wdn.T
    wdn128 = np.ascontiguousarray(
        wdn128.reshape(16, 128, 128).transpose(1, 0, 2).reshape(128, 2048))
    wup128 = np.zeros((128, E), np.float32); wup128[0:32, :] = wup.T

    cos_t, sin_t = _rope_tables()
    pat = _mask_patterns()
    consts = np.zeros((128, 16), np.float32)
    for k, j in CN.items():
        consts[:, j] = CN_VALS[k]

    common = {
        "wqkv": np.ascontiguousarray(inputs["L0_attn_w_qkv"], dtype=np.float32),
        "wout": np.ascontiguousarray(inputs["L0_attn_w_out"], dtype=np.float32),
        "ln_s": np.asarray(inputs["L0_attn_ln_scale"], np.float32).reshape(128, 1),
        "ln_b": np.asarray(inputs["L0_attn_ln_bias"], np.float32).reshape(128, 1),
        "wq_t": np.ascontiguousarray(np.asarray(inputs["L0_ffn_wq"], np.float32).T),
        "wk_t": np.ascontiguousarray(np.asarray(inputs["L0_ffn_wk"], np.float32).T),
        "wv_t": np.ascontiguousarray(np.asarray(inputs["L0_ffn_wv"], np.float32).T),
        "wo_t": np.ascontiguousarray(np.asarray(inputs["L0_ffn_wo"], np.float32).T),
        "wff1_t": np.ascontiguousarray(np.asarray(inputs["L0_ffn_w_ff1"], np.float32).T),
        "wff2_t": np.ascontiguousarray(np.asarray(inputs["L0_ffn_w_ff2"], np.float32).T),
        "wrt128": wrt128, "wdn128": wdn128, "wup128": wup128,
        "g1": _col128(inputs["L0_ffn_ln1_scale"]), "b1": _col128(inputs["L0_ffn_ln1_bias"]),
        "g2": _col128(inputs["L0_ffn_ln2_scale"]), "b2": _col128(inputs["L0_ffn_ln2_bias"]),
        "cos_t": cos_t, "sin_t": sin_t,
        "id32": np.eye(128, dtype=np.float32),
        "idbf": np.eye(128, dtype=ml_dtypes.bfloat16),
        "ones_col": np.ones((128, 1), np.float32),
        "ones_row": np.ones((1, 128), np.float32),
        "bias_bc": np.ascontiguousarray(np.tile(sbias[None, :], (128, 1))),
        "consts": consts,
    }

    in_maps = []
    for core in range(8):
        b = core // 2
        slots = SLOTS_A if core % 2 == 0 else SLOTS_B
        masks = np.empty((NS, 8, 128, 512), ml_dtypes.bfloat16)
        for s, ch in enumerate(slots):
            shift = 512 * (ch + 1) + 512 - W_SLOT[s]
            masks[s] = pat[0 if shift == 0 else 1]
        m = dict(common)
        m["x_full"] = np.ascontiguousarray(x[b])
        inv = 1.0 / 10000.0 ** (np.arange(0, Dh, 2, dtype=np.float32) / Dh)
        pos = np.concatenate([512 * c + np.arange(512) for c in slots]).astype(np.float32)
        tt = pos[:, None] * inv[None, :]
        m["cos_o"] = np.ascontiguousarray(
            np.cos(tt).astype(np.float32).reshape(NOTIL, 128, 64).transpose(1, 0, 2).reshape(128, NOTIL * 64))
        m["sin_o"] = np.ascontiguousarray(
            np.sin(tt).astype(np.float32).reshape(NOTIL, 128, 64).transpose(1, 0, 2).reshape(128, NOTIL * 64))
        m["x_own"] = np.ascontiguousarray(
            np.concatenate([x[b, 512 * c:512 * c + 512] for c in slots], 0))
        m["masks"] = np.ascontiguousarray(masks)
        in_maps.append(m)
    return in_maps


def build_nc(upto=99, dumps=(), p0n=(NTIL, NOTIL), p0parts=3):
    nc = bass.Bass("TRN2", target_bir_lowering=False, debug=False)

    D = {}
    def din(name, shape, dt):
        D[name] = nc.dram_tensor(name, list(shape), dt, kind="ExternalInput").ap()
    din("x_full", (NT, E), F32); din("x_own", (NO, E), F32)
    din("wqkv", (E, 3 * E), F32); din("wout", (E, E), F32)
    din("ln_s", (128, 1), F32); din("ln_b", (128, 1), F32)
    din("wq_t", (E, E), F32); din("wk_t", (E, E), F32)
    din("wv_t", (E, E), F32); din("wo_t", (E, E), F32)
    din("wff1_t", (E, FF), F32); din("wff2_t", (FF, E), F32)
    din("wrt128", (128, 2048), F32)
    din("wdn128", (128, 2048), F32); din("wup128", (128, E), F32)
    din("g1", (128, 16), F32); din("b1", (128, 16), F32)
    din("g2", (128, 16), F32); din("b2", (128, 16), F32)
    din("cos_t", (128, NTIL * 64), F32); din("sin_t", (128, NTIL * 64), F32)
    din("cos_o", (128, NOTIL * 64), F32); din("sin_o", (128, NOTIL * 64), F32)
    din("id32", (128, 128), F32); din("idbf", (128, 128), BF16)
    din("ones_col", (128, 1), F32); din("ones_row", (1, 128), F32)
    din("bias_bc", (128, E), F32); din("consts", (128, 16), F32)
    din("masks", (NS, 8, 128, 512), BF16)

    out_d = nc.dram_tensor("out", [NOTIL, 128, E], F32, kind="ExternalOutput").ap()

    S = {}
    def dsc(name, shape, dt):
        S[name] = nc.dram_tensor(name, list(shape), dt, kind="Internal").ap()
    dsc("xnT_sc", (NTIL, 128, E), BF16)
    dsc("xn1T_sc", (NTIL, 128, E), BF16)
    dsc("xnTo_sc", (NOTIL, 128, E), BF16)
    dsc("xn1To_sc", (NOTIL, 128, E), BF16)
    dsc("xno_sc", (NOTIL, 128, E), F32)
    dsc("ctxp_sc", (H, 128, NO), BF16)
    dsc("ctxs_sc", (H, 128, NO), BF16)
    dsc("s_sc", (NOTIL, 128, E), F32)
    dsc("xattn_sc", (NOTIL, 128, E), F32)
    dsc("g_sc", (FF // 128, 128, NO), BF16)
    dsc("wff2bf_sc", (FF // 128, 128, E), BF16)

    dump_aps = {nm: nc.dram_tensor("dump_" + nm, list(S[nm].shape), S[nm].dtype,
                                   kind="ExternalOutput").ap() for nm in dumps
                if nm != "lr1"}

    with ExitStack() as es:
        ar_t = es.enter_context(nc.sbuf_tensor("arena", [128, 100096], BF16))
        ar = Arena(ar_t, 100096)
        pw = [es.enter_context(nc.psum_tensor(f"pw{i}", [128, 1024], F32))
              for i in range(3)]
        pb = [pw[0][:, 0:512], pw[0][:, 512:1024],
              pw[1][:, 0:512], pw[1][:, 512:1024]] + \
             [es.enter_context(nc.psum_tensor(f"pb{i}", [128, 512], F32))[:]
              for i in range(4, 6)] + \
             [pw[2][:, 0:512], pw[2][:, 512:1024]]
        sD = Sem(es.enter_context(nc.semaphore("sD")))
        sW = Sem(es.enter_context(nc.semaphore("sW")))
        sP = Sem(es.enter_context(nc.semaphore("sP")))
        sS = Sem(es.enter_context(nc.semaphore("sS")))
        sV = Sem(es.enter_context(nc.semaphore("sV")))
        sG = Sem(es.enter_context(nc.semaphore("sG")))
        sL = [Sem(es.enter_context(nc.semaphore(f"sL{i}"))) for i in range(60)]
        ALLSEMS = tuple([sD, sW, sP, sS, sV, sG] + sL)
        P = Plan()
        P.esem = {"tensor": sP, "scalar": sS, "vector": sV, "gpsimd": sG}
        PT = PsumTracker()

        def snap():
            return [(s, s.n) for s in ALLSEMS]

        def gate(snp, engines=Plan.ENGINES):
            for en in engines:
                P.add(en, None, waits=snp)

        def barrier():
            sB_t = [(s, s.n) for s in ALLSEMS]
            gate(sB_t)

        def ld(dst_ap, src_ap, waits=(), sem=None):
            sem = sem if sem is not None else sD
            P.add("sync", lambda e, d=dst_ap, s=src_ap: e.dma_start(out=d, in_=s),
                  waits=[(sem, sem.n)] + list(waits), incs=[(sem, 16)], dma=True)
            return sem.n

        def st(dst_ap, src_ap, waits=(), sem=None):
            sem = sem if sem is not None else sW
            P.add("scalar", lambda e, d=dst_ap, s=src_ap: e.dma_start(out=d, in_=s),
                  waits=[(sem, sem.n), (sS, sS.n)] + list(waits),
                  incs=[(sem, 16)], dma=True)
            return sem.n

        v_id32 = ar.f32(128)
        v_idbf = ar.bf(128)
        v_consts = ar.f32(16)
        v_lns = ar.f32(1); v_lnb = ar.f32(1)
        v_g1 = ar.f32(16); v_b1 = ar.f32(16)
        v_g2 = ar.f32(16); v_b2 = ar.f32(16)
        v_ones_col = ar.f32(1)
        v_ones_row = ar.f32(128)
        v_lr1 = ar.bf(2048)
        v_m2c = ar.f32(16); v_m1c = ar.f32(16); v_m0c = ar.f32(16); v_m01c = ar.f32(16)

        def cn(name):
            return v_consts[:, CN[name]:CN[name] + 1]

        for dst, srcn in [(v_id32, "id32"), (v_idbf, "idbf"), (v_consts, "consts"),
                          (v_lns, "ln_s"), (v_lnb, "ln_b"), (v_g1, "g1"),
                          (v_b1, "b1"), (v_g2, "g2"), (v_b2, "b2"),
                          (v_ones_col, "ones_col")]:
            ld(dst, D[srcn][:])
        ld(v_ones_row[0:1, 0:128], D["ones_row"][:])
        w_persist = snap()
        gate(w_persist)

        ar.push()
        v_wrt = ar.f32(2048)
        v_wdn = ar.bf(2048)
        ld(v_wrt, D["wrt128"][:])
        ar.push()
        stg0 = ar.f32(2048)
        t1 = ld(stg0, D["wdn128"][:])
        P.add("vector", lambda e: e.tensor_copy(out=v_wdn, in_=stg0),
              waits=[(sD, t1)])
        ar.pop()
        gate(snap())
        xr = [ar.f32(2048) for _ in range(6)]
        junk = ar.f32(2048)
        xnr = [ar.f32(2048) for _ in range(3)]
        xn1r = [ar.f32(2048) for _ in range(3)]
        bstg = [ar.bf(2048) for _ in range(3)]
        b1stg = [ar.bf(2048) for _ in range(3)]
        x32stg = ar.f32(2048)
        rtmp = ar.f32(128)
        rl3 = ar.f32(16)
        sm = [ar.f32(32) for _ in range(2)]
        SC = {"ssum": 0, "ssq": 4, "m": 8, "e2": 12, "v": 16, "r": 20,
              "rr": 24, "s1": 28}

        xr_rel = [0] * 6
        bst_rel = [0, 0, 0]
        b1st_rel = [0, 0, 0]
        xno_rel = [0, 0, 0]
        xnr_tr = [0, 0, 0]
        xn1_tr = [0, 0, 0]

        def p0_stats_batch(b4, own, n_in_batch=4):
            Z = sm[b4 % 2]
            def c(nm, i=0, w=4):
                return Z[:, SC[nm] + i:SC[nm] + i + w]
            dvs = []
            for i in range(n_in_batch):
                nt = 4 * b4 + i
                par = nt % 6
                src = (D["x_own"] if own else D["x_full"]).rearrange(
                    "(n p) e -> n p e", p=128)[nt]
                xsem = sL[40 + par]
                dv = ld(xr[par], src, sem=xsem,
                        waits=[(sV, xr_rel[par])] if xr_rel[par] else [])
                dvs.append((xsem, dv))
                P.add("scalar", lambda e, par=par, i=i: e.activation(
                    out=junk, in_=xr[par], func=AF.Square,
                    accum_out=c("ssq", i, 1)), waits=[(xsem, dv)])
                P.add("vector", lambda e, par=par, i=i: e.reduce_sum(
                    out=c("ssum", i, 1), in_=xr[par], axis=AX.X),
                    waits=[(xsem, dv)])
            s_sq = sS.n
            P.add("vector", lambda e: e.tensor_scalar(out=c("m"), in0=c("ssum"),
                  scalar1=cn("inv2048"), scalar2=None, op0=ALU.mult))
            P.add("vector", lambda e: e.tensor_scalar(out=c("e2"), in0=c("ssq"),
                  scalar1=cn("inv2048"), scalar2=None, op0=ALU.mult),
                  waits=[(sS, s_sq)])
            P.add("vector", lambda e: e.tensor_mul(out=c("v"), in0=c("m"), in1=c("m")))
            P.add("vector", lambda e: e.tensor_sub(out=c("v"), in0=c("e2"), in1=c("v")))
            v_v = sV.n
            P.add("scalar", lambda e: e.activation(out=c("rr"), in_=c("v"), func=AF.Ln,
                                                   bias=cn("eps")),
                  waits=[(sV, v_v)])
            P.add("scalar", lambda e: e.activation(out=c("r"), in_=c("rr"), func=AF.Exp,
                                                   scale=cn("neghalf")))
            s_r = sS.n
            P.add("vector", lambda e: e.tensor_mul(out=c("rr"), in0=c("r"), in1=c("r")),
                  waits=[(sS, s_r)])
            P.add("vector", lambda e: e.tensor_mul(out=c("s1"), in0=c("rr"), in1=c("v")))
            v_s1 = sV.n
            P.add("scalar", lambda e: e.activation(out=c("rr"), in_=c("s1"), func=AF.Ln,
                                                   bias=cn("eps")),
                  waits=[(sV, v_s1)])
            P.add("scalar", lambda e: e.activation(out=c("s1"), in_=c("rr"),
                                                   func=AF.Exp, scale=cn("neghalf")))
            return s_r, sS.n

        def p0_tile(nt, own, s_r, s_rstd1):
            par = nt % 3
            xt = xr[nt % 6]
            Z = sm[(nt // 4) % 2]
            i = nt % 4
            def c(nm):
                return Z[:, SC[nm] + i:SC[nm] + i + 1]
            P.add("vector", lambda e: e.tensor_scalar(out=xnr[par], in0=xt,
                  scalar1=c("m"), scalar2=c("r"), op0=ALU.subtract, op1=ALU.mult),
                  waits=[(sS, s_r), (sP, xnr_tr[par])] +
                  ([(sL[49 + par], xno_rel[par])] if xno_rel[par] else []))
            v_xn = sV.n
            xr_rel[nt % 6] = v_xn
            P.add("vector", lambda e: e.tensor_scalar(out=xn1r[par], in0=xnr[par],
                  scalar1=c("s1"), scalar2=None, op0=ALU.mult),
                  waits=[(sS, s_rstd1), (sP, xn1_tr[par])])
            v_xn1 = sV.n
            bXi = par
            bYi = 3 + par
            ev_last = {}
            for half, (bi, srcbuf, v_src) in enumerate(
                    [(bXi, xnr[par], v_xn), (bYi, xn1r[par], v_xn1)]):
                bank = pb[bi]
                for grp in range(4):
                    for k4 in range(4):
                        ci = grp * 4 + k4
                        wt = []
                        if k4 == 0:
                            wt = [(sV, v_src)] + PT.write_waits(bi)
                        P.add("tensor", lambda e, ci=ci, k4=k4, bk=bank, sb=srcbuf:
                              e.transpose(out=bk[:, k4 * 128:(k4 + 1) * 128],
                                          in_=sb[:, ci * 128:(ci + 1) * 128],
                                          identity=v_id32),
                              waits=wt)
                    p_tr = sP.n
                    if half == 0:
                        xnr_tr[par] = sP.n
                        P.add("vector", lambda e, bk=bank, g=grp:
                              e.tensor_copy(out=bstg[par][:, g * 512:(g + 1) * 512],
                                            in_=bk[:]),
                              waits=[(sP, p_tr)] +
                              ([(sL[43 + par], bst_rel[par])]
                               if (grp == 0 and bst_rel[par]) else []))
                        PT.note_read(bi, sV)
                        if own and not (p0parts & 4):
                            P.add("vector", lambda e, bk=bank, g=grp:
                                  e.tensor_copy(out=x32stg[:, g * 512:(g + 1) * 512],
                                                in_=bk[:]),
                                  waits=[(sP, p_tr)])
                            PT.note_read(bi, sV)
                    else:
                        xn1_tr[par] = sP.n
                        for k4 in range(4):
                            ci = grp * 4 + k4
                            P.add("scalar", lambda e, bk=bank, ci=ci, k4=k4:
                                  e.activation(out=b1stg[par][:, ci * 128:(ci + 1) * 128],
                                               in_=bk[:, k4 * 128:(k4 + 1) * 128],
                                               func=AF.Identity,
                                               scale=v_g1[:, ci:ci + 1],
                                               bias=v_b1[:, ci:ci + 1]),
                                  waits=([(sP, p_tr)] +
                                         ([(sL[46 + par], b1st_rel[par])]
                                          if (grp == 0 and b1st_rel[par]) else []))
                                  if k4 == 0 else [])
                        PT.note_read(bi, sS)
                ev_last[0] = (sV, sV.n) if half == 0 else ev_last.get(0, (sV, sV.n))
                ev_last[1] = (sS, sS.n)
            if own:
                bst_rel[par] = st(S["xnTo_sc"][nt], bstg[par], waits=[ev_last[0]],
                                  sem=sL[43 + par])
                b1st_rel[par] = st(S["xn1To_sc"][nt], b1stg[par], waits=[ev_last[1]],
                                   sem=sL[46 + par])
                if not (p0parts & 8):
                    xno_rel[par] = st(S["xno_sc"][nt], xnr[par], waits=[(sV, v_xn)],
                                      sem=sL[49 + par])
            else:
                bst_rel[par] = st(S["xnT_sc"][nt], bstg[par], waits=[ev_last[0]],
                                  sem=sL[43 + par])
                b1st_rel[par] = st(S["xn1T_sc"][nt], b1stg[par], waits=[ev_last[1]],
                                   sem=sL[46 + par])
            return v_xn

        def p0_router_lr(nt, par):
            v_now = sV.n
            if not (p0parts & 1):
                if p0parts & 2:
                    _lowrank_only(nt, par)
                return
            for ci in range(16):
                P.add("tensor", lambda e, ci=ci: e.matmul(
                    out=pb[6][:, 0:128], lhsT=v_wrt[:, ci * 128:(ci + 1) * 128],
                    rhs=x32stg[:, ci * 128:(ci + 1) * 128],
                    start=(ci == 0), stop=(ci == 15)),
                    waits=([(sV, v_now)] + PT.write_waits(6)) if ci == 0 else [])
            p_rl = sP.n
            P.add("vector", lambda e: e.tensor_copy(out=rtmp, in_=pb[6][:, 0:128]),
                  waits=[(sP, p_rl)])
            PT.note_read(6, sV)
            v_rt = sV.n
            P.add("tensor", lambda e: e.transpose(out=pb[7][:, 0:128], in_=rtmp,
                                                  identity=v_id32),
                  waits=[(sV, v_rt)] + PT.write_waits(7))
            p_rt = sP.n
            P.add("vector", lambda e: e.tensor_copy(out=rl3[:, 0:3],
                                                    in_=pb[7][:, 0:3]),
                  waits=[(sP, p_rt)])
            PT.note_read(7, sV)
            def l(j):
                return rl3[:, j:j + 1]
            P.add("vector", lambda e: e.tensor_tensor(out=l(4), in0=l(2), in1=l(0), op=ALU.is_gt))
            P.add("vector", lambda e: e.tensor_tensor(out=l(5), in0=l(2), in1=l(1), op=ALU.is_gt))
            P.add("vector", lambda e, nt=nt: e.tensor_mul(out=v_m2c[:, nt:nt + 1], in0=l(4), in1=l(5)))
            P.add("vector", lambda e: e.tensor_tensor(out=l(6), in0=l(1), in1=l(0), op=ALU.is_gt))
            P.add("vector", lambda e: e.tensor_tensor(out=l(7), in0=l(1), in1=l(2), op=ALU.is_ge))
            P.add("vector", lambda e: e.tensor_mul(out=l(8), in0=l(6), in1=l(7)))
            P.add("vector", lambda e, nt=nt: e.tensor_scalar(
                out=v_m01c[:, nt:nt + 1], in0=v_m2c[:, nt:nt + 1],
                scalar1=cn("negone"), scalar2=cn("one"), op0=ALU.mult, op1=ALU.add))
            P.add("vector", lambda e, nt=nt: e.tensor_mul(
                out=v_m1c[:, nt:nt + 1], in0=l(8), in1=v_m01c[:, nt:nt + 1]))
            P.add("vector", lambda e, nt=nt: e.tensor_sub(
                out=v_m0c[:, nt:nt + 1], in0=v_m01c[:, nt:nt + 1],
                in1=v_m1c[:, nt:nt + 1]))
            if p0parts & 2:
                _lowrank_only(nt, par)

        def _lowrank_only(nt, par):
            s_bst = sV.n
            for ci in range(16):
                P.add("tensor", lambda e, ci=ci, par=par: e.matmul(
                    out=pb[6][:, 0:128], lhsT=v_wdn[:, ci * 128:(ci + 1) * 128],
                    rhs=bstg[par][:, ci * 128:(ci + 1) * 128],
                    start=(ci == 0), stop=(ci == 15)),
                    waits=([(sV, s_bst)] + PT.write_waits(6)) if ci == 0 else [])
            p_lr = sP.n
            P.add("vector", lambda e, nt=nt: e.tensor_copy(
                out=v_lr1[:, nt * 128:(nt + 1) * 128], in_=pb[6][:, 0:128]),
                waits=[(sP, p_lr)])
            PT.note_read(6, sV)

        snaps = {}
        for b4 in range(p0n[0] // 4):
            if b4 >= 2:
                gate(snaps[b4 - 2])
            s_r, s_rstd1 = p0_stats_batch(b4, own=False)
            for i in range(4):
                p0_tile(4 * b4 + i, False, s_r, s_rstd1)
            snaps[b4] = snap()
        barrier()
        snaps = {}
        for b4 in range(p0n[1] // 4):
            if b4 >= 2:
                gate(snaps[b4 - 2])
            s_r, s_rstd1 = p0_stats_batch(b4, own=True)
            for i in range(4):
                nt = 4 * b4 + i
                p0_tile(nt, True, s_r, s_rstd1)
                p0_router_lr(nt, nt % 3)
            snaps[b4] = snap()
        barrier()
        ar.pop()

        if upto >= 1:
            ar.push()
            a_mask = ar.bf(8 * 512)
            a_sq = [ar.bf(1024) for _ in range(2)]
            a_at = [ar.bf(1024) for _ in range(3)]
            a_acc = ar.f32(1024)
            a_rbc = ar.f32(512)
            a_ctxe = [ar.bf(512) for _ in range(2)]
            a_dr = ar.f32(1024)
            a_sm2 = ar.f32(32)
            SC2 = {"ssq": 0, "sum": 4, "mean": 8, "e2": 12, "var": 16, "rstd": 20}

            def c2(nm, h=0, w=2):
                return a_sm2[:, SC2[nm] + h:SC2[nm] + h + w]

            p_khat = ar.bf(2 * NT)
            p_vtok = ar.bf(NTIL * 256)
            p_qhat = ar.bf(2 * NO)
            s_khat = ar.bf(2 * NT)
            s_vtok = ar.bf(NTIL * 256)
            s_qhat = ar.bf(2 * NO)
            a_wst = ar.f32(1024)
            w_kv = ar.bf(16 * 512)
            w_q = ar.bf(16 * 256)
            a_xnb = ar.bf(8 * 2048)
            a_xnt = [a_xnb[:, i * 2048:(i + 1) * 2048] for i in range(8)]
            a_cos = ar.f32(NTIL * 64); a_sin = ar.f32(NTIL * 64)
            a_rope = ar.f32(256)
            a_kstg = [ar.bf(256) for _ in range(2)]
            a_ksq = a_sq[0].bitcast(F32)
            a_tmpA = a_sq[1].bitcast(F32)[:, 0:128]
            a_tmpB = a_sq[1].bitcast(F32)[:, 128:256]
            kstg_rel = [0, 0]
            xnt_rel = [0] * 8
            acc_rel = [0]
            ctxe_rel = [0, 0]

            def gen_load_w2(dst_bf, wname, col0, width, sub0, prev_users,
                            sems):
                wsrc = D[wname].rearrange("(c p) m -> p c m", p=128)
                st_half = [a_wst[:, 0:2 * width], a_wst[:, 2 * width:4 * width]]
                lw = {}
                vcp = {}

                def copy(i):
                    P.add("vector", lambda e, cc=2 * i, hb=i % 2: e.tensor_copy(
                        out=dst_bf.rearrange("p (c m) -> p c m", c=16)
                        [:, cc:cc + 2, sub0:sub0 + width],
                        in_=st_half[hb].rearrange("p (c m) -> p c m", c=2)),
                        waits=[(sems[i % 2], lw[i])])
                    vcp[i] = sV.n

                for i in range(8):
                    if i >= 2:
                        copy(i - 2)
                    war = [(sV, vcp[i - 2])] if i >= 2 else [(sV, sV.n)]
                    lw[i] = ld(st_half[i % 2].rearrange("p (c m) -> p c m", c=2),
                               wsrc[:, 2 * i:2 * i + 2, col0:col0 + width],
                               waits=war +
                               (list(prev_users) if i < 2 else []),
                               sem=sems[i % 2])
                    yield
                copy(6)
                yield
                copy(7)
                yield

            def rope_ln2(bank_src, col0, bank_tr, cosv, sinv, dst, dst_col,
                         p_src, par, evac_extra=()):
                pbk = pb[bank_src][:, col0:col0 + 256]
                x1 = pbk.rearrange("p (h z f) -> p h z f", h=2, z=2)[:, :, 0, :]
                x2 = pbk.rearrange("p (h z f) -> p h z f", h=2, z=2)[:, :, 1, :]
                cosb = cosv.rearrange("p (o f) -> p o f", o=1).broadcast_to([128, 2, 64])
                sinb = sinv.rearrange("p (o f) -> p o f", o=1).broadcast_to([128, 2, 64])
                rp1 = a_rope.rearrange("p (h z f) -> p h z f", h=2, z=2)[:, :, 0, :]
                rp2 = a_rope.rearrange("p (h z f) -> p h z f", h=2, z=2)[:, :, 1, :]
                tA = a_tmpA.rearrange("p (h f) -> p h f", h=2)
                tB = a_tmpB.rearrange("p (h f) -> p h f", h=2)
                P.add("vector", lambda e: e.tensor_tensor(out=tA, in0=x1, in1=cosb,
                                                          op=ALU.mult),
                      waits=[(sP, p_src)])
                P.add("vector", lambda e: e.tensor_tensor(out=tB, in0=x2, in1=sinb,
                                                          op=ALU.mult))
                P.add("vector", lambda e: e.tensor_tensor(out=rp1, in0=tA, in1=tB,
                                                          op=ALU.subtract))
                P.add("vector", lambda e: e.tensor_tensor(out=tA, in0=x1, in1=sinb,
                                                          op=ALU.mult))
                P.add("vector", lambda e: e.tensor_tensor(out=tB, in0=x2, in1=cosb,
                                                          op=ALU.mult))
                v_rope_rd = sV.n
                P.add("vector", lambda e: e.tensor_tensor(out=rp2, in0=tA, in1=tB,
                                                          op=ALU.add))
                PT.note_read(bank_src, sV)
                for h in range(2):
                    P.add("scalar", lambda e, h=h: e.activation(
                        out=a_ksq[:, h * 128:(h + 1) * 128],
                        in_=pbk[:, h * 128:(h + 1) * 128], func=AF.Square,
                        accum_out=c2("ssq", h, 1)),
                        waits=[(sV, v_rope_rd)] if h == 0 else [])
                PT.note_read(bank_src, sS)
                P.add("vector", lambda e: e.reduce_sum(
                    out=c2("sum"), in_=a_rope.rearrange("p (h f) -> p h f", h=2),
                    axis=AX.X))
                P.add("vector", lambda e: e.tensor_scalar(
                    out=c2("mean"), in0=c2("sum"), scalar1=cn("inv128"),
                    scalar2=None, op0=ALU.mult))
                s_ssq = sS.n
                P.add("vector", lambda e: e.tensor_scalar(
                    out=c2("e2"), in0=c2("ssq"), scalar1=cn("inv128"),
                    scalar2=None, op0=ALU.mult), waits=[(sS, s_ssq)])
                P.add("vector", lambda e: e.tensor_mul(out=c2("var"), in0=c2("mean"),
                                                       in1=c2("mean")))
                P.add("vector", lambda e: e.tensor_sub(out=c2("var"), in0=c2("e2"),
                                                       in1=c2("var")))
                v_var = sV.n
                P.add("scalar", lambda e: e.activation(out=c2("var"), in_=c2("var"),
                                                       func=AF.Ln, bias=cn("eps")),
                      waits=[(sV, v_var)])
                P.add("scalar", lambda e: e.activation(out=c2("rstd"), in_=c2("var"),
                                                       func=AF.Exp,
                                                       scale=cn("neghalf")))
                s_rstd = sS.n
                ks = a_kstg[par]
                for h in range(2):
                    P.add("vector", lambda e, h=h, ks=ks: e.tensor_scalar(
                        out=ks[:, h * 128:(h + 1) * 128],
                        in0=a_rope[:, h * 128:(h + 1) * 128],
                        scalar1=c2("mean", h, 1), scalar2=c2("rstd", h, 1),
                        op0=ALU.subtract, op1=ALU.mult),
                        waits=([(sS, s_rstd)] +
                               ([(sP, kstg_rel[par])] if kstg_rel[par] else []))
                        if h == 0 else [(sS, s_rstd)])
                v_kstg = sV.n

                def do_transposes():
                    trv = pb[bank_tr][:].bitcast(BF16)
                    for h in range(2):
                        P.add("tensor", lambda e, h=h, ks=ks: e.transpose(
                            out=trv[:, h * 128:(h + 1) * 128],
                            in_=ks[:, h * 128:(h + 1) * 128], identity=v_idbf),
                            waits=([(sV, v_kstg)] + PT.write_waits(bank_tr))
                            if h == 0 else [])
                    kstg_rel[par] = sP.n
                    p_tr = sP.n
                    for h in range(2):
                        P.add("scalar", lambda e, h=h, dc=dst_col, dst=dst:
                              e.activation(
                                  out=dst[:, dc[h]:dc[h] + 128],
                                  in_=trv[:, h * 128:(h + 1) * 128],
                                  func=AF.Identity, scale=v_lns, bias=v_lnb),
                              waits=([(sP, p_tr)] + list(evac_extra))
                              if h == 0 else [])
                    PT.note_read(bank_tr, sS)
                return do_transposes

            def gen_proj_poly(e8, att_done):
                kcol = E + e8 * 256
                vcol = 2 * E + e8 * 256
                qcol = e8 * 256
                lcs = ld(a_cos, D["cos_t"][:], waits=[(sV, sV.n)], sem=sL[10])
                lcs = ld(a_sin, D["sin_t"][:], sem=sL[10])
                lxq = {}

                def issue_x(nt, scr):
                    par = nt % 8
                    ld(a_xnt[par], scr[nt],
                       waits=[(sP, xnt_rel[par])] if xnt_rel[par] else [],
                       sem=sL[20 + par])
                    lxq[nt] = sL[20 + par].n
                yield from gen_load_w2(w_kv, "wqkv", kcol, 256, 0,
                                       list(att_done), (sL[56], sL[57]))
                yield from gen_load_w2(w_kv, "wqkv", vcol, 256, 256, [],
                                       (sL[56], sL[57]))
                v_wv = sV.n
                for nt in range(4):
                    issue_x(nt, S["xnT_sc"])
                yield
                pend_tr = None
                for nt in range(NTIL):
                    if nt + 4 < NTIL:
                        issue_x(nt + 4, S["xnT_sc"])
                    par = nt % 8
                    kbk = 0 if nt % 2 == 0 else 1
                    wk0 = PT.write_waits(kbk) + \
                        [(sL[20 + par], lxq[nt]), (sV, v_wv)]
                    if nt == 0:
                        wk0.append((sL[10], sL[10].n))
                    for ci in range(16):
                        P.add("tensor", lambda e, ci=ci, par=par, kbk=kbk: e.matmul(
                            out=pb[kbk][:], lhsT=a_xnt[par][:, ci * 128:(ci + 1) * 128],
                            rhs=w_kv[:, ci * 512:(ci + 1) * 512],
                            start=(ci == 0), stop=(ci == 15)),
                            waits=wk0 if ci == 0 else [])
                    p_kv = sP.n
                    xnt_rel[par] = p_kv
                    dcols = [h * NT + nt * 128 for h in range(2)]
                    fin = rope_ln2(kbk, 0, 5, a_cos[:, nt * 64:(nt + 1) * 64],
                                   a_sin[:, nt * 64:(nt + 1) * 64], p_khat,
                                   dcols, p_kv, nt % 2,
                                   evac_extra=att_done if nt == 0 else ())
                    P.add("scalar", lambda e, kbk=kbk, nt=nt: e.activation(
                        out=p_vtok[:, nt * 256:(nt + 1) * 256],
                        in_=pb[kbk][:, 256:512], func=AF.Copy),
                        waits=[(sP, p_kv)] + (list(att_done) if nt == 0 else []))
                    PT.note_read(kbk, sS)
                    if pend_tr is not None:
                        pend_tr()
                    pend_tr = fin
                    yield
                pend_tr()
                lco = ld(a_cos[:, 0:NOTIL * 64], D["cos_o"][:],
                         waits=[(sV, sV.n)], sem=sL[10])
                lco = ld(a_sin[:, 0:NOTIL * 64], D["sin_o"][:], sem=sL[10])
                for nt in range(4):
                    issue_x(nt, S["xnTo_sc"])
                yield from gen_load_w2(w_q, "wqkv", qcol, 256, 0,
                                       [(sP, sP.n)], (sL[56], sL[57]))
                v_wq = sV.n
                pend_tr = None
                for nt in range(NOTIL):
                    if nt + 4 < NOTIL:
                        issue_x(nt + 4, S["xnTo_sc"])
                    par = nt % 8
                    kbk = 0 if nt % 2 == 0 else 1
                    wk0 = PT.write_waits(kbk) + \
                        [(sL[20 + par], lxq[nt]), (sV, v_wq)]
                    if nt == 0:
                        wk0.append((sL[10], sL[10].n))
                    for ci in range(16):
                        P.add("tensor", lambda e, ci=ci, par=par, kbk=kbk: e.matmul(
                            out=pb[kbk][:, 0:256],
                            lhsT=a_xnt[par][:, ci * 128:(ci + 1) * 128],
                            rhs=w_q[:, ci * 256:(ci + 1) * 256],
                            start=(ci == 0), stop=(ci == 15)),
                            waits=wk0 if ci == 0 else [])
                    p_q = sP.n
                    xnt_rel[par] = p_q
                    dcols = [h * NO + nt * 128 for h in range(2)]
                    fin = rope_ln2(kbk, 0, 5, a_cos[:, nt * 64:(nt + 1) * 64],
                                   a_sin[:, nt * 64:(nt + 1) * 64], p_qhat,
                                   dcols, p_q, nt % 2,
                                   evac_extra=att_done if nt == 0 else ())
                    if pend_tr is not None:
                        pend_tr()
                    pend_tr = fin
                    yield
                pend_tr()

            def gen_proj_sm(e8, att_done):
                qcol = e8 * 256

                def issue_grp(g, scr, ng):
                    if g >= ng:
                        return []
                    out = []
                    for i in range(4):
                        sl = (g % 2) * 4 + i
                        ld(a_xnt[sl], scr[g * 4 + i],
                           waits=[(sP, xnt_rel[sl])] if xnt_rel[sl] else [],
                           sem=sL[20 + sl])
                        out.append((sL[20 + sl], sL[20 + sl].n))
                    return out
                yield from gen_load_w2(w_kv, "wk_t", qcol, 256, 0,
                                       list(att_done), (sL[56], sL[57]))
                v_wk = sV.n
                yield from gen_load_w2(w_kv, "wv_t", qcol, 256, 256, [],
                                       (sL[56], sL[57]))
                v_wv = sV.n
                nxt = issue_grp(0, S["xn1T_sc"], NTIL // 4)
                for g in range(NTIL // 4):
                    gpar = g % 2
                    lx = nxt
                    nxt = issue_grp(g + 1, S["xn1T_sc"], NTIL // 4)

                    for m in range(2):
                        kbk = m % 2
                        wk0 = PT.write_waits(kbk) + lx + [(sV, v_wk)]
                        for ci in range(16):
                            P.add("tensor", lambda e, ci=ci, m=m, kbk=kbk,
                                  gpar=gpar: e.matmul(
                                      out=pb[kbk][:],
                                      lhsT=w_kv[:, ci * 512 + m * 128:ci * 512 + m * 128 + 128],
                                      rhs=a_xnb[:, gpar * 4 * 2048:(gpar * 4 + 4) * 2048]
                                      .rearrange("p (b f) -> p b f", b=4)
                                      [:, :, ci * 128:(ci + 1) * 128],
                                      start=(ci == 0), stop=(ci == 15)),
                                  waits=wk0 if ci == 0 else [])
                        p_k = sP.n
                        P.add("scalar", lambda e, m=m, g=g, kbk=kbk: e.activation(
                            out=s_khat[:, m * NT + g * 512:m * NT + (g + 1) * 512],
                            in_=pb[kbk][:], func=AF.Copy),
                            waits=[(sP, p_k)] +
                            (list(att_done) if (g == 0 and m == 0) else []))
                        PT.note_read(kbk, sS)
                        yield
                    for i in range(4):
                        sl = gpar * 4 + i
                        nt = g * 4 + i
                        vbk = i % 2
                        wk0 = PT.write_waits(vbk) + [lx[i], (sV, v_wv)]
                        for ci in range(16):
                            P.add("tensor", lambda e, ci=ci, sl=sl, vbk=vbk:
                                  e.matmul(
                                      out=pb[vbk][:, 0:256],
                                      lhsT=a_xnt[sl][:, ci * 128:(ci + 1) * 128],
                                      rhs=w_kv[:, ci * 512 + 256:(ci + 1) * 512],
                                      start=(ci == 0), stop=(ci == 15)),
                                  waits=wk0 if ci == 0 else [])
                        p_v = sP.n
                        P.add("scalar", lambda e, nt=nt, vbk=vbk: e.activation(
                            out=s_vtok[:, nt * 256:(nt + 1) * 256],
                            in_=pb[vbk][:, 0:256], func=AF.Copy),
                            waits=[(sP, p_v)] + (list(att_done) if nt == 0 else []))
                        PT.note_read(vbk, sS)
                        yield
                    rel = sP.n
                    for i in range(4):
                        xnt_rel[gpar * 4 + i] = rel
                nxt = issue_grp(0, S["xn1To_sc"], NOTIL // 4)
                yield from gen_load_w2(w_q, "wq_t", qcol, 256, 0,
                                       [(sP, sP.n)], (sL[56], sL[57]))
                v_wq = sV.n
                for g in range(NOTIL // 4):
                    gpar = g % 2
                    lx = nxt
                    nxt = issue_grp(g + 1, S["xn1To_sc"], NOTIL // 4)
                    for m in range(2):
                        kbk = m % 2
                        wk0 = PT.write_waits(kbk) + lx + [(sV, v_wq)]
                        for ci in range(16):
                            P.add("tensor", lambda e, ci=ci, m=m, kbk=kbk,
                                  gpar=gpar: e.matmul(
                                      out=pb[kbk][:],
                                      lhsT=w_q[:, ci * 256 + m * 128:ci * 256 + m * 128 + 128],
                                      rhs=a_xnb[:, gpar * 4 * 2048:(gpar * 4 + 4) * 2048]
                                      .rearrange("p (b f) -> p b f", b=4)
                                      [:, :, ci * 128:(ci + 1) * 128],
                                      start=(ci == 0), stop=(ci == 15)),
                                  waits=wk0 if ci == 0 else [])
                        p_k = sP.n
                        P.add("scalar", lambda e, m=m, g=g, kbk=kbk: e.activation(
                            out=s_qhat[:, m * NO + g * 512:m * NO + (g + 1) * 512],
                            in_=pb[kbk][:], func=AF.Copy),
                            waits=[(sP, p_k)] +
                            (list(att_done) if (g == 0 and m == 0) else []))
                        PT.note_read(kbk, sS)
                        yield
                    rel = sP.n
                    for i in range(4):
                        xnt_rel[gpar * 4 + i] = rel

            def attention(e8, khat, vtok, qhat, kind, ctx_sc, ready_kq, filler):
                def pull():
                    if filler[0] is not None:
                        try:
                            next(filler[0])
                        except StopIteration:
                            filler[0] = None
                ctxp = 0
                for s in range(NS):
                    kt = W_SLOT[s] // 128
                    ktw = kt // 2
                    lm = ld(a_mask.rearrange("p (t f) -> p t f", t=8),
                            D["masks"][s].rearrange("t p f -> p t f"),
                            waits=[(sV, sV.n)], sem=sL[11])
                    for hl in range(2):
                        cp = ctxp % 2
                        cbk = 4
                        ctxp += 1
                        P.add("gpsimd", lambda e: e.memset(a_acc, 0.0),
                              waits=[(sP, acc_rel[0])] if acc_rel[0] else [])
                        g_ms = sG.n
                        avv = {}
                        accv = {}
                        accvv = {}
                        sq2v = {}
                        pend = None

                        def emit_av(jw, at_ready, last):
                            wav = [at_ready]
                            if jw == 0:
                                wav += PT.write_waits(cbk) + [(sS, ready_kq[1])]
                            for half in range(2):
                                j = 2 * jw + half
                                P.add("tensor", lambda e, j=j, hl=hl, ja=jw % 3,
                                      half=half, jw=jw, last=last:
                                      e.matmul(out=pb[cbk][:],
                                               lhsT=vtok[:, j * 256 + hl * 128:j * 256 + (hl + 1) * 128],
                                               rhs=a_at[ja][:, half * 512:(half + 1) * 512],
                                               start=(jw == 0 and half == 0),
                                               stop=(last and half == 1)),
                                      waits=wav if half == 0 else [])
                            avv[jw] = sP.n

                        for jw in range(ktw):
                            sc0 = 2 if jw % 2 == 0 else 6
                            scw = pw[1] if jw % 2 == 0 else pw[2]
                            wqk = PT.write_waits(sc0) + PT.write_waits(sc0 + 1)
                            if jw == 0:
                                wqk += [(sS, ready_kq[0]), (sL[11], lm)]
                            for half in range(2):
                                j = 2 * jw + half
                                P.add("tensor", lambda e, j=j, hl=hl, s=s,
                                      sc0=sc0, half=half: e.matmul(
                                          out=pb[sc0 + half][:],
                                          lhsT=khat[:, hl * NT + j * 128:hl * NT + (j + 1) * 128],
                                          rhs=qhat[:, hl * NO + s * QS:hl * NO + (s + 1) * QS],
                                          start=True, stop=True),
                                      waits=wqk if half == 0 else [])
                            p_qk = sP.n
                            if kind == "poly":
                                war_sq = [(sV, sq2v[jw - 2])] if jw - 2 in sq2v else []
                                P.add("scalar", lambda e, jb=jw % 2, scw=scw:
                                      e.activation(out=a_sq[jb], in_=scw[:],
                                                   func=AF.Square),
                                      waits=[(sP, p_qk)] + war_sq, nochain=True)
                                PT.note_read(sc0, sS)
                                PT.note_read(sc0 + 1, sS)
                                s_sq1 = sS.n
                                war_at = []
                                if jw - 3 in avv:
                                    war_at += [(sP, avv[jw - 3]), (sG, accv[jw - 3])]
                                P.add("vector", lambda e, jb=jw % 2, ja=jw % 3:
                                      e.tensor_mul(out=a_at[ja], in0=a_sq[jb],
                                                   in1=a_sq[jb]),
                                      waits=[(sS, s_sq1)] + war_at, nochain=True)
                                sq2v[jw] = sV.n
                            else:
                                war_at = []
                                if jw - 3 in avv:
                                    war_at += [(sP, avv[jw - 3]), (sG, accv[jw - 3]),
                                               (sV, accvv[jw - 3])]
                                P.add("scalar", lambda e, ja=jw % 3, scw=scw:
                                      e.activation(out=a_at[ja], in_=scw[:],
                                                   func=AF.Exp,
                                                   scale=cn("isq128")),
                                      waits=[(sP, p_qk)] + war_at, nochain=True)
                                PT.note_read(sc0, sS)
                                PT.note_read(sc0 + 1, sS)
                            if jw >= ktw - 4:
                                mi = jw - (ktw - 4)
                                w_at = [(sV, sq2v[jw])] if kind == "poly" \
                                    else [(sS, sS.n)]
                                P.add("vector", lambda e, ja=jw % 3, mi=mi:
                                      e.tensor_mul(out=a_at[ja], in0=a_at[ja],
                                                   in1=a_mask[:, mi * 1024:(mi + 1) * 1024]),
                                      waits=w_at, nochain=True)
                            at_ready = (sV, sV.n) if (kind == "poly" or jw >= ktw - 4) \
                                else (sS, sS.n)
                            wacc = [] if kind == "poly" and jw < ktw - 4 else [at_ready]
                            P.add("vector", lambda e, ja=jw % 3: e.tensor_add(
                                out=a_acc[:, 0:512], in0=a_acc[:, 0:512],
                                in1=a_at[ja][:, 0:512]),
                                waits=wacc + ([(sG, g_ms)] if jw == 0 else []))
                            accvv[jw] = sV.n
                            P.add("gpsimd", lambda e, ja=jw % 3: e.tensor_add(
                                out=a_acc[:, 512:1024], in0=a_acc[:, 512:1024],
                                in1=a_at[ja][:, 512:1024]),
                                waits=[at_ready])
                            accv[jw] = sG.n
                            if pend is not None:
                                emit_av(pend[0], pend[1], False)
                            pend = (jw, at_ready)
                            pull()
                        emit_av(pend[0], pend[1], True)
                        P.add("tensor", lambda e: e.matmul(
                            out=pb[6][0:1, :], lhsT=v_ones_col,
                            rhs=a_acc[:, 0:512], start=True, stop=False),
                            waits=[(sG, accv[ktw - 1]), (sV, sV.n)] +
                            PT.write_waits(6))
                        P.add("tensor", lambda e: e.matmul(
                            out=pb[6][0:1, :], lhsT=v_ones_col,
                            rhs=a_acc[:, 512:1024], start=False, stop=True))
                        acc_rel[0] = sP.n
                        p_den = sP.n
                        P.add("scalar", lambda e: e.activation(
                            out=a_dr[0:1, 0:512], in_=pb[6][0:1, :], func=AF.Ln),
                            waits=[(sP, p_den)])
                        PT.note_read(6, sS)
                        P.add("scalar", lambda e: e.activation(
                            out=a_dr[0:1, 512:1024], in_=a_dr[0:1, 0:512],
                            func=AF.Exp, scale=cn("negone")[0:1, :]))
                        s_rec = sS.n
                        P.add("tensor", lambda e: e.matmul(
                            out=pb[7][:], lhsT=v_ones_row[0:1, 0:128],
                            rhs=a_dr[0:1, 512:1024], start=True, stop=True),
                            waits=[(sS, s_rec)] + PT.write_waits(7))
                        p_bc = sP.n
                        P.add("vector", lambda e: e.tensor_copy(out=a_rbc,
                                                                in_=pb[7][:]),
                              waits=[(sP, p_bc)])
                        PT.note_read(7, sV)
                        P.add("vector", lambda e, cp=cp:
                              e.tensor_mul(out=a_ctxe[cp], in0=pb[cbk][:],
                                           in1=a_rbc),
                              waits=[(sP, avv[ktw - 1])] +
                              ([(sL[14 + cp], ctxe_rel[cp])] if ctxe_rel[cp] else []))
                        PT.note_read(cbk, sV)
                        ctxe_rel[cp] = st(ctx_sc[e8 * 2 + hl][:, s * QS:(s + 1) * QS],
                                          a_ctxe[cp], waits=[(sV, sV.n)],
                                          sem=sL[14 + cp])
                        pull()
                while filler[0] is not None:
                    try:
                        next(filler[0])
                    except StopIteration:
                        filler[0] = None

            att_done_p = [(sP, 0)]
            att_done_s = [(sP, 0)]
            gp = gen_proj_poly(0, [])
            for _ in gp:
                pass
            ready_p = [sS.n, sS.n]
            for e8 in range(8):
                fil = [gen_proj_sm(e8, att_done_s)]
                attention(e8, p_khat, p_vtok, p_qhat, "poly", S["ctxp_sc"],
                          ready_p, fil)
                att_done_p = [(sP, sP.n)]
                ready_s = [sS.n, sS.n]
                fil2 = [gen_proj_poly(e8 + 1, att_done_p) if e8 < 7 else None]
                if fil2[0] is None:
                    fil2 = [iter(())]
                attention(e8, s_khat, s_vtok, s_qhat, "sm", S["ctxs_sc"],
                          ready_s, fil2)
                att_done_s = [(sP, sP.n)]
                ready_p = [sS.n, sS.n]
            barrier()
            ar.pop()

            if upto >= 2:
                ar.push()
                b_wo = ar.bf(16 * 2048)
                b_wst = ar.f32(2048)
                b_ctx = [ar.bf(2048) for _ in range(2)]
                b_s = [ar.f32(2048) for _ in range(2)]
                wosrc = D["wout"].rearrange("(c p) m -> p c m", p=128)
                for cc in range(0, 16, 1):
                    lw = ld(b_wst.rearrange("p (c m) -> p c m", c=1),
                            wosrc[:, cc:cc + 1, :], waits=[(sV, sV.n)], sem=sL[12])
                    P.add("vector", lambda e, cc=cc: e.tensor_copy(
                        out=b_wo[:, cc * 2048:(cc + 1) * 2048], in_=b_wst),
                        waits=[(sL[12], lw)])
                v_wo = sV.n
                ctx_rel = [0, 0]
                s_rel = [0, 0]
                for nt in range(NOTIL):
                    par = nt % 2
                    lx = ld(b_ctx[par].rearrange("p (h f) -> p h f", h=16),
                            S["ctxp_sc"][:, :, nt * 128:(nt + 1) * 128]
                            .rearrange("h p f -> p h f"),
                            waits=[(sP, ctx_rel[par])] if ctx_rel[par] else [],
                            sem=sL[16 + par])
                    for eo in range(4):
                        obk = (0 if par == 0 else 2) + (eo % 2)
                        wk0 = PT.write_waits(obk) + \
                            [(sL[16 + par], lx), (sV, v_wo)]
                        for ci in range(16):
                            P.add("tensor", lambda e, ci=ci, par=par, eo=eo, obk=obk:
                                  e.matmul(out=pb[obk][:],
                                           lhsT=b_ctx[par][:, ci * 128:(ci + 1) * 128],
                                           rhs=b_wo[:, ci * 2048 + eo * 512:ci * 2048 + (eo + 1) * 512],
                                           start=(ci == 0), stop=(ci == 15)),
                                  waits=wk0 if ci == 0 else [])
                        ctx_rel[par] = sP.n
                        p_o = sP.n
                        P.add("vector", lambda e, par=par, eo=eo, obk=obk:
                              e.tensor_copy(out=b_s[par][:, eo * 512:(eo + 1) * 512],
                                            in_=pb[obk][:]),
                              waits=[(sP, p_o)] +
                              ([(sL[18 + par], s_rel[par])] if (s_rel[par] and eo == 0) else []))
                        PT.note_read(obk, sV)
                    s_rel[par] = st(S["s_sc"][nt], b_s[par], waits=[(sV, sV.n)],
                                    sem=sL[18 + par])
                barrier()
                ar.pop()

            if upto >= 4:
                ar.push()
                x2T = ar.bf(16 * 2048)
                ar.push()
                d_wo = ar.bf(16 * 2048)
                d_wst = ar.f32(2048)
                d_ctx = [ar.bf(2048) for _ in range(2)]
                d_xno = [ar.f32(2048) for _ in range(2)]
                d_xatt = [ar.f32(2048) for _ in range(2)]
                d_xn2s = ar.f32(2048)
                d_sm = ar.f32(16)
                wosrc = D["wo_t"].rearrange("(c p) m -> p c m", p=128)
                for cc in range(16):
                    lw = ld(d_wst.rearrange("p (c m) -> p c m", c=1),
                            wosrc[:, cc:cc + 1, :], waits=[(sV, sV.n)], sem=sL[12])
                    P.add("vector", lambda e, cc=cc: e.tensor_copy(
                        out=d_wo[:, cc * 2048:(cc + 1) * 2048], in_=d_wst),
                        waits=[(sL[12], lw)])
                v_wo = sV.n
                ctx_rel = [0, 0]
                xno_rel = [0, 0]
                xat_rel = [0, 0]
                for nt in range(NOTIL):
                    par = nt % 2
                    lx = ld(d_ctx[par].rearrange("p (h f) -> p h f", h=16),
                            S["ctxs_sc"][:, :, nt * 128:(nt + 1) * 128]
                            .rearrange("h p f -> p h f"),
                            waits=[(sP, ctx_rel[par])] if ctx_rel[par] else [],
                            sem=sL[16 + par])
                    ln = ld(d_xno[par], S["xno_sc"][nt],
                            waits=[(sV, xno_rel[par])] if xno_rel[par] else [],
                            sem=sL[24 + par])
                    for eo in range(4):
                        obk = (0 if par == 0 else 2) + (eo % 2)
                        wk0 = PT.write_waits(obk) + \
                            [(sL[16 + par], lx), (sV, v_wo)]
                        for ci in range(16):
                            P.add("tensor", lambda e, ci=ci, par=par, eo=eo, obk=obk:
                                  e.matmul(out=pb[obk][:],
                                           lhsT=d_ctx[par][:, ci * 128:(ci + 1) * 128],
                                           rhs=d_wo[:, ci * 2048 + eo * 512:ci * 2048 + (eo + 1) * 512],
                                           start=(ci == 0), stop=(ci == 15)),
                                  waits=wk0 if ci == 0 else [])
                        ctx_rel[par] = sP.n
                        p_o = sP.n
                        P.add("vector", lambda e, par=par, eo=eo, obk=obk:
                              e.tensor_add(out=d_xatt[par][:, eo * 512:(eo + 1) * 512],
                                           in0=pb[obk][:],
                                           in1=d_xno[par][:, eo * 512:(eo + 1) * 512]),
                              waits=[(sP, p_o), (sL[24 + par], ln)] +
                              ([(sL[26 + par], xat_rel[par])] if (xat_rel[par] and eo == 0) else []))
                        PT.note_read(obk, sV)
                    xno_rel[par] = sV.n
                    v_xat = sV.n
                    xat_rel[par] = st(S["xattn_sc"][nt], d_xatt[par],
                                      waits=[(sV, v_xat)], sem=sL[26 + par])
                    def d2(j):
                        return d_sm[:, j:j + 1]
                    P.add("scalar", lambda e, par=par: e.activation(
                        out=d_xn2s, in_=d_xatt[par], func=AF.Square,
                        accum_out=d2(0)), waits=[(sV, v_xat)])
                    s_sq = sS.n
                    P.add("vector", lambda e, par=par: e.reduce_sum(
                        out=d2(1), in_=d_xatt[par], axis=AX.X))
                    P.add("vector", lambda e: e.tensor_scalar(
                        out=d2(2), in0=d2(1), scalar1=cn("inv2048"), scalar2=None,
                        op0=ALU.mult))
                    P.add("vector", lambda e: e.tensor_scalar(
                        out=d2(3), in0=d2(0), scalar1=cn("inv2048"), scalar2=None,
                        op0=ALU.mult), waits=[(sS, s_sq)])
                    P.add("vector", lambda e: e.tensor_mul(out=d2(4), in0=d2(2),
                                                           in1=d2(2)))
                    P.add("vector", lambda e: e.tensor_sub(out=d2(4), in0=d2(3),
                                                           in1=d2(4)))
                    v_var = sV.n
                    P.add("scalar", lambda e: e.activation(out=d2(5), in_=d2(4),
                                                           func=AF.Ln, bias=cn("eps")),
                          waits=[(sV, v_var)])
                    P.add("scalar", lambda e: e.activation(out=d2(6), in_=d2(5),
                                                           func=AF.Exp,
                                                           scale=cn("neghalf")))
                    s_r2 = sS.n
                    P.add("vector", lambda e, par=par: e.tensor_scalar(
                        out=d_xn2s, in0=d_xatt[par], scalar1=d2(2), scalar2=d2(6),
                        op0=ALU.subtract, op1=ALU.mult), waits=[(sS, s_r2)])
                    v_x2s = sV.n
                    for grp in range(4):
                        tbk = 4 + (grp % 2)
                        for k4 in range(4):
                            ci = grp * 4 + k4
                            P.add("tensor", lambda e, ci=ci, k4=k4, tbk=tbk:
                                  e.transpose(out=pb[tbk][:, k4 * 128:(k4 + 1) * 128],
                                              in_=d_xn2s[:, ci * 128:(ci + 1) * 128],
                                              identity=v_id32),
                                  waits=([(sV, v_x2s)] + PT.write_waits(tbk))
                                  if k4 == 0 else [])
                        p_tr = sP.n
                        for k4 in range(4):
                            ci = grp * 4 + k4
                            P.add("scalar", lambda e, ci=ci, k4=k4, tbk=tbk, nt=nt:
                                  e.activation(out=x2T[:, nt * 2048 + ci * 128:nt * 2048 + (ci + 1) * 128],
                                               in_=pb[tbk][:, k4 * 128:(k4 + 1) * 128],
                                               func=AF.Identity,
                                               scale=v_g2[:, ci:ci + 1],
                                               bias=v_b2[:, ci:ci + 1]),
                                  waits=[(sP, p_tr)] if k4 == 0 else [])
                        PT.note_read(tbk, sS)
                barrier()
                ar.pop()

                ar.push()
                e_wst = [ar.f32(2048) for _ in range(2)]
                e_wf1 = [ar.bf(2048) for _ in range(2)]
                e_sqg = ar.f32(512)
                e_z = ar.f32(512)
                e_u = ar.f32(512)
                e_sg = ar.f32(512)
                e_g = [ar.bf(2048) for _ in range(2)]
                e_cw = ar.f32(2048)
                e_cb = ar.bf(2048)
                f1src = D["wff1_t"].rearrange("(c p) m -> p c m", p=128)
                f2src = D["wff2_t"].rearrange("(m p) e -> p m e", p=128)
                wf1_rel = [0, 0]
                wf1_rel_p = [0, 0]
                g_rel = [0, 0]
                cw_rel = [0]
                for mm in range(64):
                    par = mm % 2
                    lw = ld(e_wst[par].rearrange("p (c m) -> p c m", c=16),
                            f1src[:, :, mm * 128:(mm + 1) * 128],
                            waits=[(sV, wf1_rel[par])] if wf1_rel[par] else [],
                            sem=sL[12 + par])
                    P.add("vector", lambda e, par=par: e.tensor_copy(
                        out=e_wf1[par], in_=e_wst[par]),
                        waits=[(sL[12 + par], lw)] +
                        ([(sP, wf1_rel_p[par])] if wf1_rel_p[par] else []))
                    wf1_rel[par] = sV.n
                    v_w = sV.n
                    for g in range(4):
                        hbk = g % 2
                        wk0 = PT.write_waits(hbk) + [(sV, v_w)]
                        for ci in range(16):
                            P.add("tensor", lambda e, ci=ci, g=g, hbk=hbk, par=par:
                                  e.matmul(out=pb[hbk][:],
                                           lhsT=e_wf1[par][:, ci * 128:(ci + 1) * 128],
                                           rhs=x2T[:, g * 4 * 2048:(g * 4 + 4) * 2048]
                                           .rearrange("p (b f) -> p b f", b=4)
                                           [:, :, ci * 128:(ci + 1) * 128],
                                           start=(ci == 0), stop=(ci == 15)),
                                  waits=wk0 if ci == 0 else [])
                        wf1_rel_p[par] = sP.n
                        p_h = sP.n
                        P.add("scalar", lambda e, hbk=hbk: e.activation(
                            out=e_sqg, in_=pb[hbk][:], func=AF.Square),
                            waits=[(sP, p_h), (sV, sV.n)])
                        s_hsq = sS.n
                        P.add("vector", lambda e: e.tensor_scalar(
                            out=e_z, in0=e_sqg, scalar1=cn("c0447"),
                            scalar2=cn("one"), op0=ALU.mult, op1=ALU.add),
                            waits=[(sS, s_hsq)])
                        P.add("vector", lambda e, hbk=hbk: e.tensor_mul(
                            out=e_u, in0=pb[hbk][:], in1=e_z),
                            waits=[(sS, sS.n)])
                        v_u = sV.n
                        P.add("scalar", lambda e: e.activation(
                            out=e_sg, in_=e_u, func=AF.Sigmoid,
                            scale=cn("sig_gelu")), waits=[(sV, v_u)])
                        s_sg = sS.n
                        P.add("vector", lambda e, hbk=hbk, g=g, par=par:
                              e.tensor_mul(out=e_g[par][:, g * 512:(g + 1) * 512],
                                           in0=pb[hbk][:], in1=e_sg),
                              waits=[(sS, s_sg)] +
                              ([(sL[16 + par], g_rel[par])] if (g_rel[par] and g == 0) else []))
                        PT.note_read(hbk, sV)
                    g_rel[par] = st(S["g_sc"][mm], e_g[par], waits=[(sV, sV.n)],
                                    sem=sL[16 + par])
                    lc = ld(e_cw, f2src[:, mm, :],
                            waits=[(sG, cw_rel[0])] if cw_rel[0] else [], sem=sL[28])
                    P.add("gpsimd", lambda e: e.tensor_copy(out=e_cb, in_=e_cw),
                          waits=[(sL[28], lc), (sL[29], sL[29].n)])
                    cw_rel[0] = sG.n
                    st(S["wff2bf_sc"][mm], e_cb, waits=[(sG, sG.n)], sem=sL[29])
                barrier()
                ar.pop()

                ar.push()
                f_g = ar.bf(64 * 512)
                f_w2 = [ar.bf(1024) for _ in range(2)]
                fr_xat = [ar.f32(512) for _ in range(2)]
                fr_xn = [ar.f32(512) for _ in range(2)]
                fr_s = [ar.f32(512) for _ in range(2)]
                fr_x = [ar.f32(512) for _ in range(2)]
                fr_t0 = [ar.f32(512) for _ in range(2)]
                fr_t1 = [ar.f32(512) for _ in range(2)]
                fr_t2 = [ar.f32(512) for _ in range(2)]
                fr_sg2 = [ar.f32(512) for _ in range(2)]
                fr_out = [ar.f32(512) for _ in range(2)]
                f_t0, f_t1, f_t2, f_sg2 = fr_t0[0], fr_t1[0], fr_t2[0], fr_sg2[0]
                f_wup = ar.bf(2048)
                f_bias = ar.f32(2048)
                lwu = ld(f_bias, D["bias_bc"][:], sem=sL[30])
                lu = ld(f_t0, D["wup128"][:, 0:512], sem=sL[31])
                lu2 = ld(f_t1, D["wup128"][:, 512:1024], sem=sL[31],
                         waits=[(sL[31], lu)])
                lu3 = ld(f_t2, D["wup128"][:, 1024:1536], sem=sL[31],
                         waits=[(sL[31], lu2)])
                lu4 = ld(f_sg2, D["wup128"][:, 1536:2048], sem=sL[31],
                         waits=[(sL[31], lu3)])
                P.add("vector", lambda e, b=f_t0: e.tensor_copy(
                    out=f_wup[:, 0:512], in_=b), waits=[(sL[31], lu)])
                P.add("vector", lambda e, b=f_t1: e.tensor_copy(
                    out=f_wup[:, 512:1024], in_=b), waits=[(sL[31], lu2)])
                P.add("vector", lambda e, b=f_t2: e.tensor_copy(
                    out=f_wup[:, 1024:1536], in_=b), waits=[(sL[31], lu3)])
                P.add("vector", lambda e, b=f_sg2: e.tensor_copy(
                    out=f_wup[:, 1536:2048], in_=b), waits=[(sL[31], lu4)])
                v_wup_done = sV.n
                if "lr1" in dumps:
                    wup2_d = nc.dram_tensor("dump_wup_early", [128, 2048], BF16,
                                            kind="ExternalOutput").ap()
                    P.add("scalar", lambda e: e.dma_start(out=wup2_d, in_=f_wup),
                          waits=[(sV, v_wup_done)], incs=[(sW, 16)], dma=True)
                g_blk_rel = [0]
                w2_rel = [0, 0]
                ep_ctr = [0]
                ep_rel = [(0, 0), (0, 0)]
                for g in range(4):
                    glw = []
                    for mq in range(0, 64, 16):
                        glw.append(ld(
                            f_g.rearrange("p (m f) -> p m f", m=64)[:, mq:mq + 16, :],
                            S["g_sc"][mq:mq + 16, :, g * 512:(g + 1) * 512]
                            .rearrange("m p f -> p m f"),
                            waits=[(sP, g_blk_rel[0])] if (g_blk_rel[0] and mq == 0) else [],
                            sem=sL[32]))
                    lg = (sL[32], sL[32].n)
                    for ep in range(2):
                        for mm in range(64):
                            par = mm % 2
                            lw2 = ld(f_w2[par], S["wff2bf_sc"][mm][:, ep * 1024:(ep + 1) * 1024],
                                     waits=[(sP, w2_rel[par])] if w2_rel[par] else [],
                                     sem=sL[34 + par])
                            first_w = [(sL[34 + par], lw2), lg]
                            for nt4 in range(4):
                                for eoi in range(2):
                                    bk = nt4 * 2 + eoi
                                    wk0 = list(first_w) if (nt4 == 0 and eoi == 0) else []
                                    if mm == 0:
                                        wk0 += PT.write_waits(bk)
                                    P.add("tensor", lambda e, mm=mm, nt4=nt4, eoi=eoi,
                                          ep=ep, bk=bk, par=par, g=g: e.matmul(
                                              out=pb[bk][:],
                                              lhsT=f_g[:, mm * 512 + nt4 * 128:mm * 512 + (nt4 + 1) * 128],
                                              rhs=f_w2[par][:, eoi * 512:(eoi + 1) * 512],
                                              start=(mm == 0), stop=(mm == 63)),
                                          waits=wk0)
                            w2_rel[par] = sP.n
                        p_ff2 = sP.n
                        g_blk_rel[0] = p_ff2
                        for nt4 in range(4):
                            nt = g * 4 + nt4
                            for eoi in range(2):
                                eo = ep * 2 + eoi
                                bk = nt4 * 2 + eoi
                                col = slice(eo * 512, (eo + 1) * 512)
                                ip = ep_ctr[0] % 2
                                ep_ctr[0] += 1
                                f_xat = fr_xat[ip]; f_xn = fr_xn[ip]
                                f_s = fr_s[ip]; f_x = fr_x[ip]
                                f_t0 = fr_t0[ip]; f_t1 = fr_t1[ip]
                                f_t2 = fr_t2[ip]; f_sg2 = fr_sg2[ip]
                                f_out = fr_out[ip]
                                rv, rg = ep_rel[ip]
                                lda = ld(f_xat, S["xattn_sc"][nt][:, col],
                                         waits=[(sV, rv)], sem=sL[36 + ip])
                                ldn = ld(f_xn, S["xno_sc"][nt][:, col],
                                         waits=[(sV, rv)], sem=sL[38 + ip])
                                lds = ld(f_s, S["s_sc"][nt][:, col],
                                         waits=[(sG, rg)], sem=sL[52 + ip])
                                ldx = ld(f_x, D["x_own"].rearrange(
                                    "(n p) e -> n p e", p=128)[nt][:, col],
                                    waits=[(sG, rg)], sem=sL[54 + ip])
                                m0c = v_m0c[:, nt:nt + 1]
                                m1c = v_m1c[:, nt:nt + 1]
                                m2c = v_m2c[:, nt:nt + 1]
                                m01c = v_m01c[:, nt:nt + 1]
                                P.add("vector", lambda e, m01c=m01c, f_t0=f_t0,
                                      f_xn=f_xn: e.tensor_scalar(
                                    out=f_t0, in0=f_xn, scalar1=m01c, scalar2=None,
                                    op0=ALU.mult),
                                    waits=[(sL[38 + ip], ldn), (sL[30], lwu),
                                           (sV, v_wup_done), (sG, rg)])
                                P.add("vector", lambda e, m2c=m2c, f_t1=f_t1,
                                      f_xat=f_xat, f_t0=f_t0:
                                      e.scalar_tensor_tensor(
                                          out=f_t1, in0=f_xat, scalar=m2c, in1=f_t0,
                                          op0=ALU.mult, op1=ALU.add),
                                      waits=[(sL[36 + ip], lda)])
                                P.add("vector", lambda e, m2c=m2c, bk=bk, f_t2=f_t2,
                                      f_t1=f_t1: e.scalar_tensor_tensor(
                                          out=f_t2, in0=pb[bk][:], scalar=m2c,
                                          in1=f_t1, op0=ALU.mult, op1=ALU.add),
                                      waits=[(sP, p_ff2)])
                                PT.note_read(bk, sV)
                                P.add("tensor", lambda e, nt=nt, eo=eo, bk=bk:
                                      e.matmul(out=pb[bk][:],
                                               lhsT=v_lr1[:, nt * 128:(nt + 1) * 128],
                                               rhs=f_wup[:, eo * 512:(eo + 1) * 512],
                                               start=True, stop=True),
                                      waits=[(sV, sV.n)] + PT.write_waits(bk))
                                p_lr = sP.n
                                P.add("vector", lambda e, m1c=m1c, bk=bk, f_t2=f_t2:
                                      e.scalar_tensor_tensor(
                                          out=f_t2, in0=pb[bk][:], scalar=m1c,
                                          in1=f_t2, op0=ALU.mult, op1=ALU.add),
                                      waits=[(sP, p_lr)])
                                PT.note_read(bk, sV)
                                P.add("vector", lambda e, m0c=m0c, col=col,
                                      f_t0=f_t0, f_t2=f_t2:
                                      e.scalar_tensor_tensor(
                                          out=f_t0, in0=f_bias[:, col], scalar=m0c,
                                          in1=f_t2, op0=ALU.mult, op1=ALU.add),
                                      waits=[(sG, rg)])
                                g_sel = sV.n
                                P.add("scalar", lambda e, f_sg2=f_sg2, f_t0=f_t0:
                                      e.activation(
                                          out=f_sg2, in_=f_t0, func=AF.Sigmoid),
                                      waits=[(sV, g_sel), (sG, rg)])
                                s_sig = sS.n
                                P.add("gpsimd", lambda e, f_t1=f_t1, f_t0=f_t0,
                                      f_sg2=f_sg2: e.tensor_mul(
                                          out=f_t1, in0=f_t0, in1=f_sg2),
                                      waits=[(sS, s_sig)])
                                P.add("gpsimd", lambda e, f_t1=f_t1, f_s=f_s:
                                      e.tensor_mul(
                                          out=f_t1, in0=f_t1, in1=f_s),
                                      waits=[(sL[52 + ip], lds)])
                                P.add("gpsimd", lambda e, f_out=f_out, f_t1=f_t1,
                                      f_x=f_x: e.tensor_add(
                                          out=f_out, in0=f_t1, in1=f_x),
                                      waits=[(sL[54 + ip], ldx),
                                             (sL[33 + ip * 18], sL[33 + ip * 18].n)])
                                v_o = sG.n
                                st(out_d[nt][:, col], f_out, waits=[(sG, v_o)],
                                   sem=sL[33 + ip * 18])
                                ep_rel[ip] = (sV.n, sG.n)
                barrier()
                ar.pop()
                ar.pop()

        if "lr1" in dumps:
            lr1_d = nc.dram_tensor("dump_lr1", [128, 2048], BF16,
                                   kind="ExternalOutput").ap()
            P.add("scalar", lambda e: e.dma_start(out=lr1_d, in_=v_lr1),
                  waits=[(s, s.n) for s in (sP, sS, sV, sG)], incs=[(sW, 16)],
                  dma=True)
            wup_d = nc.dram_tensor("dump_wup", [128, 2048], BF16,
                                   kind="ExternalOutput").ap()
            P.add("scalar", lambda e: e.dma_start(out=wup_d, in_=f_wup),
                  incs=[(sW, 16)], dma=True)
            msk_d = nc.dram_tensor("dump_masks", [128, 64], F32,
                                   kind="ExternalOutput").ap()
            P.add("scalar", lambda e: e.dma_start(out=msk_d[:, 0:16], in_=v_m0c),
                  incs=[(sW, 16)], dma=True)
            P.add("scalar", lambda e: e.dma_start(out=msk_d[:, 16:32], in_=v_m1c),
                  incs=[(sW, 16)], dma=True)
            P.add("scalar", lambda e: e.dma_start(out=msk_d[:, 32:48], in_=v_m2c),
                  incs=[(sW, 16)], dma=True)
            P.add("scalar", lambda e: e.dma_start(out=msk_d[:, 48:64], in_=v_m01c),
                  incs=[(sW, 16)], dma=True)
        for nm in dumps:
            if nm == "lr1":
                continue
            P.add("sync", lambda e, d=dump_aps[nm], s=S[nm]: e.dma_start(
                out=d[:], in_=s[:]), incs=[(sD, 16)], dma=True)

        P.add("sync", None, waits=[(s, s.n) for s in ALLSEMS])

        block = es.enter_context(nc.Block())

        @block.sync
        def _(sync):
            P.emit("sync", sync)

        @block.tensor
        def _(tensor):
            P.emit("tensor", tensor)

        @block.scalar
        def _(scalar):
            P.emit("scalar", scalar)

        @block.vector
        def _(vector):
            P.emit("vector", vector)

        @block.gpsimd
        def _(gpsimd):
            P.emit("gpsimd", gpsimd)

    return nc


def kernel(**inputs):
    in_maps = host_prep(inputs)
    nc = build_nc()
    res = run_bass_kernel_spmd(nc, in_maps, core_ids=list(range(8)))
    out = np.empty((B, N, E), np.float32)
    for core in range(8):
        b = core // 2
        slots = SLOTS_A if core % 2 == 0 else SLOTS_B
        o = np.asarray(res.results[core]["out"]).reshape(NO, E)
        for s, ch in enumerate(slots):
            out[b, 512 * ch:512 * ch + 512] = o[512 * s:512 * (s + 1)]
    return out


# version 12
# speedup vs baseline: 1.1046x; 1.0029x over previous
import numpy as np
import ml_dtypes
import concourse.bass as bass
import concourse.mybir as mybir
from concourse.bass_utils import run_bass_kernel_spmd
from contextlib import ExitStack

F32 = mybir.dt.float32
BF16 = mybir.dt.bfloat16
AF = mybir.ActivationFunctionType
ALU = mybir.AluOpType
AX = mybir.AxisListType

B, N, E, H, Dh = 4, 4096, 2048, 16, 128
NT = N
NO = 2048
NTIL = NT // 128
NOTIL = NO // 128
NS = 4
QS = 512
W_SLOT = [4096, 3072, 2048, 1024]
FF = 8192
EPS = 1e-5
SLOTS_A = [7, 4, 3, 0]
SLOTS_B = [6, 5, 2, 1]

CN = {"eps": 0, "neghalf": 1, "negone": 2, "inv128": 3, "inv2048": 4,
      "c0447": 5, "one": 6, "sig_gelu": 7, "isq128": 8, "half": 9}
CN_VALS = {"eps": EPS, "neghalf": -0.5, "negone": -1.0, "inv128": 1.0 / 128,
           "inv2048": 1.0 / 2048, "c0447": 0.044715, "one": 1.0,
           "sig_gelu": 1.5957691216057308, "isq128": 1.0 / np.sqrt(128.0),
           "half": 0.5}


class Sem:
    def __init__(self, h):
        self.h = h
        self.n = 0


class Plan:
    ENGINES = ("sync", "tensor", "scalar", "vector", "gpsimd")

    def __init__(self):
        self.ev = {e: [] for e in self.ENGINES}
        self.esem = {}

    def add(self, engine, fn, waits=(), incs=(), dma=False, nochain=False):
        waits = list(waits)
        incs = list(incs)
        es = self.esem.get(engine)
        if es is not None and not dma and fn is not None:
            if engine != "tensor" and not nochain and es.n > 0:
                waits.append((es, es.n))
            incs.append((es, 1))
        for s, k in incs:
            s.n += k
        self.ev[engine].append((fn, [(s, v) for s, v in waits if v > 0],
                                [(s, k) for s, k in incs]))

    def emit(self, eng_name, eng):
        for fn, waits, incs in self.ev[eng_name]:
            for s, v in waits:
                eng.wait_ge(s.h, v)
            inst = fn(eng) if fn is not None else None
            if incs:
                if inst is None:
                    inst = eng.nop()
                for s, k in incs:
                    inst.then_inc(s.h, k)


class PsumTracker:

    def __init__(self, nbanks=8):
        self.rel = [[] for _ in range(nbanks)]

    def note_read(self, bank, sem):
        self.rel[bank].append((sem, sem.n))

    def write_waits(self, bank):
        w = self.rel[bank]
        self.rel[bank] = []
        return w


class Arena:
    def __init__(self, t, cols):
        self.t = t
        self.cols = cols
        self.off = 0
        self.marks = []

    def alloc(self, n_bf16):
        n = (n_bf16 + 15) & ~15
        assert self.off + n <= self.cols, f"arena overflow {self.off}+{n}>{self.cols}"
        o = self.off
        self.off += n
        return o

    def bf(self, n):
        o = self.alloc(n)
        return self.t[:, o:o + n]

    def f32(self, n):
        o = self.alloc(2 * n)
        return self.t[:, o:o + 2 * n].bitcast(F32)

    def push(self):
        self.marks.append(self.off)

    def pop(self):
        self.off = self.marks.pop()


def _mask_patterns():
    r = np.arange(1024)[:, None]
    q = np.arange(512)[None, :]
    m0 = (r <= q).astype(ml_dtypes.bfloat16)
    m512 = ((r - 512) <= q).astype(ml_dtypes.bfloat16)
    return np.ascontiguousarray(np.stack([m0, m512]).reshape(2, 8, 128, 512))


def _rope_tables():
    inv = 1.0 / 10000.0 ** (np.arange(0, Dh, 2, dtype=np.float32) / Dh)
    t = np.arange(NT, dtype=np.float32)[:, None] * inv[None, :]
    def shp(a):
        return np.ascontiguousarray(
            a.reshape(NTIL, 128, 64).transpose(1, 0, 2).reshape(128, NTIL * 64))
    return shp(np.cos(t).astype(np.float32)), shp(np.sin(t).astype(np.float32))


def _col128(v):
    return np.ascontiguousarray(np.asarray(v, np.float32).reshape(16, 128).T)


def host_prep(inputs):
    x = np.asarray(inputs["L0_attn_x"], np.float32)
    wrt = np.asarray(inputs["L0_ffn_w_router"], np.float32)
    wdn = np.asarray(inputs["L0_ffn_w_down"], np.float32)
    wup = np.asarray(inputs["L0_ffn_w_up"], np.float32)
    sbias = np.asarray(inputs["L0_ffn_simple_bias"], np.float32)

    wrt128 = np.zeros((E, 128), np.float32); wrt128[:, 0:3] = wrt.T
    wrt128 = np.ascontiguousarray(
        wrt128.reshape(16, 128, 128).transpose(1, 0, 2).reshape(128, 2048))
    wdn128 = np.zeros((E, 128), np.float32); wdn128[:, 0:32] = wdn.T
    wdn128 = np.ascontiguousarray(
        wdn128.reshape(16, 128, 128).transpose(1, 0, 2).reshape(128, 2048))
    wup128 = np.zeros((128, E), np.float32); wup128[0:32, :] = wup.T

    cos_t, sin_t = _rope_tables()
    pat = _mask_patterns()
    consts = np.zeros((128, 16), np.float32)
    for k, j in CN.items():
        consts[:, j] = CN_VALS[k]

    common = {
        "wqkv": np.ascontiguousarray(inputs["L0_attn_w_qkv"], dtype=np.float32),
        "wout": np.ascontiguousarray(inputs["L0_attn_w_out"], dtype=np.float32),
        "ln_s": np.asarray(inputs["L0_attn_ln_scale"], np.float32).reshape(128, 1),
        "ln_b": np.asarray(inputs["L0_attn_ln_bias"], np.float32).reshape(128, 1),
        "wq_t": np.ascontiguousarray(np.asarray(inputs["L0_ffn_wq"], np.float32).T),
        "wk_t": np.ascontiguousarray(np.asarray(inputs["L0_ffn_wk"], np.float32).T),
        "wv_t": np.ascontiguousarray(np.asarray(inputs["L0_ffn_wv"], np.float32).T),
        "wo_t": np.ascontiguousarray(np.asarray(inputs["L0_ffn_wo"], np.float32).T),
        "wff1_t": np.ascontiguousarray(np.asarray(inputs["L0_ffn_w_ff1"], np.float32).T),
        "wff2_t": np.ascontiguousarray(np.asarray(inputs["L0_ffn_w_ff2"], np.float32).T),
        "wrt128": wrt128, "wdn128": wdn128, "wup128": wup128,
        "g1": _col128(inputs["L0_ffn_ln1_scale"]), "b1": _col128(inputs["L0_ffn_ln1_bias"]),
        "g2": _col128(inputs["L0_ffn_ln2_scale"]), "b2": _col128(inputs["L0_ffn_ln2_bias"]),
        "cos_t": cos_t, "sin_t": sin_t,
        "id32": np.eye(128, dtype=np.float32),
        "idbf": np.eye(128, dtype=ml_dtypes.bfloat16),
        "ones_col": np.ones((128, 1), np.float32),
        "ones_row": np.ones((1, 128), np.float32),
        "bias_bc": np.ascontiguousarray(np.tile(sbias[None, :], (128, 1))),
        "consts": consts,
    }

    in_maps = []
    for core in range(8):
        b = core // 2
        slots = SLOTS_A if core % 2 == 0 else SLOTS_B
        masks = np.empty((NS, 8, 128, 512), ml_dtypes.bfloat16)
        for s, ch in enumerate(slots):
            shift = 512 * (ch + 1) + 512 - W_SLOT[s]
            masks[s] = pat[0 if shift == 0 else 1]
        m = dict(common)
        m["x_full"] = np.ascontiguousarray(x[b])
        inv = 1.0 / 10000.0 ** (np.arange(0, Dh, 2, dtype=np.float32) / Dh)
        pos = np.concatenate([512 * c + np.arange(512) for c in slots]).astype(np.float32)
        tt = pos[:, None] * inv[None, :]
        m["cos_o"] = np.ascontiguousarray(
            np.cos(tt).astype(np.float32).reshape(NOTIL, 128, 64).transpose(1, 0, 2).reshape(128, NOTIL * 64))
        m["sin_o"] = np.ascontiguousarray(
            np.sin(tt).astype(np.float32).reshape(NOTIL, 128, 64).transpose(1, 0, 2).reshape(128, NOTIL * 64))
        m["x_own"] = np.ascontiguousarray(
            np.concatenate([x[b, 512 * c:512 * c + 512] for c in slots], 0))
        m["masks"] = np.ascontiguousarray(masks)
        in_maps.append(m)
    return in_maps


def build_nc(upto=99, dumps=(), p0n=(NTIL, NOTIL), p0parts=3):
    nc = bass.Bass("TRN2", target_bir_lowering=False, debug=False)

    D = {}
    def din(name, shape, dt):
        D[name] = nc.dram_tensor(name, list(shape), dt, kind="ExternalInput").ap()
    din("x_full", (NT, E), F32); din("x_own", (NO, E), F32)
    din("wqkv", (E, 3 * E), F32); din("wout", (E, E), F32)
    din("ln_s", (128, 1), F32); din("ln_b", (128, 1), F32)
    din("wq_t", (E, E), F32); din("wk_t", (E, E), F32)
    din("wv_t", (E, E), F32); din("wo_t", (E, E), F32)
    din("wff1_t", (E, FF), F32); din("wff2_t", (FF, E), F32)
    din("wrt128", (128, 2048), F32)
    din("wdn128", (128, 2048), F32); din("wup128", (128, E), F32)
    din("g1", (128, 16), F32); din("b1", (128, 16), F32)
    din("g2", (128, 16), F32); din("b2", (128, 16), F32)
    din("cos_t", (128, NTIL * 64), F32); din("sin_t", (128, NTIL * 64), F32)
    din("cos_o", (128, NOTIL * 64), F32); din("sin_o", (128, NOTIL * 64), F32)
    din("id32", (128, 128), F32); din("idbf", (128, 128), BF16)
    din("ones_col", (128, 1), F32); din("ones_row", (1, 128), F32)
    din("bias_bc", (128, E), F32); din("consts", (128, 16), F32)
    din("masks", (NS, 8, 128, 512), BF16)

    out_d = nc.dram_tensor("out", [NOTIL, 128, E], F32, kind="ExternalOutput").ap()

    S = {}
    def dsc(name, shape, dt):
        S[name] = nc.dram_tensor(name, list(shape), dt, kind="Internal").ap()
    dsc("xnT_sc", (NTIL, 128, E), BF16)
    dsc("xn1T_sc", (NTIL, 128, E), BF16)
    dsc("xnTo_sc", (NOTIL, 128, E), BF16)
    dsc("xn1To_sc", (NOTIL, 128, E), BF16)
    dsc("xno_sc", (NOTIL, 128, E), F32)
    dsc("ctxp_sc", (H, 128, NO), BF16)
    dsc("ctxs_sc", (H, 128, NO), BF16)
    dsc("s_sc", (NOTIL, 128, E), F32)
    dsc("xattn_sc", (NOTIL, 128, E), F32)
    dsc("g_sc", (FF // 128, 128, NO), BF16)
    dsc("wff2bf_sc", (FF // 128, 128, E), BF16)

    dump_aps = {nm: nc.dram_tensor("dump_" + nm, list(S[nm].shape), S[nm].dtype,
                                   kind="ExternalOutput").ap() for nm in dumps
                if nm != "lr1"}

    with ExitStack() as es:
        ar_t = es.enter_context(nc.sbuf_tensor("arena", [128, 100096], BF16))
        ar = Arena(ar_t, 100096)
        pw = [es.enter_context(nc.psum_tensor(f"pw{i}", [128, 1024], F32))
              for i in range(3)]
        pb = [pw[0][:, 0:512], pw[0][:, 512:1024],
              pw[1][:, 0:512], pw[1][:, 512:1024]] + \
             [es.enter_context(nc.psum_tensor(f"pb{i}", [128, 512], F32))[:]
              for i in range(4, 6)] + \
             [pw[2][:, 0:512], pw[2][:, 512:1024]]
        sD = Sem(es.enter_context(nc.semaphore("sD")))
        sW = Sem(es.enter_context(nc.semaphore("sW")))
        sP = Sem(es.enter_context(nc.semaphore("sP")))
        sS = Sem(es.enter_context(nc.semaphore("sS")))
        sV = Sem(es.enter_context(nc.semaphore("sV")))
        sG = Sem(es.enter_context(nc.semaphore("sG")))
        sL = [Sem(es.enter_context(nc.semaphore(f"sL{i}"))) for i in range(60)]
        ALLSEMS = tuple([sD, sW, sP, sS, sV, sG] + sL)
        P = Plan()
        P.esem = {"tensor": sP, "scalar": sS, "vector": sV, "gpsimd": sG}
        PT = PsumTracker()

        def snap():
            return [(s, s.n) for s in ALLSEMS]

        def gate(snp, engines=Plan.ENGINES):
            for en in engines:
                P.add(en, None, waits=snp)

        def barrier():
            sB_t = [(s, s.n) for s in ALLSEMS]
            gate(sB_t)

        def ld(dst_ap, src_ap, waits=(), sem=None):
            sem = sem if sem is not None else sD
            P.add("sync", lambda e, d=dst_ap, s=src_ap: e.dma_start(out=d, in_=s),
                  waits=[(sem, sem.n)] + list(waits), incs=[(sem, 16)], dma=True)
            return sem.n

        def st(dst_ap, src_ap, waits=(), sem=None):
            sem = sem if sem is not None else sW
            P.add("scalar", lambda e, d=dst_ap, s=src_ap: e.dma_start(out=d, in_=s),
                  waits=[(sem, sem.n), (sS, sS.n)] + list(waits),
                  incs=[(sem, 16)], dma=True)
            return sem.n

        v_id32 = ar.f32(128)
        v_idbf = ar.bf(128)
        v_consts = ar.f32(16)
        v_lns = ar.f32(1); v_lnb = ar.f32(1)
        v_g1 = ar.f32(16); v_b1 = ar.f32(16)
        v_g2 = ar.f32(16); v_b2 = ar.f32(16)
        v_ones_col = ar.f32(1)
        v_ones_row = ar.f32(128)
        v_lr1 = ar.bf(2048)
        v_m2c = ar.f32(16); v_m1c = ar.f32(16); v_m0c = ar.f32(16); v_m01c = ar.f32(16)

        def cn(name):
            return v_consts[:, CN[name]:CN[name] + 1]

        for dst, srcn in [(v_id32, "id32"), (v_idbf, "idbf"), (v_consts, "consts"),
                          (v_lns, "ln_s"), (v_lnb, "ln_b"), (v_g1, "g1"),
                          (v_b1, "b1"), (v_g2, "g2"), (v_b2, "b2"),
                          (v_ones_col, "ones_col")]:
            ld(dst, D[srcn][:])
        ld(v_ones_row[0:1, 0:128], D["ones_row"][:])
        w_persist = snap()
        gate(w_persist)

        ar.push()
        v_wrt = ar.f32(2048)
        v_wdn = ar.bf(2048)
        ld(v_wrt, D["wrt128"][:])
        ar.push()
        stg0 = ar.f32(2048)
        t1 = ld(stg0, D["wdn128"][:])
        P.add("vector", lambda e: e.tensor_copy(out=v_wdn, in_=stg0),
              waits=[(sD, t1)])
        ar.pop()
        gate(snap())
        xr = [ar.f32(2048) for _ in range(6)]
        junk = ar.f32(2048)
        xnr = [ar.f32(2048) for _ in range(3)]
        xn1r = [ar.f32(2048) for _ in range(3)]
        bstg = [ar.bf(2048) for _ in range(3)]
        b1stg = [ar.bf(2048) for _ in range(3)]
        x32stg = ar.f32(2048)
        rtmp = ar.f32(128)
        rl3 = ar.f32(16)
        sm = [ar.f32(32) for _ in range(2)]
        SC = {"ssum": 0, "ssq": 4, "m": 8, "e2": 12, "v": 16, "r": 20,
              "rr": 24, "s1": 28}

        xr_rel = [0] * 6
        bst_rel = [0, 0, 0]
        b1st_rel = [0, 0, 0]
        xno_rel = [0, 0, 0]
        xnr_tr = [0, 0, 0]
        xn1_tr = [0, 0, 0]

        def p0_stats_batch(b4, own, n_in_batch=4):
            Z = sm[b4 % 2]
            def c(nm, i=0, w=4):
                return Z[:, SC[nm] + i:SC[nm] + i + w]
            dvs = []
            for i in range(n_in_batch):
                nt = 4 * b4 + i
                par = nt % 6
                src = (D["x_own"] if own else D["x_full"]).rearrange(
                    "(n p) e -> n p e", p=128)[nt]
                xsem = sL[40 + par]
                dv = ld(xr[par], src, sem=xsem,
                        waits=[(sV, xr_rel[par])] if xr_rel[par] else [])
                dvs.append((xsem, dv))
                P.add("scalar", lambda e, par=par, i=i: e.activation(
                    out=junk, in_=xr[par], func=AF.Square,
                    accum_out=c("ssq", i, 1)), waits=[(xsem, dv)])
                P.add("vector", lambda e, par=par, i=i: e.reduce_sum(
                    out=c("ssum", i, 1), in_=xr[par], axis=AX.X),
                    waits=[(xsem, dv)])
            s_sq = sS.n
            P.add("vector", lambda e: e.tensor_scalar(out=c("m"), in0=c("ssum"),
                  scalar1=cn("inv2048"), scalar2=None, op0=ALU.mult))
            P.add("vector", lambda e: e.tensor_scalar(out=c("e2"), in0=c("ssq"),
                  scalar1=cn("inv2048"), scalar2=None, op0=ALU.mult),
                  waits=[(sS, s_sq)])
            P.add("vector", lambda e: e.tensor_mul(out=c("v"), in0=c("m"), in1=c("m")))
            P.add("vector", lambda e: e.tensor_sub(out=c("v"), in0=c("e2"), in1=c("v")))
            v_v = sV.n
            P.add("scalar", lambda e: e.activation(out=c("rr"), in_=c("v"), func=AF.Ln,
                                                   bias=cn("eps")),
                  waits=[(sV, v_v)])
            P.add("scalar", lambda e: e.activation(out=c("r"), in_=c("rr"), func=AF.Exp,
                                                   scale=cn("neghalf")))
            s_r = sS.n
            P.add("vector", lambda e: e.tensor_mul(out=c("rr"), in0=c("r"), in1=c("r")),
                  waits=[(sS, s_r)])
            P.add("vector", lambda e: e.tensor_mul(out=c("s1"), in0=c("rr"), in1=c("v")))
            v_s1 = sV.n
            P.add("scalar", lambda e: e.activation(out=c("rr"), in_=c("s1"), func=AF.Ln,
                                                   bias=cn("eps")),
                  waits=[(sV, v_s1)])
            P.add("scalar", lambda e: e.activation(out=c("s1"), in_=c("rr"),
                                                   func=AF.Exp, scale=cn("neghalf")))
            return s_r, sS.n

        def p0_tile(nt, own, s_r, s_rstd1):
            par = nt % 3
            xt = xr[nt % 6]
            Z = sm[(nt // 4) % 2]
            i = nt % 4
            def c(nm):
                return Z[:, SC[nm] + i:SC[nm] + i + 1]
            P.add("vector", lambda e: e.tensor_scalar(out=xnr[par], in0=xt,
                  scalar1=c("m"), scalar2=c("r"), op0=ALU.subtract, op1=ALU.mult),
                  waits=[(sS, s_r), (sP, xnr_tr[par])] +
                  ([(sL[49 + par], xno_rel[par])] if xno_rel[par] else []))
            v_xn = sV.n
            xr_rel[nt % 6] = v_xn
            P.add("vector", lambda e: e.tensor_scalar(out=xn1r[par], in0=xnr[par],
                  scalar1=c("s1"), scalar2=None, op0=ALU.mult),
                  waits=[(sS, s_rstd1), (sP, xn1_tr[par])])
            v_xn1 = sV.n
            bXi = par
            bYi = 3 + par
            ev_last = {}
            for half, (bi, srcbuf, v_src) in enumerate(
                    [(bXi, xnr[par], v_xn), (bYi, xn1r[par], v_xn1)]):
                bank = pb[bi]
                for grp in range(4):
                    for k4 in range(4):
                        ci = grp * 4 + k4
                        wt = []
                        if k4 == 0:
                            wt = [(sV, v_src)] + PT.write_waits(bi)
                        P.add("tensor", lambda e, ci=ci, k4=k4, bk=bank, sb=srcbuf:
                              e.transpose(out=bk[:, k4 * 128:(k4 + 1) * 128],
                                          in_=sb[:, ci * 128:(ci + 1) * 128],
                                          identity=v_id32),
                              waits=wt)
                    p_tr = sP.n
                    if half == 0:
                        xnr_tr[par] = sP.n
                        P.add("vector", lambda e, bk=bank, g=grp:
                              e.tensor_copy(out=bstg[par][:, g * 512:(g + 1) * 512],
                                            in_=bk[:]),
                              waits=[(sP, p_tr)] +
                              ([(sL[43 + par], bst_rel[par])]
                               if (grp == 0 and bst_rel[par]) else []))
                        PT.note_read(bi, sV)
                        if own and not (p0parts & 4):
                            P.add("vector", lambda e, bk=bank, g=grp:
                                  e.tensor_copy(out=x32stg[:, g * 512:(g + 1) * 512],
                                                in_=bk[:]),
                                  waits=[(sP, p_tr)])
                            PT.note_read(bi, sV)
                    else:
                        xn1_tr[par] = sP.n
                        for k4 in range(4):
                            ci = grp * 4 + k4
                            P.add("scalar", lambda e, bk=bank, ci=ci, k4=k4:
                                  e.activation(out=b1stg[par][:, ci * 128:(ci + 1) * 128],
                                               in_=bk[:, k4 * 128:(k4 + 1) * 128],
                                               func=AF.Identity,
                                               scale=v_g1[:, ci:ci + 1],
                                               bias=v_b1[:, ci:ci + 1]),
                                  waits=([(sP, p_tr)] +
                                         ([(sL[46 + par], b1st_rel[par])]
                                          if (grp == 0 and b1st_rel[par]) else []))
                                  if k4 == 0 else [])
                        PT.note_read(bi, sS)
                ev_last[0] = (sV, sV.n) if half == 0 else ev_last.get(0, (sV, sV.n))
                ev_last[1] = (sS, sS.n)
            if own:
                bst_rel[par] = st(S["xnTo_sc"][nt], bstg[par], waits=[ev_last[0]],
                                  sem=sL[43 + par])
                b1st_rel[par] = st(S["xn1To_sc"][nt], b1stg[par], waits=[ev_last[1]],
                                   sem=sL[46 + par])
                if not (p0parts & 8):
                    xno_rel[par] = st(S["xno_sc"][nt], xnr[par], waits=[(sV, v_xn)],
                                      sem=sL[49 + par])
            else:
                bst_rel[par] = st(S["xnT_sc"][nt], bstg[par], waits=[ev_last[0]],
                                  sem=sL[43 + par])
                b1st_rel[par] = st(S["xn1T_sc"][nt], b1stg[par], waits=[ev_last[1]],
                                   sem=sL[46 + par])
            return v_xn

        def p0_router_lr(nt, par):
            v_now = sV.n
            if not (p0parts & 1):
                if p0parts & 2:
                    _lowrank_only(nt, par)
                return
            for ci in range(16):
                P.add("tensor", lambda e, ci=ci: e.matmul(
                    out=pb[6][:, 0:128], lhsT=v_wrt[:, ci * 128:(ci + 1) * 128],
                    rhs=x32stg[:, ci * 128:(ci + 1) * 128],
                    start=(ci == 0), stop=(ci == 15)),
                    waits=([(sV, v_now)] + PT.write_waits(6)) if ci == 0 else [])
            p_rl = sP.n
            P.add("vector", lambda e: e.tensor_copy(out=rtmp, in_=pb[6][:, 0:128]),
                  waits=[(sP, p_rl)])
            PT.note_read(6, sV)
            v_rt = sV.n
            P.add("tensor", lambda e: e.transpose(out=pb[7][:, 0:128], in_=rtmp,
                                                  identity=v_id32),
                  waits=[(sV, v_rt)] + PT.write_waits(7))
            p_rt = sP.n
            P.add("vector", lambda e: e.tensor_copy(out=rl3[:, 0:3],
                                                    in_=pb[7][:, 0:3]),
                  waits=[(sP, p_rt)])
            PT.note_read(7, sV)
            def l(j):
                return rl3[:, j:j + 1]
            P.add("vector", lambda e: e.tensor_tensor(out=l(4), in0=l(2), in1=l(0), op=ALU.is_gt))
            P.add("vector", lambda e: e.tensor_tensor(out=l(5), in0=l(2), in1=l(1), op=ALU.is_gt))
            P.add("vector", lambda e, nt=nt: e.tensor_mul(out=v_m2c[:, nt:nt + 1], in0=l(4), in1=l(5)))
            P.add("vector", lambda e: e.tensor_tensor(out=l(6), in0=l(1), in1=l(0), op=ALU.is_gt))
            P.add("vector", lambda e: e.tensor_tensor(out=l(7), in0=l(1), in1=l(2), op=ALU.is_ge))
            P.add("vector", lambda e: e.tensor_mul(out=l(8), in0=l(6), in1=l(7)))
            P.add("vector", lambda e, nt=nt: e.tensor_scalar(
                out=v_m01c[:, nt:nt + 1], in0=v_m2c[:, nt:nt + 1],
                scalar1=cn("negone"), scalar2=cn("one"), op0=ALU.mult, op1=ALU.add))
            P.add("vector", lambda e, nt=nt: e.tensor_mul(
                out=v_m1c[:, nt:nt + 1], in0=l(8), in1=v_m01c[:, nt:nt + 1]))
            P.add("vector", lambda e, nt=nt: e.tensor_sub(
                out=v_m0c[:, nt:nt + 1], in0=v_m01c[:, nt:nt + 1],
                in1=v_m1c[:, nt:nt + 1]))
            if p0parts & 2:
                _lowrank_only(nt, par)

        def _lowrank_only(nt, par):
            s_bst = sV.n
            for ci in range(16):
                P.add("tensor", lambda e, ci=ci, par=par: e.matmul(
                    out=pb[6][:, 0:128], lhsT=v_wdn[:, ci * 128:(ci + 1) * 128],
                    rhs=bstg[par][:, ci * 128:(ci + 1) * 128],
                    start=(ci == 0), stop=(ci == 15)),
                    waits=([(sV, s_bst)] + PT.write_waits(6)) if ci == 0 else [])
            p_lr = sP.n
            P.add("vector", lambda e, nt=nt: e.tensor_copy(
                out=v_lr1[:, nt * 128:(nt + 1) * 128], in_=pb[6][:, 0:128]),
                waits=[(sP, p_lr)])
            PT.note_read(6, sV)

        snaps = {}
        for b4 in range(p0n[0] // 4):
            if b4 >= 2:
                gate(snaps[b4 - 2])
            s_r, s_rstd1 = p0_stats_batch(b4, own=False)
            for i in range(4):
                p0_tile(4 * b4 + i, False, s_r, s_rstd1)
            snaps[b4] = snap()
        barrier()
        snaps = {}
        for b4 in range(p0n[1] // 4):
            if b4 >= 2:
                gate(snaps[b4 - 2])
            s_r, s_rstd1 = p0_stats_batch(b4, own=True)
            for i in range(4):
                nt = 4 * b4 + i
                p0_tile(nt, True, s_r, s_rstd1)
                p0_router_lr(nt, nt % 3)
            snaps[b4] = snap()
        barrier()
        ar.pop()

        if upto >= 1:
            ar.push()
            a_mask = ar.bf(8 * 512)
            a_sq = [ar.bf(1024) for _ in range(2)]
            a_at = [ar.bf(1024) for _ in range(3)]
            a_acc = ar.f32(1024)
            a_rbc = ar.f32(512)
            a_ctxe = [ar.bf(512) for _ in range(2)]
            a_dr = ar.f32(1024)
            a_sm2 = ar.f32(32)
            SC2 = {"ssq": 0, "sum": 4, "mean": 8, "e2": 12, "var": 16, "rstd": 20}

            def c2(nm, h=0, w=2):
                return a_sm2[:, SC2[nm] + h:SC2[nm] + h + w]

            p_khat = ar.bf(2 * NT)
            p_vtok = ar.bf(NTIL * 256)
            p_qhat = ar.bf(2 * NO)
            s_khat = ar.bf(2 * NT)
            s_vtok = ar.bf(NTIL * 256)
            s_qhat = ar.bf(2 * NO)
            a_wst = ar.f32(1024)
            w_kv = ar.bf(16 * 512)
            w_q = ar.bf(16 * 256)
            a_xnb = ar.bf(8 * 2048)
            a_xnt = [a_xnb[:, i * 2048:(i + 1) * 2048] for i in range(8)]
            a_cos = ar.f32(NTIL * 64); a_sin = ar.f32(NTIL * 64)
            a_rope = ar.f32(256)
            a_kstg = [ar.bf(256) for _ in range(2)]
            a_ksq = a_sq[0].bitcast(F32)
            a_tmpA = a_sq[1].bitcast(F32)[:, 0:128]
            a_tmpB = a_sq[1].bitcast(F32)[:, 128:256]
            kstg_rel = [0, 0]
            xnt_rel = [0] * 8
            acc_rel = [0]
            ctxe_rel = [0, 0]

            def gen_load_w2(dst_bf, wname, col0, width, sub0, prev_users,
                            sems):
                wsrc = D[wname].rearrange("(c p) m -> p c m", p=128)
                st_half = [a_wst[:, 0:2 * width], a_wst[:, 2 * width:4 * width]]
                lw = {}
                vcp = {}

                def copy(i):
                    P.add("vector", lambda e, cc=2 * i, hb=i % 2: e.tensor_copy(
                        out=dst_bf.rearrange("p (c m) -> p c m", c=16)
                        [:, cc:cc + 2, sub0:sub0 + width],
                        in_=st_half[hb].rearrange("p (c m) -> p c m", c=2)),
                        waits=[(sems[i % 2], lw[i])])
                    vcp[i] = sV.n

                for i in range(8):
                    if i >= 2:
                        copy(i - 2)
                    war = [(sV, vcp[i - 2])] if i >= 2 else [(sV, sV.n)]
                    lw[i] = ld(st_half[i % 2].rearrange("p (c m) -> p c m", c=2),
                               wsrc[:, 2 * i:2 * i + 2, col0:col0 + width],
                               waits=war +
                               (list(prev_users) if i < 2 else []),
                               sem=sems[i % 2])
                    yield
                copy(6)
                yield
                copy(7)
                yield

            def rope_ln2(bank_src, col0, bank_tr, cosv, sinv, dst, dst_col,
                         p_src, par, evac_extra=()):
                pbk = pb[bank_src][:, col0:col0 + 256]
                x1 = pbk.rearrange("p (h z f) -> p h z f", h=2, z=2)[:, :, 0, :]
                x2 = pbk.rearrange("p (h z f) -> p h z f", h=2, z=2)[:, :, 1, :]
                cosb = cosv.rearrange("p (o f) -> p o f", o=1).broadcast_to([128, 2, 64])
                sinb = sinv.rearrange("p (o f) -> p o f", o=1).broadcast_to([128, 2, 64])
                rp1 = a_rope.rearrange("p (h z f) -> p h z f", h=2, z=2)[:, :, 0, :]
                rp2 = a_rope.rearrange("p (h z f) -> p h z f", h=2, z=2)[:, :, 1, :]
                tA = a_tmpA.rearrange("p (h f) -> p h f", h=2)
                tB = a_tmpB.rearrange("p (h f) -> p h f", h=2)
                P.add("vector", lambda e: e.tensor_tensor(out=tA, in0=x1, in1=cosb,
                                                          op=ALU.mult),
                      waits=[(sP, p_src)])
                P.add("vector", lambda e: e.tensor_tensor(out=tB, in0=x2, in1=sinb,
                                                          op=ALU.mult))
                P.add("vector", lambda e: e.tensor_tensor(out=rp1, in0=tA, in1=tB,
                                                          op=ALU.subtract))
                P.add("vector", lambda e: e.tensor_tensor(out=tA, in0=x1, in1=sinb,
                                                          op=ALU.mult))
                P.add("vector", lambda e: e.tensor_tensor(out=tB, in0=x2, in1=cosb,
                                                          op=ALU.mult))
                v_rope_rd = sV.n
                P.add("vector", lambda e: e.tensor_tensor(out=rp2, in0=tA, in1=tB,
                                                          op=ALU.add))
                PT.note_read(bank_src, sV)
                for h in range(2):
                    P.add("scalar", lambda e, h=h: e.activation(
                        out=a_ksq[:, h * 128:(h + 1) * 128],
                        in_=pbk[:, h * 128:(h + 1) * 128], func=AF.Square,
                        accum_out=c2("ssq", h, 1)),
                        waits=[(sV, v_rope_rd)] if h == 0 else [])
                PT.note_read(bank_src, sS)
                P.add("vector", lambda e: e.reduce_sum(
                    out=c2("sum"), in_=a_rope.rearrange("p (h f) -> p h f", h=2),
                    axis=AX.X))
                P.add("vector", lambda e: e.tensor_scalar(
                    out=c2("mean"), in0=c2("sum"), scalar1=cn("inv128"),
                    scalar2=None, op0=ALU.mult))
                s_ssq = sS.n
                P.add("vector", lambda e: e.tensor_scalar(
                    out=c2("e2"), in0=c2("ssq"), scalar1=cn("inv128"),
                    scalar2=None, op0=ALU.mult), waits=[(sS, s_ssq)])
                P.add("vector", lambda e: e.tensor_mul(out=c2("var"), in0=c2("mean"),
                                                       in1=c2("mean")))
                P.add("vector", lambda e: e.tensor_sub(out=c2("var"), in0=c2("e2"),
                                                       in1=c2("var")))
                v_var = sV.n
                P.add("scalar", lambda e: e.activation(out=c2("var"), in_=c2("var"),
                                                       func=AF.Ln, bias=cn("eps")),
                      waits=[(sV, v_var)])
                P.add("scalar", lambda e: e.activation(out=c2("rstd"), in_=c2("var"),
                                                       func=AF.Exp,
                                                       scale=cn("neghalf")))
                s_rstd = sS.n
                ks = a_kstg[par]
                for h in range(2):
                    P.add("vector", lambda e, h=h, ks=ks: e.tensor_scalar(
                        out=ks[:, h * 128:(h + 1) * 128],
                        in0=a_rope[:, h * 128:(h + 1) * 128],
                        scalar1=c2("mean", h, 1), scalar2=c2("rstd", h, 1),
                        op0=ALU.subtract, op1=ALU.mult),
                        waits=([(sS, s_rstd)] +
                               ([(sP, kstg_rel[par])] if kstg_rel[par] else []))
                        if h == 0 else [(sS, s_rstd)])
                v_kstg = sV.n

                def do_transposes():
                    trv = pb[bank_tr][:].bitcast(BF16)
                    for h in range(2):
                        P.add("tensor", lambda e, h=h, ks=ks: e.transpose(
                            out=trv[:, h * 128:(h + 1) * 128],
                            in_=ks[:, h * 128:(h + 1) * 128], identity=v_idbf),
                            waits=([(sV, v_kstg)] + PT.write_waits(bank_tr))
                            if h == 0 else [])
                    kstg_rel[par] = sP.n
                    p_tr = sP.n
                    for h in range(2):
                        P.add("scalar", lambda e, h=h, dc=dst_col, dst=dst:
                              e.activation(
                                  out=dst[:, dc[h]:dc[h] + 128],
                                  in_=trv[:, h * 128:(h + 1) * 128],
                                  func=AF.Identity, scale=v_lns, bias=v_lnb),
                              waits=([(sP, p_tr)] + list(evac_extra))
                              if h == 0 else [])
                    PT.note_read(bank_tr, sS)
                return do_transposes

            def gen_proj_poly(e8, att_done):
                kcol = E + e8 * 256
                vcol = 2 * E + e8 * 256
                qcol = e8 * 256
                lcs = ld(a_cos, D["cos_t"][:], waits=[(sV, sV.n)], sem=sL[10])
                lcs = ld(a_sin, D["sin_t"][:], sem=sL[10])
                lxq = {}

                def issue_x(nt, scr):
                    par = nt % 8
                    ld(a_xnt[par], scr[nt],
                       waits=[(sP, xnt_rel[par])] if xnt_rel[par] else [],
                       sem=sL[20 + par])
                    lxq[nt] = sL[20 + par].n
                yield from gen_load_w2(w_kv, "wqkv", kcol, 256, 0,
                                       list(att_done), (sL[56], sL[57]))
                yield from gen_load_w2(w_kv, "wqkv", vcol, 256, 256, [],
                                       (sL[58], sL[59]))
                v_wv = sV.n
                for nt in range(6):
                    issue_x(nt, S["xnT_sc"])
                yield
                pend_tr = None
                for nt in range(NTIL):
                    if nt + 6 < NTIL:
                        issue_x(nt + 6, S["xnT_sc"])
                    par = nt % 8
                    kbk = 0 if nt % 2 == 0 else 1
                    wk0 = PT.write_waits(kbk) + \
                        [(sL[20 + par], lxq[nt]), (sV, v_wv)]
                    if nt == 0:
                        wk0.append((sL[10], sL[10].n))
                    for ci in range(16):
                        P.add("tensor", lambda e, ci=ci, par=par, kbk=kbk: e.matmul(
                            out=pb[kbk][:], lhsT=a_xnt[par][:, ci * 128:(ci + 1) * 128],
                            rhs=w_kv[:, ci * 512:(ci + 1) * 512],
                            start=(ci == 0), stop=(ci == 15)),
                            waits=wk0 if ci == 0 else [])
                    p_kv = sP.n
                    xnt_rel[par] = p_kv
                    dcols = [h * NT + nt * 128 for h in range(2)]
                    fin = rope_ln2(kbk, 0, 5, a_cos[:, nt * 64:(nt + 1) * 64],
                                   a_sin[:, nt * 64:(nt + 1) * 64], p_khat,
                                   dcols, p_kv, nt % 2,
                                   evac_extra=att_done if nt == 0 else ())
                    P.add("scalar", lambda e, kbk=kbk, nt=nt: e.activation(
                        out=p_vtok[:, nt * 256:(nt + 1) * 256],
                        in_=pb[kbk][:, 256:512], func=AF.Copy),
                        waits=[(sP, p_kv)] + (list(att_done) if nt == 0 else []))
                    PT.note_read(kbk, sS)
                    if pend_tr is not None:
                        pend_tr()
                    pend_tr = fin
                    yield
                pend_tr()
                lco = ld(a_cos[:, 0:NOTIL * 64], D["cos_o"][:],
                         waits=[(sV, sV.n)], sem=sL[10])
                lco = ld(a_sin[:, 0:NOTIL * 64], D["sin_o"][:], sem=sL[10])
                for nt in range(6):
                    issue_x(nt, S["xnTo_sc"])
                yield from gen_load_w2(w_q, "wqkv", qcol, 256, 0,
                                       [(sP, sP.n)], (sL[58], sL[59]))
                v_wq = sV.n
                pend_tr = None
                for nt in range(NOTIL):
                    if nt + 6 < NOTIL:
                        issue_x(nt + 6, S["xnTo_sc"])
                    par = nt % 8
                    kbk = 0 if nt % 2 == 0 else 1
                    wk0 = PT.write_waits(kbk) + \
                        [(sL[20 + par], lxq[nt]), (sV, v_wq)]
                    if nt == 0:
                        wk0.append((sL[10], sL[10].n))
                    for ci in range(16):
                        P.add("tensor", lambda e, ci=ci, par=par, kbk=kbk: e.matmul(
                            out=pb[kbk][:, 0:256],
                            lhsT=a_xnt[par][:, ci * 128:(ci + 1) * 128],
                            rhs=w_q[:, ci * 256:(ci + 1) * 256],
                            start=(ci == 0), stop=(ci == 15)),
                            waits=wk0 if ci == 0 else [])
                    p_q = sP.n
                    xnt_rel[par] = p_q
                    dcols = [h * NO + nt * 128 for h in range(2)]
                    fin = rope_ln2(kbk, 0, 5, a_cos[:, nt * 64:(nt + 1) * 64],
                                   a_sin[:, nt * 64:(nt + 1) * 64], p_qhat,
                                   dcols, p_q, nt % 2,
                                   evac_extra=att_done if nt == 0 else ())
                    if pend_tr is not None:
                        pend_tr()
                    pend_tr = fin
                    yield
                pend_tr()

            def gen_proj_sm(e8, att_done):
                qcol = e8 * 256

                def issue_grp(g, scr, ng):
                    if g >= ng:
                        return []
                    out = []
                    for i in range(4):
                        sl = (g % 2) * 4 + i
                        ld(a_xnt[sl], scr[g * 4 + i],
                           waits=[(sP, xnt_rel[sl])] if xnt_rel[sl] else [],
                           sem=sL[20 + sl])
                        out.append((sL[20 + sl], sL[20 + sl].n))
                    return out
                yield from gen_load_w2(w_kv, "wk_t", qcol, 256, 0,
                                       list(att_done), (sL[56], sL[57]))
                v_wk = sV.n
                yield from gen_load_w2(w_kv, "wv_t", qcol, 256, 256, [],
                                       (sL[58], sL[59]))
                v_wv = sV.n
                nxt = issue_grp(0, S["xn1T_sc"], NTIL // 4)
                for g in range(NTIL // 4):
                    gpar = g % 2
                    lx = nxt
                    nxt = issue_grp(g + 1, S["xn1T_sc"], NTIL // 4)

                    for m in range(2):
                        kbk = m % 2
                        wk0 = PT.write_waits(kbk) + lx + [(sV, v_wk)]
                        for ci in range(16):
                            P.add("tensor", lambda e, ci=ci, m=m, kbk=kbk,
                                  gpar=gpar: e.matmul(
                                      out=pb[kbk][:],
                                      lhsT=w_kv[:, ci * 512 + m * 128:ci * 512 + m * 128 + 128],
                                      rhs=a_xnb[:, gpar * 4 * 2048:(gpar * 4 + 4) * 2048]
                                      .rearrange("p (b f) -> p b f", b=4)
                                      [:, :, ci * 128:(ci + 1) * 128],
                                      start=(ci == 0), stop=(ci == 15)),
                                  waits=wk0 if ci == 0 else [])
                        p_k = sP.n
                        P.add("scalar", lambda e, m=m, g=g, kbk=kbk: e.activation(
                            out=s_khat[:, m * NT + g * 512:m * NT + (g + 1) * 512],
                            in_=pb[kbk][:], func=AF.Copy),
                            waits=[(sP, p_k)] +
                            (list(att_done) if (g == 0 and m == 0) else []))
                        PT.note_read(kbk, sS)
                        yield
                    for i in range(4):
                        sl = gpar * 4 + i
                        nt = g * 4 + i
                        vbk = i % 2
                        wk0 = PT.write_waits(vbk) + [lx[i], (sV, v_wv)]
                        for ci in range(16):
                            P.add("tensor", lambda e, ci=ci, sl=sl, vbk=vbk:
                                  e.matmul(
                                      out=pb[vbk][:, 0:256],
                                      lhsT=a_xnt[sl][:, ci * 128:(ci + 1) * 128],
                                      rhs=w_kv[:, ci * 512 + 256:(ci + 1) * 512],
                                      start=(ci == 0), stop=(ci == 15)),
                                  waits=wk0 if ci == 0 else [])
                        p_v = sP.n
                        P.add("scalar", lambda e, nt=nt, vbk=vbk: e.activation(
                            out=s_vtok[:, nt * 256:(nt + 1) * 256],
                            in_=pb[vbk][:, 0:256], func=AF.Copy),
                            waits=[(sP, p_v)] + (list(att_done) if nt == 0 else []))
                        PT.note_read(vbk, sS)
                        yield
                    rel = sP.n
                    for i in range(4):
                        xnt_rel[gpar * 4 + i] = rel
                nxt = issue_grp(0, S["xn1To_sc"], NOTIL // 4)
                yield from gen_load_w2(w_q, "wq_t", qcol, 256, 0,
                                       [(sP, sP.n)], (sL[58], sL[59]))
                v_wq = sV.n
                for g in range(NOTIL // 4):
                    gpar = g % 2
                    lx = nxt
                    nxt = issue_grp(g + 1, S["xn1To_sc"], NOTIL // 4)
                    for m in range(2):
                        kbk = m % 2
                        wk0 = PT.write_waits(kbk) + lx + [(sV, v_wq)]
                        for ci in range(16):
                            P.add("tensor", lambda e, ci=ci, m=m, kbk=kbk,
                                  gpar=gpar: e.matmul(
                                      out=pb[kbk][:],
                                      lhsT=w_q[:, ci * 256 + m * 128:ci * 256 + m * 128 + 128],
                                      rhs=a_xnb[:, gpar * 4 * 2048:(gpar * 4 + 4) * 2048]
                                      .rearrange("p (b f) -> p b f", b=4)
                                      [:, :, ci * 128:(ci + 1) * 128],
                                      start=(ci == 0), stop=(ci == 15)),
                                  waits=wk0 if ci == 0 else [])
                        p_k = sP.n
                        P.add("scalar", lambda e, m=m, g=g, kbk=kbk: e.activation(
                            out=s_qhat[:, m * NO + g * 512:m * NO + (g + 1) * 512],
                            in_=pb[kbk][:], func=AF.Copy),
                            waits=[(sP, p_k)] +
                            (list(att_done) if (g == 0 and m == 0) else []))
                        PT.note_read(kbk, sS)
                        yield
                    rel = sP.n
                    for i in range(4):
                        xnt_rel[gpar * 4 + i] = rel

            def attention(e8, khat, vtok, qhat, kind, ctx_sc, ready_kq, filler):
                def pull():
                    if filler[0] is not None:
                        try:
                            next(filler[0])
                        except StopIteration:
                            filler[0] = None
                ctxp = 0
                for s in range(NS):
                    kt = W_SLOT[s] // 128
                    ktw = kt // 2
                    lm = ld(a_mask.rearrange("p (t f) -> p t f", t=8),
                            D["masks"][s].rearrange("t p f -> p t f"),
                            waits=[(sV, sV.n)], sem=sL[11])
                    for hl in range(2):
                        cp = ctxp % 2
                        cbk = 4
                        ctxp += 1
                        P.add("gpsimd", lambda e: e.memset(a_acc, 0.0),
                              waits=[(sP, acc_rel[0])] if acc_rel[0] else [])
                        g_ms = sG.n
                        avv = {}
                        accv = {}
                        accvv = {}
                        sq2v = {}
                        pend = None

                        def emit_av(jw, at_ready, last):
                            wav = [at_ready]
                            if jw == 0:
                                wav += PT.write_waits(cbk) + [(sS, ready_kq[1])]
                            for half in range(2):
                                j = 2 * jw + half
                                P.add("tensor", lambda e, j=j, hl=hl, ja=jw % 3,
                                      half=half, jw=jw, last=last:
                                      e.matmul(out=pb[cbk][:],
                                               lhsT=vtok[:, j * 256 + hl * 128:j * 256 + (hl + 1) * 128],
                                               rhs=a_at[ja][:, half * 512:(half + 1) * 512],
                                               start=(jw == 0 and half == 0),
                                               stop=(last and half == 1)),
                                      waits=wav if half == 0 else [])
                            avv[jw] = sP.n

                        for jw in range(ktw):
                            sc0 = 2 if jw % 2 == 0 else 6
                            scw = pw[1] if jw % 2 == 0 else pw[2]
                            wqk = PT.write_waits(sc0) + PT.write_waits(sc0 + 1)
                            if jw == 0:
                                wqk += [(sS, ready_kq[0]), (sL[11], lm)]
                            for half in range(2):
                                j = 2 * jw + half
                                P.add("tensor", lambda e, j=j, hl=hl, s=s,
                                      sc0=sc0, half=half: e.matmul(
                                          out=pb[sc0 + half][:],
                                          lhsT=khat[:, hl * NT + j * 128:hl * NT + (j + 1) * 128],
                                          rhs=qhat[:, hl * NO + s * QS:hl * NO + (s + 1) * QS],
                                          start=True, stop=True),
                                      waits=wqk if half == 0 else [])
                            p_qk = sP.n
                            if kind == "poly":
                                war_sq = [(sV, sq2v[jw - 2])] if jw - 2 in sq2v else []
                                P.add("scalar", lambda e, jb=jw % 2, scw=scw:
                                      e.activation(out=a_sq[jb], in_=scw[:],
                                                   func=AF.Square),
                                      waits=[(sP, p_qk)] + war_sq, nochain=True)
                                PT.note_read(sc0, sS)
                                PT.note_read(sc0 + 1, sS)
                                s_sq1 = sS.n
                                war_at = []
                                if jw - 3 in avv:
                                    war_at += [(sP, avv[jw - 3]), (sG, accv[jw - 3])]
                                P.add("vector", lambda e, jb=jw % 2, ja=jw % 3:
                                      e.tensor_mul(out=a_at[ja], in0=a_sq[jb],
                                                   in1=a_sq[jb]),
                                      waits=[(sS, s_sq1)] + war_at, nochain=True)
                                sq2v[jw] = sV.n
                            else:
                                war_at = []
                                if jw - 3 in avv:
                                    war_at += [(sP, avv[jw - 3]), (sG, accv[jw - 3]),
                                               (sV, accvv[jw - 3])]
                                P.add("scalar", lambda e, ja=jw % 3, scw=scw:
                                      e.activation(out=a_at[ja], in_=scw[:],
                                                   func=AF.Exp,
                                                   scale=cn("isq128")),
                                      waits=[(sP, p_qk)] + war_at, nochain=True)
                                PT.note_read(sc0, sS)
                                PT.note_read(sc0 + 1, sS)
                            if jw >= ktw - 4:
                                mi = jw - (ktw - 4)
                                w_at = [(sV, sq2v[jw])] if kind == "poly" \
                                    else [(sS, sS.n)]
                                P.add("vector", lambda e, ja=jw % 3, mi=mi:
                                      e.tensor_mul(out=a_at[ja], in0=a_at[ja],
                                                   in1=a_mask[:, mi * 1024:(mi + 1) * 1024]),
                                      waits=w_at, nochain=True)
                            at_ready = (sV, sV.n) if (kind == "poly" or jw >= ktw - 4) \
                                else (sS, sS.n)
                            wacc = [] if kind == "poly" and jw < ktw - 4 else [at_ready]
                            P.add("vector", lambda e, ja=jw % 3: e.tensor_add(
                                out=a_acc[:, 0:512], in0=a_acc[:, 0:512],
                                in1=a_at[ja][:, 0:512]),
                                waits=wacc + ([(sG, g_ms)] if jw == 0 else []))
                            accvv[jw] = sV.n
                            P.add("gpsimd", lambda e, ja=jw % 3: e.tensor_add(
                                out=a_acc[:, 512:1024], in0=a_acc[:, 512:1024],
                                in1=a_at[ja][:, 512:1024]),
                                waits=[at_ready])
                            accv[jw] = sG.n
                            if pend is not None:
                                emit_av(pend[0], pend[1], False)
                            pend = (jw, at_ready)
                            pull()
                        emit_av(pend[0], pend[1], True)
                        P.add("tensor", lambda e: e.matmul(
                            out=pb[6][0:1, :], lhsT=v_ones_col,
                            rhs=a_acc[:, 0:512], start=True, stop=False),
                            waits=[(sG, accv[ktw - 1]), (sV, sV.n)] +
                            PT.write_waits(6))
                        P.add("tensor", lambda e: e.matmul(
                            out=pb[6][0:1, :], lhsT=v_ones_col,
                            rhs=a_acc[:, 512:1024], start=False, stop=True))
                        acc_rel[0] = sP.n
                        p_den = sP.n
                        P.add("scalar", lambda e: e.activation(
                            out=a_dr[0:1, 0:512], in_=pb[6][0:1, :], func=AF.Ln),
                            waits=[(sP, p_den)])
                        PT.note_read(6, sS)
                        P.add("scalar", lambda e: e.activation(
                            out=a_dr[0:1, 512:1024], in_=a_dr[0:1, 0:512],
                            func=AF.Exp, scale=cn("negone")[0:1, :]))
                        s_rec = sS.n
                        P.add("tensor", lambda e: e.matmul(
                            out=pb[7][:], lhsT=v_ones_row[0:1, 0:128],
                            rhs=a_dr[0:1, 512:1024], start=True, stop=True),
                            waits=[(sS, s_rec)] + PT.write_waits(7))
                        p_bc = sP.n
                        P.add("vector", lambda e: e.tensor_copy(out=a_rbc,
                                                                in_=pb[7][:]),
                              waits=[(sP, p_bc)])
                        PT.note_read(7, sV)
                        P.add("vector", lambda e, cp=cp:
                              e.tensor_mul(out=a_ctxe[cp], in0=pb[cbk][:],
                                           in1=a_rbc),
                              waits=[(sP, avv[ktw - 1])] +
                              ([(sL[14 + cp], ctxe_rel[cp])] if ctxe_rel[cp] else []))
                        PT.note_read(cbk, sV)
                        ctxe_rel[cp] = st(ctx_sc[e8 * 2 + hl][:, s * QS:(s + 1) * QS],
                                          a_ctxe[cp], waits=[(sV, sV.n)],
                                          sem=sL[14 + cp])
                        pull()
                while filler[0] is not None:
                    try:
                        next(filler[0])
                    except StopIteration:
                        filler[0] = None

            att_done_p = [(sP, 0)]
            att_done_s = [(sP, 0)]
            gp = gen_proj_poly(0, [])
            for _ in gp:
                pass
            ready_p = [sS.n, sS.n]
            for e8 in range(8):
                fil = [gen_proj_sm(e8, att_done_s)]
                attention(e8, p_khat, p_vtok, p_qhat, "poly", S["ctxp_sc"],
                          ready_p, fil)
                att_done_p = [(sP, sP.n)]
                ready_s = [sS.n, sS.n]
                fil2 = [gen_proj_poly(e8 + 1, att_done_p) if e8 < 7 else None]
                if fil2[0] is None:
                    fil2 = [iter(())]
                attention(e8, s_khat, s_vtok, s_qhat, "sm", S["ctxs_sc"],
                          ready_s, fil2)
                att_done_s = [(sP, sP.n)]
                ready_p = [sS.n, sS.n]
            barrier()
            ar.pop()

            if upto >= 2:
                ar.push()
                b_wo = ar.bf(16 * 2048)
                b_wst = ar.f32(2048)
                b_ctx = [ar.bf(2048) for _ in range(2)]
                b_s = [ar.f32(2048) for _ in range(2)]
                wosrc = D["wout"].rearrange("(c p) m -> p c m", p=128)
                for cc in range(0, 16, 1):
                    lw = ld(b_wst.rearrange("p (c m) -> p c m", c=1),
                            wosrc[:, cc:cc + 1, :], waits=[(sV, sV.n)], sem=sL[12])
                    P.add("vector", lambda e, cc=cc: e.tensor_copy(
                        out=b_wo[:, cc * 2048:(cc + 1) * 2048], in_=b_wst),
                        waits=[(sL[12], lw)])
                v_wo = sV.n
                ctx_rel = [0, 0]
                s_rel = [0, 0]
                for nt in range(NOTIL):
                    par = nt % 2
                    lx = ld(b_ctx[par].rearrange("p (h f) -> p h f", h=16),
                            S["ctxp_sc"][:, :, nt * 128:(nt + 1) * 128]
                            .rearrange("h p f -> p h f"),
                            waits=[(sP, ctx_rel[par])] if ctx_rel[par] else [],
                            sem=sL[16 + par])
                    for eo in range(4):
                        obk = (0 if par == 0 else 2) + (eo % 2)
                        wk0 = PT.write_waits(obk) + \
                            [(sL[16 + par], lx), (sV, v_wo)]
                        for ci in range(16):
                            P.add("tensor", lambda e, ci=ci, par=par, eo=eo, obk=obk:
                                  e.matmul(out=pb[obk][:],
                                           lhsT=b_ctx[par][:, ci * 128:(ci + 1) * 128],
                                           rhs=b_wo[:, ci * 2048 + eo * 512:ci * 2048 + (eo + 1) * 512],
                                           start=(ci == 0), stop=(ci == 15)),
                                  waits=wk0 if ci == 0 else [])
                        ctx_rel[par] = sP.n
                        p_o = sP.n
                        P.add("vector", lambda e, par=par, eo=eo, obk=obk:
                              e.tensor_copy(out=b_s[par][:, eo * 512:(eo + 1) * 512],
                                            in_=pb[obk][:]),
                              waits=[(sP, p_o)] +
                              ([(sL[18 + par], s_rel[par])] if (s_rel[par] and eo == 0) else []))
                        PT.note_read(obk, sV)
                    s_rel[par] = st(S["s_sc"][nt], b_s[par], waits=[(sV, sV.n)],
                                    sem=sL[18 + par])
                barrier()
                ar.pop()

            if upto >= 4:
                ar.push()
                x2T = ar.bf(16 * 2048)
                ar.push()
                d_wo = ar.bf(16 * 2048)
                d_wst = ar.f32(2048)
                d_ctx = [ar.bf(2048) for _ in range(2)]
                d_xno = [ar.f32(2048) for _ in range(2)]
                d_xatt = [ar.f32(2048) for _ in range(2)]
                d_xn2s = ar.f32(2048)
                d_sm = ar.f32(16)
                wosrc = D["wo_t"].rearrange("(c p) m -> p c m", p=128)
                for cc in range(16):
                    lw = ld(d_wst.rearrange("p (c m) -> p c m", c=1),
                            wosrc[:, cc:cc + 1, :], waits=[(sV, sV.n)], sem=sL[12])
                    P.add("vector", lambda e, cc=cc: e.tensor_copy(
                        out=d_wo[:, cc * 2048:(cc + 1) * 2048], in_=d_wst),
                        waits=[(sL[12], lw)])
                v_wo = sV.n
                ctx_rel = [0, 0]
                xno_rel = [0, 0]
                xat_rel = [0, 0]
                for nt in range(NOTIL):
                    par = nt % 2
                    lx = ld(d_ctx[par].rearrange("p (h f) -> p h f", h=16),
                            S["ctxs_sc"][:, :, nt * 128:(nt + 1) * 128]
                            .rearrange("h p f -> p h f"),
                            waits=[(sP, ctx_rel[par])] if ctx_rel[par] else [],
                            sem=sL[16 + par])
                    ln = ld(d_xno[par], S["xno_sc"][nt],
                            waits=[(sV, xno_rel[par])] if xno_rel[par] else [],
                            sem=sL[24 + par])
                    for eo in range(4):
                        obk = (0 if par == 0 else 2) + (eo % 2)
                        wk0 = PT.write_waits(obk) + \
                            [(sL[16 + par], lx), (sV, v_wo)]
                        for ci in range(16):
                            P.add("tensor", lambda e, ci=ci, par=par, eo=eo, obk=obk:
                                  e.matmul(out=pb[obk][:],
                                           lhsT=d_ctx[par][:, ci * 128:(ci + 1) * 128],
                                           rhs=d_wo[:, ci * 2048 + eo * 512:ci * 2048 + (eo + 1) * 512],
                                           start=(ci == 0), stop=(ci == 15)),
                                  waits=wk0 if ci == 0 else [])
                        ctx_rel[par] = sP.n
                        p_o = sP.n
                        P.add("vector", lambda e, par=par, eo=eo, obk=obk:
                              e.tensor_add(out=d_xatt[par][:, eo * 512:(eo + 1) * 512],
                                           in0=pb[obk][:],
                                           in1=d_xno[par][:, eo * 512:(eo + 1) * 512]),
                              waits=[(sP, p_o), (sL[24 + par], ln)] +
                              ([(sL[26 + par], xat_rel[par])] if (xat_rel[par] and eo == 0) else []))
                        PT.note_read(obk, sV)
                    xno_rel[par] = sV.n
                    v_xat = sV.n
                    xat_rel[par] = st(S["xattn_sc"][nt], d_xatt[par],
                                      waits=[(sV, v_xat)], sem=sL[26 + par])
                    def d2(j):
                        return d_sm[:, j:j + 1]
                    P.add("scalar", lambda e, par=par: e.activation(
                        out=d_xn2s, in_=d_xatt[par], func=AF.Square,
                        accum_out=d2(0)), waits=[(sV, v_xat)])
                    s_sq = sS.n
                    P.add("vector", lambda e, par=par: e.reduce_sum(
                        out=d2(1), in_=d_xatt[par], axis=AX.X))
                    P.add("vector", lambda e: e.tensor_scalar(
                        out=d2(2), in0=d2(1), scalar1=cn("inv2048"), scalar2=None,
                        op0=ALU.mult))
                    P.add("vector", lambda e: e.tensor_scalar(
                        out=d2(3), in0=d2(0), scalar1=cn("inv2048"), scalar2=None,
                        op0=ALU.mult), waits=[(sS, s_sq)])
                    P.add("vector", lambda e: e.tensor_mul(out=d2(4), in0=d2(2),
                                                           in1=d2(2)))
                    P.add("vector", lambda e: e.tensor_sub(out=d2(4), in0=d2(3),
                                                           in1=d2(4)))
                    v_var = sV.n
                    P.add("scalar", lambda e: e.activation(out=d2(5), in_=d2(4),
                                                           func=AF.Ln, bias=cn("eps")),
                          waits=[(sV, v_var)])
                    P.add("scalar", lambda e: e.activation(out=d2(6), in_=d2(5),
                                                           func=AF.Exp,
                                                           scale=cn("neghalf")))
                    s_r2 = sS.n
                    P.add("vector", lambda e, par=par: e.tensor_scalar(
                        out=d_xn2s, in0=d_xatt[par], scalar1=d2(2), scalar2=d2(6),
                        op0=ALU.subtract, op1=ALU.mult), waits=[(sS, s_r2)])
                    v_x2s = sV.n
                    for grp in range(4):
                        tbk = 4 + (grp % 2)
                        for k4 in range(4):
                            ci = grp * 4 + k4
                            P.add("tensor", lambda e, ci=ci, k4=k4, tbk=tbk:
                                  e.transpose(out=pb[tbk][:, k4 * 128:(k4 + 1) * 128],
                                              in_=d_xn2s[:, ci * 128:(ci + 1) * 128],
                                              identity=v_id32),
                                  waits=([(sV, v_x2s)] + PT.write_waits(tbk))
                                  if k4 == 0 else [])
                        p_tr = sP.n
                        for k4 in range(4):
                            ci = grp * 4 + k4
                            P.add("scalar", lambda e, ci=ci, k4=k4, tbk=tbk, nt=nt:
                                  e.activation(out=x2T[:, nt * 2048 + ci * 128:nt * 2048 + (ci + 1) * 128],
                                               in_=pb[tbk][:, k4 * 128:(k4 + 1) * 128],
                                               func=AF.Identity,
                                               scale=v_g2[:, ci:ci + 1],
                                               bias=v_b2[:, ci:ci + 1]),
                                  waits=[(sP, p_tr)] if k4 == 0 else [])
                        PT.note_read(tbk, sS)
                barrier()
                ar.pop()

                ar.push()
                e_wst = [ar.f32(2048) for _ in range(2)]
                e_wf1 = [ar.bf(2048) for _ in range(2)]
                e_sqg = ar.f32(512)
                e_z = ar.f32(512)
                e_u = ar.f32(512)
                e_sg = ar.f32(512)
                e_g = [ar.bf(2048) for _ in range(2)]
                e_cw = ar.f32(2048)
                e_cb = ar.bf(2048)
                f1src = D["wff1_t"].rearrange("(c p) m -> p c m", p=128)
                f2src = D["wff2_t"].rearrange("(m p) e -> p m e", p=128)
                wf1_rel = [0, 0]
                wf1_rel_p = [0, 0]
                g_rel = [0, 0]
                cw_rel = [0]
                for mm in range(64):
                    par = mm % 2
                    lw = ld(e_wst[par].rearrange("p (c m) -> p c m", c=16),
                            f1src[:, :, mm * 128:(mm + 1) * 128],
                            waits=[(sV, wf1_rel[par])] if wf1_rel[par] else [],
                            sem=sL[12 + par])
                    P.add("vector", lambda e, par=par: e.tensor_copy(
                        out=e_wf1[par], in_=e_wst[par]),
                        waits=[(sL[12 + par], lw)] +
                        ([(sP, wf1_rel_p[par])] if wf1_rel_p[par] else []))
                    wf1_rel[par] = sV.n
                    v_w = sV.n
                    for g in range(4):
                        hbk = g % 2
                        wk0 = PT.write_waits(hbk) + [(sV, v_w)]
                        for ci in range(16):
                            P.add("tensor", lambda e, ci=ci, g=g, hbk=hbk, par=par:
                                  e.matmul(out=pb[hbk][:],
                                           lhsT=e_wf1[par][:, ci * 128:(ci + 1) * 128],
                                           rhs=x2T[:, g * 4 * 2048:(g * 4 + 4) * 2048]
                                           .rearrange("p (b f) -> p b f", b=4)
                                           [:, :, ci * 128:(ci + 1) * 128],
                                           start=(ci == 0), stop=(ci == 15)),
                                  waits=wk0 if ci == 0 else [])
                        wf1_rel_p[par] = sP.n
                        p_h = sP.n
                        P.add("scalar", lambda e, hbk=hbk: e.activation(
                            out=e_sqg, in_=pb[hbk][:], func=AF.Square),
                            waits=[(sP, p_h), (sV, sV.n)])
                        s_hsq = sS.n
                        P.add("vector", lambda e: e.tensor_scalar(
                            out=e_z, in0=e_sqg, scalar1=cn("c0447"),
                            scalar2=cn("one"), op0=ALU.mult, op1=ALU.add),
                            waits=[(sS, s_hsq)])
                        P.add("vector", lambda e, hbk=hbk: e.tensor_mul(
                            out=e_u, in0=pb[hbk][:], in1=e_z),
                            waits=[(sS, sS.n)])
                        v_u = sV.n
                        P.add("scalar", lambda e: e.activation(
                            out=e_sg, in_=e_u, func=AF.Sigmoid,
                            scale=cn("sig_gelu")), waits=[(sV, v_u)])
                        s_sg = sS.n
                        P.add("vector", lambda e, hbk=hbk, g=g, par=par:
                              e.tensor_mul(out=e_g[par][:, g * 512:(g + 1) * 512],
                                           in0=pb[hbk][:], in1=e_sg),
                              waits=[(sS, s_sg)] +
                              ([(sL[16 + par], g_rel[par])] if (g_rel[par] and g == 0) else []))
                        PT.note_read(hbk, sV)
                    g_rel[par] = st(S["g_sc"][mm], e_g[par], waits=[(sV, sV.n)],
                                    sem=sL[16 + par])
                    lc = ld(e_cw, f2src[:, mm, :],
                            waits=[(sG, cw_rel[0])] if cw_rel[0] else [], sem=sL[28])
                    P.add("gpsimd", lambda e: e.tensor_copy(out=e_cb, in_=e_cw),
                          waits=[(sL[28], lc), (sL[29], sL[29].n)])
                    cw_rel[0] = sG.n
                    st(S["wff2bf_sc"][mm], e_cb, waits=[(sG, sG.n)], sem=sL[29])
                barrier()
                ar.pop()

                ar.push()
                f_g = ar.bf(64 * 512)
                f_w2 = [ar.bf(1024) for _ in range(2)]
                fr_xat = [ar.f32(512) for _ in range(2)]
                fr_xn = [ar.f32(512) for _ in range(2)]
                fr_s = [ar.f32(512) for _ in range(2)]
                fr_x = [ar.f32(512) for _ in range(2)]
                fr_t0 = [ar.f32(512) for _ in range(2)]
                fr_t1 = [ar.f32(512) for _ in range(2)]
                fr_t2 = [ar.f32(512) for _ in range(2)]
                fr_sg2 = [ar.f32(512) for _ in range(2)]
                fr_out = [ar.f32(512) for _ in range(2)]
                f_t0, f_t1, f_t2, f_sg2 = fr_t0[0], fr_t1[0], fr_t2[0], fr_sg2[0]
                f_wup = ar.bf(2048)
                f_bias = ar.f32(2048)
                lwu = ld(f_bias, D["bias_bc"][:], sem=sL[30])
                lu = ld(f_t0, D["wup128"][:, 0:512], sem=sL[31])
                lu2 = ld(f_t1, D["wup128"][:, 512:1024], sem=sL[31],
                         waits=[(sL[31], lu)])
                lu3 = ld(f_t2, D["wup128"][:, 1024:1536], sem=sL[31],
                         waits=[(sL[31], lu2)])
                lu4 = ld(f_sg2, D["wup128"][:, 1536:2048], sem=sL[31],
                         waits=[(sL[31], lu3)])
                P.add("vector", lambda e, b=f_t0: e.tensor_copy(
                    out=f_wup[:, 0:512], in_=b), waits=[(sL[31], lu)])
                P.add("vector", lambda e, b=f_t1: e.tensor_copy(
                    out=f_wup[:, 512:1024], in_=b), waits=[(sL[31], lu2)])
                P.add("vector", lambda e, b=f_t2: e.tensor_copy(
                    out=f_wup[:, 1024:1536], in_=b), waits=[(sL[31], lu3)])
                P.add("vector", lambda e, b=f_sg2: e.tensor_copy(
                    out=f_wup[:, 1536:2048], in_=b), waits=[(sL[31], lu4)])
                v_wup_done = sV.n
                if "lr1" in dumps:
                    wup2_d = nc.dram_tensor("dump_wup_early", [128, 2048], BF16,
                                            kind="ExternalOutput").ap()
                    P.add("scalar", lambda e: e.dma_start(out=wup2_d, in_=f_wup),
                          waits=[(sV, v_wup_done)], incs=[(sW, 16)], dma=True)
                g_blk_rel = [0]
                w2_rel = [0, 0]
                ep_ctr = [0]
                ep_rel = [(0, 0), (0, 0)]
                for g in range(4):
                    glw = []
                    for mq in range(0, 64, 16):
                        glw.append(ld(
                            f_g.rearrange("p (m f) -> p m f", m=64)[:, mq:mq + 16, :],
                            S["g_sc"][mq:mq + 16, :, g * 512:(g + 1) * 512]
                            .rearrange("m p f -> p m f"),
                            waits=[(sP, g_blk_rel[0])] if (g_blk_rel[0] and mq == 0) else [],
                            sem=sL[32]))
                    lg = (sL[32], sL[32].n)
                    for ep in range(2):
                        for mm in range(64):
                            par = mm % 2
                            lw2 = ld(f_w2[par], S["wff2bf_sc"][mm][:, ep * 1024:(ep + 1) * 1024],
                                     waits=[(sP, w2_rel[par])] if w2_rel[par] else [],
                                     sem=sL[34 + par])
                            first_w = [(sL[34 + par], lw2), lg]
                            for nt4 in range(4):
                                for eoi in range(2):
                                    bk = nt4 * 2 + eoi
                                    wk0 = list(first_w) if (nt4 == 0 and eoi == 0) else []
                                    if mm == 0:
                                        wk0 += PT.write_waits(bk)
                                    P.add("tensor", lambda e, mm=mm, nt4=nt4, eoi=eoi,
                                          ep=ep, bk=bk, par=par, g=g: e.matmul(
                                              out=pb[bk][:],
                                              lhsT=f_g[:, mm * 512 + nt4 * 128:mm * 512 + (nt4 + 1) * 128],
                                              rhs=f_w2[par][:, eoi * 512:(eoi + 1) * 512],
                                              start=(mm == 0), stop=(mm == 63)),
                                          waits=wk0)
                            w2_rel[par] = sP.n
                        p_ff2 = sP.n
                        g_blk_rel[0] = p_ff2
                        for nt4 in range(4):
                            nt = g * 4 + nt4
                            for eoi in range(2):
                                eo = ep * 2 + eoi
                                bk = nt4 * 2 + eoi
                                col = slice(eo * 512, (eo + 1) * 512)
                                ip = ep_ctr[0] % 2
                                ep_ctr[0] += 1
                                f_xat = fr_xat[ip]; f_xn = fr_xn[ip]
                                f_s = fr_s[ip]; f_x = fr_x[ip]
                                f_t0 = fr_t0[ip]; f_t1 = fr_t1[ip]
                                f_t2 = fr_t2[ip]; f_sg2 = fr_sg2[ip]
                                f_out = fr_out[ip]
                                rv, rg = ep_rel[ip]
                                lda = ld(f_xat, S["xattn_sc"][nt][:, col],
                                         waits=[(sV, rv)], sem=sL[36 + ip])
                                ldn = ld(f_xn, S["xno_sc"][nt][:, col],
                                         waits=[(sV, rv)], sem=sL[38 + ip])
                                lds = ld(f_s, S["s_sc"][nt][:, col],
                                         waits=[(sG, rg)], sem=sL[52 + ip])
                                ldx = ld(f_x, D["x_own"].rearrange(
                                    "(n p) e -> n p e", p=128)[nt][:, col],
                                    waits=[(sG, rg)], sem=sL[54 + ip])
                                m0c = v_m0c[:, nt:nt + 1]
                                m1c = v_m1c[:, nt:nt + 1]
                                m2c = v_m2c[:, nt:nt + 1]
                                m01c = v_m01c[:, nt:nt + 1]
                                P.add("vector", lambda e, m01c=m01c, f_t0=f_t0,
                                      f_xn=f_xn: e.tensor_scalar(
                                    out=f_t0, in0=f_xn, scalar1=m01c, scalar2=None,
                                    op0=ALU.mult),
                                    waits=[(sL[38 + ip], ldn), (sL[30], lwu),
                                           (sV, v_wup_done), (sG, rg)])
                                P.add("vector", lambda e, m2c=m2c, f_t1=f_t1,
                                      f_xat=f_xat, f_t0=f_t0:
                                      e.scalar_tensor_tensor(
                                          out=f_t1, in0=f_xat, scalar=m2c, in1=f_t0,
                                          op0=ALU.mult, op1=ALU.add),
                                      waits=[(sL[36 + ip], lda)])
                                P.add("vector", lambda e, m2c=m2c, bk=bk, f_t2=f_t2,
                                      f_t1=f_t1: e.scalar_tensor_tensor(
                                          out=f_t2, in0=pb[bk][:], scalar=m2c,
                                          in1=f_t1, op0=ALU.mult, op1=ALU.add),
                                      waits=[(sP, p_ff2)])
                                PT.note_read(bk, sV)
                                P.add("tensor", lambda e, nt=nt, eo=eo, bk=bk:
                                      e.matmul(out=pb[bk][:],
                                               lhsT=v_lr1[:, nt * 128:(nt + 1) * 128],
                                               rhs=f_wup[:, eo * 512:(eo + 1) * 512],
                                               start=True, stop=True),
                                      waits=[(sV, sV.n)] + PT.write_waits(bk))
                                p_lr = sP.n
                                P.add("vector", lambda e, m1c=m1c, bk=bk, f_t2=f_t2:
                                      e.scalar_tensor_tensor(
                                          out=f_t2, in0=pb[bk][:], scalar=m1c,
                                          in1=f_t2, op0=ALU.mult, op1=ALU.add),
                                      waits=[(sP, p_lr)])
                                PT.note_read(bk, sV)
                                P.add("vector", lambda e, m0c=m0c, col=col,
                                      f_t0=f_t0, f_t2=f_t2:
                                      e.scalar_tensor_tensor(
                                          out=f_t0, in0=f_bias[:, col], scalar=m0c,
                                          in1=f_t2, op0=ALU.mult, op1=ALU.add),
                                      waits=[(sG, rg)])
                                g_sel = sV.n
                                P.add("scalar", lambda e, f_sg2=f_sg2, f_t0=f_t0:
                                      e.activation(
                                          out=f_sg2, in_=f_t0, func=AF.Sigmoid),
                                      waits=[(sV, g_sel), (sG, rg)])
                                s_sig = sS.n
                                P.add("gpsimd", lambda e, f_t1=f_t1, f_t0=f_t0,
                                      f_sg2=f_sg2: e.tensor_mul(
                                          out=f_t1, in0=f_t0, in1=f_sg2),
                                      waits=[(sS, s_sig)])
                                P.add("gpsimd", lambda e, f_t1=f_t1, f_s=f_s:
                                      e.tensor_mul(
                                          out=f_t1, in0=f_t1, in1=f_s),
                                      waits=[(sL[52 + ip], lds)])
                                P.add("gpsimd", lambda e, f_out=f_out, f_t1=f_t1,
                                      f_x=f_x: e.tensor_add(
                                          out=f_out, in0=f_t1, in1=f_x),
                                      waits=[(sL[54 + ip], ldx),
                                             (sL[33 + ip * 18], sL[33 + ip * 18].n)])
                                v_o = sG.n
                                st(out_d[nt][:, col], f_out, waits=[(sG, v_o)],
                                   sem=sL[33 + ip * 18])
                                ep_rel[ip] = (sV.n, sG.n)
                barrier()
                ar.pop()
                ar.pop()

        if "lr1" in dumps:
            lr1_d = nc.dram_tensor("dump_lr1", [128, 2048], BF16,
                                   kind="ExternalOutput").ap()
            P.add("scalar", lambda e: e.dma_start(out=lr1_d, in_=v_lr1),
                  waits=[(s, s.n) for s in (sP, sS, sV, sG)], incs=[(sW, 16)],
                  dma=True)
            wup_d = nc.dram_tensor("dump_wup", [128, 2048], BF16,
                                   kind="ExternalOutput").ap()
            P.add("scalar", lambda e: e.dma_start(out=wup_d, in_=f_wup),
                  incs=[(sW, 16)], dma=True)
            msk_d = nc.dram_tensor("dump_masks", [128, 64], F32,
                                   kind="ExternalOutput").ap()
            P.add("scalar", lambda e: e.dma_start(out=msk_d[:, 0:16], in_=v_m0c),
                  incs=[(sW, 16)], dma=True)
            P.add("scalar", lambda e: e.dma_start(out=msk_d[:, 16:32], in_=v_m1c),
                  incs=[(sW, 16)], dma=True)
            P.add("scalar", lambda e: e.dma_start(out=msk_d[:, 32:48], in_=v_m2c),
                  incs=[(sW, 16)], dma=True)
            P.add("scalar", lambda e: e.dma_start(out=msk_d[:, 48:64], in_=v_m01c),
                  incs=[(sW, 16)], dma=True)
        for nm in dumps:
            if nm == "lr1":
                continue
            P.add("sync", lambda e, d=dump_aps[nm], s=S[nm]: e.dma_start(
                out=d[:], in_=s[:]), incs=[(sD, 16)], dma=True)

        P.add("sync", None, waits=[(s, s.n) for s in ALLSEMS])

        block = es.enter_context(nc.Block())

        @block.sync
        def _(sync):
            P.emit("sync", sync)

        @block.tensor
        def _(tensor):
            P.emit("tensor", tensor)

        @block.scalar
        def _(scalar):
            P.emit("scalar", scalar)

        @block.vector
        def _(vector):
            P.emit("vector", vector)

        @block.gpsimd
        def _(gpsimd):
            P.emit("gpsimd", gpsimd)

    return nc


def kernel(**inputs):
    in_maps = host_prep(inputs)
    nc = build_nc()
    res = run_bass_kernel_spmd(nc, in_maps, core_ids=list(range(8)))
    out = np.empty((B, N, E), np.float32)
    for core in range(8):
        b = core // 2
        slots = SLOTS_A if core % 2 == 0 else SLOTS_B
        o = np.asarray(res.results[core]["out"]).reshape(NO, E)
        for s, ch in enumerate(slots):
            out[b, 512 * ch:512 * ch + 512] = o[512 * s:512 * (s + 1)]
    return out
